# version 10
# baseline (speedup 1.0000x reference)
import math
import numpy as np
import ml_dtypes
import concourse.bass as bass
import concourse.mybir as mybir
from concourse.bass_utils import run_bass_kernel_spmd

F32 = mybir.dt.float32
BF16 = mybir.dt.bfloat16
FP8 = mybir.dt.float8e4
I32 = mybir.dt.int32
AF = mybir.ActivationFunctionType
ALU = mybir.AluOpType
AX = mybir.AxisListType

D = 2048
DFF = 8192
NH = 16
HD = 128
TOK = 1024
NTT = 8
SEQ = 2048
EXP_C = 32.0
ISQ = 1.0 / math.sqrt(128.0)

SB_LO = 16512
SB_HI = 229344


def _merge(waits):
    d = {}
    for s, v in waits:
        if v > 0 and (s not in d or v > d[s]):
            d[s] = v
    return [(s, v) for s, v in d.items()]


class Sem:
    def __init__(self, nc, name):
        self.h = nc.alloc_semaphore(name)
        self.n = 0


class Buf:

    def __init__(self):
        self.w = []
        self.r = []

    def read(self, my_sv=None):
        if my_sv is not None:
            self.r.append(my_sv)
        return list(self.w)

    def write(self, my_sv=None):
        w = self.r + self.w
        if my_sv is not None:
            self.w = [my_sv]
            self.r = []
        return w


class K:

    ENGS = ("sync", "gpsimd", "tensor", "vector", "scalar")

    def __init__(self, nc):
        self.nc = nc
        self.items = {e: [] for e in self.ENGS}
        self.sem = {e: Sem(nc, "s_" + e) for e in self.ENGS}
        self.sem_cc = Sem(nc, "s_cc")

    def _wrap(self, eng_name, fn, waits, sem, inc):
        if inc:
            sem.n += 1 if eng_name not in ("sync", "gpsimd") else 16
        val = sem.n
        waits = _merge(waits or [])

        def f(eng, ctx):
            for (s, v) in waits:
                if ctx.get(s.h.num, -1) < v:
                    eng.wait_ge(s.h, v)
                    ctx[s.h.num] = v
            ins = fn(eng)
            if inc:
                k = 16 if eng_name in ("sync", "gpsimd") else 1
                ins.then_inc(sem.h, k)
        self.items[eng_name].append(f)
        return (sem, val)

    def dma(self, q, out_ap, in_ap, waits=None):
        return self._wrap(q, lambda e: e.dma_start(out=out_ap, in_=in_ap),
                          waits, self.sem[q], True)

    def op(self, eng, fn, waits=None, inc=True):
        return self._wrap(eng, fn, waits, self.sem[eng], inc)

    def raw(self, eng, fn):
        self.items[eng].append(lambda e, ctx: fn(e))

    def barrier(self):
        snap = [(s, s.n) for s in self.sem.values()] + [(self.sem_cc, self.sem_cc.n)]
        for e in self.ENGS:
            mine = self.sem[e]
            waits = _merge([(s, v) for (s, v) in snap if s is not mine])

            def f(eng, ctx, waits=waits):
                for (s, v) in waits:
                    if ctx.get(s.h.num, -1) < v:
                        eng.wait_ge(s.h, v)
                        ctx[s.h.num] = v
            self.items[e].append(f)

    def replay(self):
        nc = self.nc
        with nc.Block() as blk:
            for e in self.ENGS:
                def make(e):
                    def body(eng):
                        ctx = {}
                        for f in self.items[e]:
                            f(eng, ctx)
                    return body
                getattr(blk, e)(make(e))


def build_program(n_sub=4, dbg=False):
    nc = bass.Bass("TRN2", target_bir_lowering=False, num_devices=8)
    k = K(nc)

    din = {}
    def dram_in(name, shape, dtype=F32):
        din[name] = nc.dram_tensor(name, shape, dtype, kind="ExternalInput").ap()
        return din[name]

    x_own = dram_in("x_own", [TOK, D])
    x_oth = dram_in("x_oth", [TOK, D])
    wq_a = dram_in("wq_a", [D, D])
    wk_a = dram_in("wk_a", [D, 512])
    wv_a = dram_in("wv_a", [D, 512])
    wo_a = dram_in("wo_a", [D, D])
    bo_a = dram_in("bo_a", [1, D])
    wq_b = dram_in("wq_b", [D, D])
    wk_b = dram_in("wk_b", [D, D])
    wv_b = dram_in("wv_b", [D, D])
    wo_b = dram_in("wo_b", [D, D])
    gate = dram_in("gate", [1, NH])
    w1 = dram_in("w1", [DFF, D])
    w2 = dram_in("w2", [DFF, D])
    w3 = dram_in("w3", [D, DFF])
    ident_in = dram_in("ident", [128, 128])
    prep_src = dram_in("prep_src", [32, 128, D])
    sel_in = dram_in("sel", [128, 1])
    maskT_in = dram_in("maskT", [SEQ, TOK], FP8)
    cos_in = dram_in("cos_t", [D, TOK])
    sin_in = dram_in("sin_t", [D, TOK])
    y = nc.dram_tensor("y", [TOK, D], F32, kind="ExternalOutput").ap()

    PRANK = 8200
    prep_in = nc.dram_tensor("prep_in", [PRANK, 1024], BF16, kind="Internal").ap()
    prep_all = nc.dram_tensor("prep_all", [8 * PRANK, 1024], BF16, kind="Internal").ap()

    def blk_view(b):
        r0 = (b // 8) * PRANK + (b % 8) * 1024
        return prep_all[r0:r0 + 1024, :].rearrange("(p a) j -> p (a j)", p=128)

    w1T = [blk_view(ds) for ds in range(16)]
    w2T = [blk_view(16 + ds) for ds in range(16)]
    w3T = [[blk_view(32 + g * 4 + dms) for g in range(4)] for dms in range(4)]
    wqbT = [blk_view(48 + i) for i in range(4)]
    wkbT = [blk_view(52 + i) for i in range(4)]
    wvbT = [blk_view(56 + i) for i in range(4)]
    wobT = [blk_view(60 + i) for i in range(4)]
    kv_pack = nc.dram_tensor("kv_pack", [2 * D, TOK], BF16, kind="Internal").ap()
    kv_g = nc.dram_tensor("kv_g", [4 * D, TOK], BF16, kind="Internal").ap()

    used = []
    def S(name, shape, dtype, off):
        nb = int(np.prod(shape[1:])) * mybir.dt.size(dtype)
        assert off % 32 == 0 and off >= SB_LO and off + nb <= SB_HI, (name, off, nb)
        used.append((name, off, off + nb))
        return nc.alloc_sbuf_tensor_at(name, shape, dtype, offset=off)

    o = SB_LO
    def P(name, shape, dtype):
        nonlocal o
        t = S(name, shape, dtype, o)
        o += (int(np.prod(shape[1:])) * mybir.dt.size(dtype) + 31) & ~31
        return t

    x_sb = P("x_sb", [128, NTT * D], F32)
    ident = P("identb", [128, 128], BF16)
    ones128 = P("ones128", [128, 1], BF16)
    ones128f = P("ones128f", [128, 1], F32)
    ones1f = P("ones1f", [1, 128], F32)
    bo_b = P("bo_b", [128, D], BF16)
    g_b = P("g_b", [128, NH], F32)
    wsc = P("wsc", [128, 4], F32)
    scr = P("scr", [128, 16], F32)
    cst = P("cst", [128, 4], F32)
    scr2 = P("scr2", [128, 64], F32)
    sumacc = P("sumacc", [128, 4], F32)
    ntb = P("ntb", [128, 2], BF16)
    rowscr = P("rowscr", [1, TOK], F32)
    scrA = P("scrA", [128, D], F32)
    scrB = P("scrB", [128, D], BF16)
    scrB2 = P("scrB2", [128, D], BF16)
    A = (o + 31) & ~31
    ARENA_SZ = SB_HI - A
    assert ARENA_SZ >= 116 * 1024, (A, ARENA_SZ)

    pr_pan2 = S("pr_pan2", [128, D], F32, A)
    pr_sgn2 = S("pr_sgn2", [128, D], BF16, A + 0x2000)
    pr_asm = S("pr_asm", [128, 16 * 512], BF16, A + 0x3000)
    ln_f = S("ln_f", [128, 16 * TOK], BF16, A)
    aa_q = S("aa_q", [128, 16 * TOK], BF16, A + 0x8000)
    aa_k = S("aa_k", [128, 4 * SEQ], BF16, A + 0x10000)
    aa_v = S("aa_v", [128, 16 * 512], BF16, A + 0x14000)
    aa_lnfo = S("aa_lnfo", [128, 16 * 128], BF16, A + 0x18000)
    aa_w = S("aa_w", [128, 16 * 512], BF16, A + 0x19000)
    aa_ctx = S("aa_ctx", [128, 16 * TOK], BF16, A)
    aa_mask = S("aa_mask", [128, 16 * 256], FP8, A + 0x18000)
    aa_exp = [S("aa_exp%d" % i, [128, 16 * 256], BF16, A + 0x19000 + i * 0x2000)
              for i in range(2)]
    aa_w2 = S("aa_w2", [128, 16 * 512], BF16, A + 0x8000)
    ff_qa = S("ff_qa", [128, 16 * TOK], BF16, A)
    ff_qh = S("ff_qh", [128, 16 * TOK], BF16, A + 0x8000)
    ff_w1 = S("ff_w1", [128, 16 * 512], BF16, A + 0x10000)
    ff_w2 = S("ff_w2", [128, 16 * 512], BF16, A + 0x14000)
    ff_w3 = [S("ff_w3_%d" % i, [128, 8 * 512], BF16, A + 0x18000 + i * 0x2000)
             for i in range(2)]
    ff_ln = S("ff_ln", [128, D], F32, A + 0x8000)
    ff_qi = S("ff_qi", [128, D], I32, A + 0xA000)
    ff_h = scrA[:, 0:512]
    ff_sl = scrA[:, 512:1024]
    ff_qih = scrA[:, 1024:1536].bitcast(I32)
    ff_qt = scrB2[:, 0:512]
    ff_qt2 = scrB2[:, 512:1024]
    ab_ln = S("ab_ln", [128, 16 * TOK], BF16, A)
    ab_q = S("ab_q", [128, 16 * TOK], BF16, A + 0x8000)
    ab_qr = S("ab_qr", [128, 8 * TOK], BF16, A + 0x10000)
    ab_w = S("ab_w", [128, 16 * 512], BF16, A + 0x14000)
    ab_cos = S("ab_cos", [128, 2 * TOK], F32, A + 0x18000)
    ab_sin = S("ab_sin", [128, 2 * TOK], F32, A + 0x1A000)
    ab_ctx = S("ab_ctx", [128, 16 * TOK], BF16, A)
    ab_kf = S("ab_kf", [128, SEQ], BF16, A + 0x10000)
    ab_kft = S("ab_kft", [128, SEQ], BF16, A + 0x11000)
    ab_qf = [S("ab_qf%d" % i, [128, TOK], BF16, A + 0x12000 + i * 0x800)
             for i in range(2)]
    ab_mem = [S("ab_mem%d" % i, [128, 128], BF16, A + 0x1B000 + i * 0x100)
              for i in range(2)]
    ab_exp = [S("ab_exp%d" % i, [128, 16 * 256], BF16, A + 0x13000 + i * 0x2000)
              for i in range(2)]
    ab_kv = [S("ab_kv%d" % i, [128, 2 * SEQ], BF16, A + 0x17000 + i * 0x2000)
             for i in range(2)]
    ab_w3 = S("ab_w3", [128, 16 * 512], BF16, A + 0x10000)

    psA = [nc.alloc_psum_tensor("psA%d" % i, [128, 512], F32) for i in range(4)]
    psT = nc.alloc_psum_tensor("psT", [128, 1024], BF16)
    psE = nc.alloc_psum_tensor("psE", [128, 512], F32)
    psD = [nc.alloc_psum_tensor("psD%d" % i, [1, 512], F32) for i in range(2)]

    bufs = {}
    def B(name):
        if name not in bufs:
            bufs[name] = Buf()
        return bufs[name]

    SY, GP, TE, VE, SC = "sync", "gpsimd", "tensor", "vector", "scalar"

    for t in range(NTT):
        sv = k.dma(SY, x_sb[:, t * D:(t + 1) * D], x_own[t * 128:(t + 1) * 128, :])
        B("x%d" % t).write(sv)
    sv = k.dma(GP, ident[:], ident_in)
    B("ident").write(sv)
    sv = k.op(VE, lambda e: e.memset(ones128[:], 1.0))
    B("ones").write(sv)
    sv = k.op(VE, lambda e: e.memset(ones128f[:], 1.0))
    B("ones").write(sv)
    sv = k.op(VE, lambda e: e.memset(ones1f[:], 1.0))
    B("ones").write(sv)
    k.op(VE, lambda e: e.memset(cst[:, 0:1], 1e-5))
    k.op(VE, lambda e: e.memset(cst[:, 1:2], -EXP_C))
    sv = k.op(VE, lambda e: e.memset(cst[:, 2:3], ISQ))
    B("cst").write(sv)
    sv = k.dma(SY, rowscr[0:1, 0:NH], gate)
    B("rowscr").write(sv)
    sv = k.op(SC, lambda e: e.activation(rowscr[0:1, 0:NH], rowscr[0:1, 0:NH], AF.Sigmoid),
              waits=B("rowscr").write())
    B("rowscr").write(sv)
    sv = k.op(TE, lambda e: e.matmul(psE[:, 0:NH], ones1f[:], rowscr[0:1, 0:NH],
                                     start=True, stop=True),
              waits=B("rowscr").read((k.sem[TE], k.sem[TE].n + 1))
              + B("ones").read() + B("psE").write())
    B("psE").write(sv)
    sv = k.op(VE, lambda e: e.tensor_copy(g_b[:], psE[:, 0:NH]),
              waits=B("psE").read((k.sem[VE], k.sem[VE].n + 1)))
    B("g_b").write(sv)
    for hh in range(2):
        sv = k.dma(SY, rowscr[0:1, :], bo_a[0:1, hh * TOK:(hh + 1) * TOK],
                   waits=B("rowscr").write())
        B("rowscr").write(sv)
        for i in range(2):
            sv = k.op(TE, lambda e, i=i: e.matmul(psA[i][:], ones1f[:],
                                                  rowscr[0:1, i * 512:(i + 1) * 512],
                                                  start=True, stop=True),
                      waits=B("rowscr").read((k.sem[TE], k.sem[TE].n + 1))
                      + B("ones").read() + B("psA%d" % i).write())
            B("psA%d" % i).write(sv)
            sv = k.op(VE, lambda e, i=i, hh=hh: e.tensor_copy(
                bo_b[:, hh * TOK + i * 512: hh * TOK + i * 512 + 512], psA[i][:]),
                      waits=B("psA%d" % i).read((k.sem[VE], k.sem[VE].n + 1)))
            B("bo_b").write(sv)
    k.barrier()

    sv = k.op(VE, lambda e: e.memset(sumacc[:], 0.0))
    B("sumacc").write(sv)
    selb = B("sel")
    sv = k.dma(SY, scr[:, 12:13], sel_in)
    selb.write(sv)
    for lb in range(8):
        for pi in range(4):
            u = lb * 4 + pi
            pan = scrA if u % 2 == 0 else pr_pan2
            sgn = scrB if u % 2 == 0 else pr_sgn2
            pb = B("prep_pan%d" % (u % 2))
            sb_ = B("prep_sgn%d" % (u % 2))
            lv = k.dma(SY, pan[:], prep_src[u], waits=pb.write())
            pb.write(lv)
            s1 = k.op(SC, lambda e, pan=pan, sgn=sgn: e.sign(sgn[:], pan[:]),
                      waits=pb.read((k.sem[SC], k.sem[SC].n + 1)) + sb_.write())
            v1 = k.op(VE, lambda e, pan=pan, sgn=sgn: e.tensor_sub(
                sgn[:], sgn[:], pan[:]),
                waits=[s1] + pb.read((k.sem[VE], k.sem[VE].n + 1)))
            v2 = k.op(VE, lambda e, pan=pan, sgn=sgn: e.scalar_tensor_tensor(
                sgn[:], sgn[:], scr[:, 12:13], pan[:], ALU.mult, ALU.add),
                waits=selb.read((k.sem[VE], k.sem[VE].n + 1))
                + pb.read((k.sem[VE], k.sem[VE].n + 1)))
            sb_.write(v2)
            rv = k.op(VE, lambda e, pan=pan: e.tensor_reduce(
                scr2[:, 63:64], pan[:], AX.X, ALU.add, apply_absolute_value=True),
                waits=pb.read((k.sem[VE], k.sem[VE].n + 1)) + B("scr2r").write())
            B("scr2r").write(rv)
            av = k.op(VE, lambda e: e.tensor_tensor(
                sumacc[:, 0:1], sumacc[:, 0:1], scr2[:, 63:64], ALU.add),
                waits=B("scr2r").read((k.sem[VE], k.sem[VE].n + 1)) + B("sumacc").write())
            B("sumacc").write(av)
            for rnd in range(2):
                for j in range(8):
                    dc = rnd * 8 + j
                    tv = k.op(TE, lambda e, sgn=sgn, dc=dc, j=j: e.transpose(
                        psT[:, j * 128:(j + 1) * 128], sgn[:, dc * 128:(dc + 1) * 128],
                        ident[:]),
                        waits=(sb_.read() + B("ident").read() + B("psT").write()
                               + B("asm").write())
                        if j == 0 else None, inc=(j == 7))
                if rnd == 1:
                    sb_.read((k.sem[TE], tv[1]))
                B("psT").write(tv)
                vv = k.op(VE, lambda e, pi=pi, rnd=rnd: e.tensor_copy(
                    pr_asm[:].rearrange("p (dc f j) -> p dc f j", dc=16, f=4)
                    [:, rnd * 8:(rnd + 1) * 8, pi, :],
                    psT[:].rearrange("p (dc j) -> p dc j", dc=8)),
                    waits=B("psT").read((k.sem[VE], k.sem[VE].n + 1)))
                B("asm").w.append(vv)
        wv = k.dma(SY, prep_in[lb * 1024:(lb + 1) * 1024, :]
                   .rearrange("(p a) j -> p (a j)", p=128),
                   pr_asm[:], waits=B("asm").write())
        B("asm").r.append(wv)
        B("asm").w = []
    tv = k.op(TE, lambda e: e.matmul(psD[0][0:1, 0:1], ones128f[:], sumacc[:, 0:1],
                                     start=True, stop=True),
              waits=B("sumacc").read() + B("ones").read() + B("psD0").write())
    B("psD0").write(tv)
    vv = k.op(VE, lambda e: e.tensor_copy(rowscr[0:1, 0:1], psD[0][0:1, 0:1]),
              waits=B("psD0").read((k.sem[VE], k.sem[VE].n + 1)) + B("rowscr").write())
    B("rowscr").write(vv)
    tw = k.dma(SY, prep_in[8192:8193, 0:2], rowscr[0:1, 0:1].bitcast(BF16),
               waits=B("rowscr").read((k.sem[SY], k.sem[SY].n + 16)))
    k.barrier()
    k.sem_cc.n += 1
    ccv0 = k.sem_cc.n

    def _ccp(eng, ctx, ccv0=ccv0):
        eng.collective_compute(
            "AllGather", ALU.bypass,
            replica_groups=[[0, 1, 2, 3, 4, 5, 6, 7]],
            ins=[prep_in], outs=[prep_all]).then_inc(k.sem_cc.h, 1)
    k.items["gpsimd"].append(_ccp)
    cc0_done = (k.sem_cc, ccv0)
    for r in range(6):
        lv = k.dma(SY, scrB[0:1, 16 + 2 * r:16 + 2 * r + 2],
                   prep_all[r * PRANK + 8192:r * PRANK + 8193, 0:2],
                   waits=[cc0_done] if r == 0 else None)
    ld_tails = (k.sem[SY], k.sem[SY].n)
    f32v = scrB[0:1, 16:28].bitcast(F32)
    v1 = k.op(VE, lambda e: e.tensor_add(
        rowscr[0:1, 0:3], f32v[0:1, 0:6:2], f32v[0:1, 1:6:2]),
        waits=[ld_tails] + B("rowscr").write())
    v2 = k.op(VE, lambda e: e.tensor_scalar(
        rowscr[0:1, 0:3], rowscr[0:1, 0:3], 1.0 / (DFF * D), None, ALU.mult),
        waits=[v1])
    B("rowscr").write(v2)
    tv = k.op(TE, lambda e: e.matmul(psE[:, 0:3], ones1f[:], rowscr[0:1, 0:3],
                                     start=True, stop=True),
              waits=B("rowscr").read((k.sem[TE], k.sem[TE].n + 1)) + B("psE").write())
    B("psE").write(tv)
    vv = k.op(VE, lambda e: e.tensor_copy(wsc[:, 0:3], psE[:, 0:3]),
              waits=B("psE").read((k.sem[VE], k.sem[VE].n + 1)))
    B("wsc").write(vv)
    k.barrier()

    def emit_ln(x_ap, x_buf, outs):
        stats = B("lnstats")
        vv = k.op(VE, lambda e: e.tensor_reduce(scr[:, 0:1], x_ap, AX.X, ALU.add),
                  waits=x_buf.read((k.sem[VE], k.sem[VE].n + 1)) + stats.write())
        sv = k.op(SC, lambda e: e.activation(scrB2[:], x_ap, AF.Square,
                                             accum_out=scr[:, 1:2]),
                  waits=x_buf.read((k.sem[SC], k.sem[SC].n + 1)) + B("scrB2").write())
        B("scrB2").write(sv)
        v2 = k.op(VE, lambda e: e.tensor_scalar(scr[:, 2:3], scr[:, 0:1], 1.0 / D,
                                                None, ALU.mult),
                  waits=[vv])
        v3 = k.op(VE, lambda e: e.tensor_scalar(scr[:, 3:4], scr[:, 1:2], 1.0 / D,
                                                None, ALU.mult),
                  waits=[sv])
        v4a = k.op(VE, lambda e: e.tensor_mul(scr[:, 4:5], scr[:, 2:3], scr[:, 2:3]),
                   waits=[v2])
        v4 = k.op(VE, lambda e: e.tensor_sub(scr[:, 8:9], scr[:, 3:4], scr[:, 4:5]),
                  waits=[v3, v4a])
        s2 = k.op(SC, lambda e: e.activation(scr[:, 5:6], scr[:, 8:9], AF.Sqrt,
                                             bias=cst[:, 0:1]),
                  waits=[v4] + B("cst").read())
        v5 = k.op(VE, lambda e: e.reciprocal(scr[:, 6:7], scr[:, 5:6]), waits=[s2])
        v6 = k.op(VE, lambda e: e.scalar_tensor_tensor(
            scr[:, 7:8], scr[:, 2:3], -1.0, scr[:, 6:7], ALU.mult, ALU.mult), waits=[v5])
        last = []
        for (ap, obuf) in outs:
            ov = k.op(SC, lambda e, ap=ap: e.activation(ap, x_ap, AF.Identity,
                                                        bias=scr[:, 7:8], scale=scr[:, 6:7]),
                      waits=[v6] + x_buf.read((k.sem[SC], k.sem[SC].n + 1))
                      + (obuf.write() if obuf else []))
            if obuf:
                obuf.write(ov)
            stats.r = [ov]
            last.append(ov)
        return last[-1]

    def transpose16(src_fn, dst_view_fn, src_waits, dst_buf):
        for rnd in range(2):
            for j in range(8):
                dc = rnd * 8 + j
                tv = k.op(TE, lambda e, dc=dc, j=j: e.transpose(
                    psT[:, j * 128:(j + 1) * 128], src_fn(dc), ident[:]),
                    waits=(src_waits + B("ident").read() + B("psT").write()) if j == 0
                    else None, inc=(j == 7))
            B("psT").write(tv)
            vv = k.op(VE, lambda e, rnd=rnd: e.tensor_copy(
                dst_view_fn(rnd), psT[:].rearrange("p (dc j) -> p dc j", dc=8)),
                waits=B("psT").read((k.sem[VE], k.sem[VE].n + 1)) + dst_buf.write())
            dst_buf.w.append(vv)
        return vv


    def emit_attn_a2():
        lnf_b = B("ln_f")
        for t in range(NTT):
            emit_ln(x_sb[:, t * D:(t + 1) * D], B("x%d" % t), [(scrB[:], B("scrB"))])
            transpose16(lambda dc: scrB[:, dc * 128:(dc + 1) * 128],
                        lambda rnd, t=t: ln_f[:].rearrange("p (dc t) -> p dc t", dc=16)
                        [:, rnd * 8:(rnd + 1) * 8, t * 128:(t + 1) * 128],
                        B("scrB").read(), lnf_b)
            B("scrB").r.append((k.sem[VE], k.sem[VE].n))
        kbuf, vbuf, wbuf = B("aa_k"), B("aa_v"), B("aa_w")
        lnfo_b = B("aa_lnfo")
        xo_b = B("xo")

        wld = {}

        def load_wcol(dram_w, ocb):
            ws = wbuf.write()
            for dc in range(16):
                lv = k.dma(GP, aa_w[:, dc * 512:(dc + 1) * 512],
                           dram_w[dc * 128:(dc + 1) * 128, ocb * 512:(ocb + 1) * 512],
                           waits=ws if dc == 0 else None)
                wld[dc] = lv
            wbuf.write(lv)
            wbuf.w = [wld[15]]

        load_wcol(wk_a, 0)
        for oc in range(4):
            for ts in range(2):
                pa = B("psA%d" % (2 * (oc % 2) + ts))
                for dc in range(16):
                    tv = k.op(TE, lambda e, oc=oc, ts=ts, dc=dc: e.matmul(
                        psA[2 * (oc % 2) + ts][:],
                        aa_w[:, dc * 512 + oc * 128: dc * 512 + oc * 128 + 128],
                        ln_f[:, dc * TOK + ts * 512: dc * TOK + ts * 512 + 512],
                        start=(dc == 0), stop=(dc == 15)),
                        waits=([wld[dc]] + (wbuf.read() + lnf_b.read() + pa.write()
                               if dc == 0 else [])), inc=(dc == 15))
                wbuf.r.append(tv); lnf_b.r.append(tv)
                pa.write(tv)
                vv = k.op(VE, lambda e, oc=oc, ts=ts: e.tensor_copy(
                    aa_k[:, oc * SEQ + ts * 512: oc * SEQ + ts * 512 + 512],
                    psA[2 * (oc % 2) + ts][:]),
                    waits=pa.read((k.sem[VE], k.sem[VE].n + 1)) + kbuf.write())
                kbuf.w.append(vv)
        for t in range(NTT):
            lv = k.dma(SY, scrA[:], x_oth[t * 128:(t + 1) * 128, :], waits=xo_b.write())
            xo_b.write(lv)
            emit_ln(scrA[:], xo_b, [(scrB[:], B("scrB"))])
            transpose16(lambda dc: scrB[:, dc * 128:(dc + 1) * 128],
                        lambda rnd: aa_lnfo[:].rearrange("p (dc t) -> p dc t", dc=16)
                        [:, rnd * 8:(rnd + 1) * 8, :],
                        B("scrB").read(), lnfo_b)
            B("scrB").r.append((k.sem[VE], k.sem[VE].n))
            for oc in range(4):
                pa = B("psA%d" % (oc % 2))
                for dc in range(16):
                    tv = k.op(TE, lambda e, oc=oc, dc=dc: e.matmul(
                        psA[oc % 2][:, 0:128],
                        aa_w[:, dc * 512 + oc * 128: dc * 512 + oc * 128 + 128],
                        aa_lnfo[:, dc * 128:(dc + 1) * 128],
                        start=(dc == 0), stop=(dc == 15)),
                        waits=(lnfo_b.read() + pa.write()) if dc == 0 else None,
                        inc=(dc == 15))
                lnfo_b.r.append(tv)
                wbuf.r.append(tv)
                pa.write(tv)
                vv = k.op(VE, lambda e, oc=oc, t=t: e.tensor_copy(
                    aa_k[:, oc * SEQ + TOK + t * 128: oc * SEQ + TOK + t * 128 + 128],
                    psA[oc % 2][:, 0:128]),
                    waits=pa.read((k.sem[VE], k.sem[VE].n + 1)) + kbuf.write())
                kbuf.w.append(vv)
        load_wcol(wv_a, 0)
        for tt in range(NTT):
            pa = B("psA%d" % (tt % 4))
            for dc in range(16):
                tv = k.op(TE, lambda e, tt=tt, dc=dc: e.matmul(
                    psA[tt % 4][:],
                    ln_f[:, dc * TOK + tt * 128: dc * TOK + tt * 128 + 128],
                    aa_w[:, dc * 512:(dc + 1) * 512],
                    start=(dc == 0), stop=(dc == 15)),
                    waits=(wbuf.read() + lnf_b.read() + pa.write()) if dc == 0 else None,
                    inc=(dc == 15))
            wbuf.r.append(tv); lnf_b.r.append(tv)
            pa.write(tv)
            vv = k.op(VE, lambda e, tt=tt: e.tensor_copy(
                aa_v[:, tt * 512:(tt + 1) * 512], psA[tt % 4][:]),
                waits=pa.read((k.sem[VE], k.sem[VE].n + 1)) + vbuf.write())
            vbuf.w.append(vv)
        for t in range(NTT):
            lv = k.dma(SY, scrA[:], x_oth[t * 128:(t + 1) * 128, :], waits=xo_b.write())
            xo_b.write(lv)
            emit_ln(scrA[:], xo_b, [(scrB[:], B("scrB"))])
            transpose16(lambda dc: scrB[:, dc * 128:(dc + 1) * 128],
                        lambda rnd: aa_lnfo[:].rearrange("p (dc t) -> p dc t", dc=16)
                        [:, rnd * 8:(rnd + 1) * 8, :],
                        B("scrB").read(), lnfo_b)
            B("scrB").r.append((k.sem[VE], k.sem[VE].n))
            pa = B("psA%d" % (t % 2))
            for dc in range(16):
                tv = k.op(TE, lambda e, t=t, dc=dc: e.matmul(
                    psA[t % 2][:],
                    aa_lnfo[:, dc * 128:(dc + 1) * 128],
                    aa_w[:, dc * 512:(dc + 1) * 512],
                    start=(dc == 0), stop=(dc == 15)),
                    waits=(lnfo_b.read() + pa.write()) if dc == 0 else None,
                    inc=(dc == 15))
            lnfo_b.r.append(tv)
            wbuf.r.append(tv)
            pa.write(tv)
            vv = k.op(VE, lambda e, t=t: e.tensor_copy(
                aa_v[:, (NTT + t) * 512:(NTT + t + 1) * 512], psA[t % 2][:]),
                waits=pa.read((k.sem[VE], k.sem[VE].n + 1)) + vbuf.write())
            vbuf.w.append(vv)
        qb = B("aa_q")
        for ocb in range(4):
            load_wcol(wq_a, ocb)
            for oc4 in range(4):
                for ts in range(2):
                    pa = B("psA%d" % (2 * (oc4 % 2) + ts))
                    for dc in range(16):
                        tv = k.op(TE, lambda e, oc4=oc4, ts=ts, dc=dc: e.matmul(
                            psA[2 * (oc4 % 2) + ts][:],
                            aa_w[:, dc * 512 + oc4 * 128: dc * 512 + oc4 * 128 + 128],
                            ln_f[:, dc * TOK + ts * 512: dc * TOK + ts * 512 + 512],
                            start=(dc == 0), stop=(dc == 15)),
                            waits=([wld[dc]] + (wbuf.read() + lnf_b.read()
                                   + pa.write() if dc == 0 else [])), inc=(dc == 15))
                    wbuf.r.append(tv); lnf_b.r.append(tv)
                    pa.write(tv)
                    oc = ocb * 4 + oc4
                    vv = k.op(VE, lambda e, oc=oc, ts=ts, oc4=oc4: e.tensor_copy(
                        aa_q[:, oc * TOK + ts * 512: oc * TOK + ts * 512 + 512],
                        psA[2 * (oc4 % 2) + ts][:]),
                        waits=pa.read((k.sem[VE], k.sem[VE].n + 1)) + qb.write())
                    qb.w.append(vv)

        eb = [B("aa_exp0"), B("aa_exp1")]
        cb = B("aa_ctx")
        proj_gate = [(k.sem[TE], k.sem[TE].n), (k.sem[VE], k.sem[VE].n)]
        mb = B("aa_mask")
        cur_mask_qq = [-1]

        def load_mask(qq):
            if cur_mask_qq[0] != qq:
                lv = k.dma(SY, aa_mask[:].rearrange("p (c t) -> p c t", c=16),
                           maskT_in.rearrange("(c p) t -> p c t", p=128)
                           [:, :, qq * 256:(qq + 1) * 256],
                           waits=mb.write() + (proj_gate if qq == 0 else []))
                mb.write(lv)
                cur_mask_qq[0] = qq

        def emit_scores(i):
            qq, h = divmod(i, NH)
            g = h // 4
            p = i % 2
            load_mask(qq)
            q_ap = aa_q[:, h * TOK + qq * 256: h * TOK + qq * 256 + 256]
            exp_t = aa_exp[0] if p == 0 else aa_exp[1]
            for kt in range(16):
                pa = B("psA%d" % (kt % 2))
                tv = k.op(TE, lambda e, g=g, kt=kt, q_ap=q_ap: e.matmul(
                    psA[kt % 2][:, 0:256],
                    aa_k[:, g * SEQ + kt * 128: g * SEQ + kt * 128 + 128],
                    q_ap, start=True, stop=True),
                    waits=pa.write() + (kbuf.read() + qb.read() if i == 0 and kt == 0
                                        else []))
                pa.write(tv)
                sv = k.op(SC, lambda e, kt=kt, exp_t=exp_t: e.activation(
                    exp_t[:, kt * 256:(kt + 1) * 256], psA[kt % 2][:, 0:256],
                    AF.Exp, bias=cst[:, 1:2], scale=cst[:, 2:3]),
                    waits=pa.read((k.sem[SC], k.sem[SC].n + 1)) + eb[p].write()
                    + B("cst").read())
                eb[p].w.append(sv)
            if i == 0:
                kbuf.r.append(tv); qb.r.append(tv)
            mv = k.op(VE, lambda e, exp_t=exp_t: e.tensor_tensor(
                exp_t[:], exp_t[:], aa_mask[:], ALU.mult),
                waits=eb[p].write() + mb.read((k.sem[VE], k.sem[VE].n + 1)))
            eb[p].write(mv)

        def emit_tail(i):
            qq, h = divmod(i, NH)
            g = h // 4
            p = i % 2
            exp_t = aa_exp[0] if p == 0 else aa_exp[1]
            pd = B("psD0")
            for kt in range(16):
                tv = k.op(TE, lambda e, kt=kt, exp_t=exp_t: e.matmul(
                    psD[0][:, 0:256], ones128[:], exp_t[:, kt * 256:(kt + 1) * 256],
                    start=(kt == 0), stop=(kt == 15)),
                    waits=(eb[p].read() + pd.write() + B("ones").read())
                    if kt == 0 else None, inc=(kt == 15))
            eb[p].r.append(tv)
            pd.write(tv)
            vr = k.op(VE, lambda e: e.reciprocal(rowscr[0:1, 0:256], psD[0][0:1, 0:256]),
                      waits=pd.read((k.sem[VE], k.sem[VE].n + 1)) + B("rowscr").write())
            B("rowscr").write(vr)
            pc = B("psA2")
            for kt in range(16):
                tv = k.op(TE, lambda e, kt=kt, g=g, exp_t=exp_t: e.matmul(
                    psA[2][:, 0:256],
                    aa_v[:, kt * 512 + g * 128: kt * 512 + g * 128 + 128],
                    exp_t[:, kt * 256:(kt + 1) * 256],
                    start=(kt == 0), stop=(kt == 15)),
                    waits=(eb[p].read() + pc.write() + vbuf.read()) if kt == 0 else None,
                    inc=(kt == 15))
            eb[p].r.append(tv); vbuf.r.append(tv)
            pc.write(tv)
            pe = B("psE")
            tb = k.op(TE, lambda e: e.matmul(psE[:, 0:256], ones1f[:],
                                             rowscr[0:1, 0:256], start=True, stop=True),
                      waits=B("rowscr").read((k.sem[TE], k.sem[TE].n + 1)) + pe.write())
            pe.write(tb)
            sre = k.op(SC, lambda e: e.copy(scrA[:, 0:256], psE[:, 0:256]),
                       waits=pe.read((k.sem[SC], k.sem[SC].n + 1)) + B("scrAh").write())
            B("scrAh").write(sre)
            vv = k.op(VE, lambda e, h=h, qq=qq: e.tensor_tensor(
                aa_ctx[:, h * TOK + qq * 256: h * TOK + qq * 256 + 256],
                psA[2][:, 0:256], scrA[:, 0:256], ALU.mult),
                waits=pc.read((k.sem[VE], k.sem[VE].n + 1))
                + B("scrAh").read((k.sem[VE], k.sem[VE].n + 1)) + cb.write())
            cb.w.append(vv)

        emit_scores(0)
        for i in range(4 * NH):
            if i + 1 < 4 * NH:
                emit_scores(i + 1)
            emit_tail(i)

        for t in range(NTT):
            xb = B("x%d" % t)
            vv = k.op(VE, lambda e, t=t: e.tensor_add(
                x_sb[:, t * D:(t + 1) * D], x_sb[:, t * D:(t + 1) * D], bo_b[:]),
                waits=xb.write() + B("bo_b").read())
            xb.write(vv)
        w2b = B("aa_w2")
        head_gate = [(k.sem[TE], k.sem[TE].n), (k.sem[SC], k.sem[SC].n)]
        wld2 = {}
        for ocb in range(4):
            ws = w2b.write() + (head_gate if ocb == 0 else [])
            for dc in range(16):
                lv = k.dma(GP, aa_w2[:, dc * 512:(dc + 1) * 512],
                           wo_a[dc * 128:(dc + 1) * 128, ocb * 512:(ocb + 1) * 512],
                           waits=ws if dc == 0 else None)
                wld2[dc] = lv
            w2b.write(lv)
            w2b.w = [wld2[15]]
            for tt in range(NTT):
                pa = B("psA%d" % (tt % 4))
                for dc in range(16):
                    tv = k.op(TE, lambda e, tt=tt, dc=dc: e.matmul(
                        psA[tt % 4][:],
                        aa_ctx[:, dc * TOK + tt * 128: dc * TOK + tt * 128 + 128],
                        aa_w2[:, dc * 512:(dc + 1) * 512],
                        start=(dc == 0), stop=(dc == 15)),
                        waits=([wld2[dc]] + (w2b.read() + cb.read() + pa.write()
                               if dc == 0 else [])), inc=(dc == 15))
                w2b.r.append(tv); cb.r.append(tv)
                pa.write(tv)
                xb = B("x%d" % tt)
                vv = k.op(VE, lambda e, tt=tt, ocb=ocb: e.tensor_add(
                    x_sb[:, tt * D + ocb * 512: tt * D + ocb * 512 + 512],
                    x_sb[:, tt * D + ocb * 512: tt * D + ocb * 512 + 512],
                    psA[tt % 4][:]),
                    waits=pa.read((k.sem[VE], k.sem[VE].n + 1)) + xb.write())
                xb.write(vv)

    def emit_ffn():
        qab, qhb = B("ff_qa"), B("ff_qh")
        w1b, w2b_ = B("ff_w1"), B("ff_w2")
        w3b = [B("ff_w30"), B("ff_w31")]
        pend_q = []

        def emit_qh_transpose(si, tl, qt, qtb):
            for i in range(4):
                tv = k.op(TE, lambda e, i=i, qt=qt: e.transpose(
                    psT[:, i * 128:(i + 1) * 128],
                    qt[:, i * 128:(i + 1) * 128], ident[:]),
                    waits=(qtb.read() + B("psT").write()) if i == 0 else None,
                    inc=(i == 3))
            qtb.r.append((k.sem[TE], tv[1]))
            B("psT").write(tv)
            vq = k.op(VE, lambda e, si=si, tl=tl: e.tensor_copy(
                ff_qh[:].rearrange("p (c t) -> p c t", c=16)
                [:, 4 * si:4 * si + 4, tl * 128:(tl + 1) * 128],
                psT[:, 0:512].rearrange("p (c t) -> p c t", c=4)),
                waits=B("psT").read((k.sem[VE], k.sem[VE].n + 1)) + qhb.write())
            qhb.w.append(vq)
        for t in range(NTT):
            emit_ln(x_sb[:, t * D:(t + 1) * D], B("x%d" % t),
                    [(ff_ln[:], B("ff_ln"))])
            flb = B("ff_ln")
            v1 = k.op(VE, lambda e: e.tensor_reduce(
                scr2[:, 0:32], ff_ln[:].rearrange("p (g c) -> p g c", g=32),
                AX.X, ALU.max, apply_absolute_value=True),
                waits=flb.read((k.sem[VE], k.sem[VE].n + 1)) + B("qs").write())
            v2 = k.op(VE, lambda e: e.tensor_scalar(
                scr2[:, 32:64], scr2[:, 0:32], 1e-5, 1.0 / 127.0, ALU.max, ALU.mult),
                waits=[v1])
            v3 = k.op(VE, lambda e: e.reciprocal(scr2[:, 0:32], scr2[:, 32:64]),
                      waits=[v2])
            B("qs").write(v3)
            qib = B("ff_qi")
            v4 = k.op(VE, lambda e: e.tensor_tensor(
                ff_qi[:].rearrange("p (g c) -> p g c", g=32),
                ff_ln[:].rearrange("p (g c) -> p g c", g=32),
                scr2[:, 0:32].rearrange("p (g u) -> p g u", u=1)
                .broadcast_to([128, 32, 64]), ALU.mult),
                waits=[v3] + flb.read((k.sem[VE], k.sem[VE].n + 1)) + qib.write())
            qib.write(v4)
            v5 = k.op(VE, lambda e: e.tensor_tensor(
                scrB[:].rearrange("p (g c) -> p g c", g=32),
                ff_qi[:].rearrange("p (g c) -> p g c", g=32),
                scr2[:, 32:64].rearrange("p (g u) -> p g u", u=1)
                .broadcast_to([128, 32, 64]), ALU.mult),
                waits=[v4] + B("scrB").write() + B("qs").read((k.sem[VE], k.sem[VE].n + 1)))
            B("scrB").write(v5)
            transpose16(lambda dc: scrB[:, dc * 128:(dc + 1) * 128],
                        lambda rnd, t=t: ff_qa[:].rearrange("p (dc t) -> p dc t", dc=16)
                        [:, rnd * 8:(rnd + 1) * 8, t * 128:(t + 1) * 128],
                        B("scrB").read(), qab)
            B("scrB").r.append((k.sem[VE], k.sem[VE].n))

        for g in range(4):
            for si in range(4):
                sl = g * 4 + si
                w1q = {}
                ws = w1b.write()
                for qq_ in range(4):
                    l1 = k.dma(SY, ff_w1[:, qq_ * 2048:(qq_ + 1) * 2048],
                               w1T[sl][:, qq_ * 2048:(qq_ + 1) * 2048],
                               waits=ws if qq_ == 0 else None)
                    w1q[qq_] = l1
                w1b.write(l1)
                w1b.w = [w1q[3]]
                w2q = {}
                l2 = None
                for tl in range(NTT):
                    pa = B("psA%d" % (tl % 2))
                    pb2 = B("psA%d" % (2 + tl % 2))
                    for dc in range(16):
                        tv = k.op(TE, lambda e, tl=tl, dc=dc: e.matmul(
                            psA[tl % 2][:],
                            ff_qa[:, dc * TOK + tl * 128: dc * TOK + tl * 128 + 128],
                            ff_w1[:, dc * 512:(dc + 1) * 512],
                            start=(dc == 0), stop=(dc == 15)),
                            waits=([w1q[dc // 4]] + (w1b.read() + qab.read()
                                   + pa.write() if dc == 0 else [])), inc=(dc == 15))
                    w1b.r.append(tv); qab.r.append(tv)
                    pa.write(tv)
                    if tl == 0:
                        ws = w2b_.write()
                        for qq_ in range(4):
                            l2 = k.dma(SY, ff_w2[:, qq_ * 2048:(qq_ + 1) * 2048],
                                       w2T[sl][:, qq_ * 2048:(qq_ + 1) * 2048],
                                       waits=ws if qq_ == 0 else None)
                            w2q[qq_] = l2
                        w2b_.write(l2)
                        w2b_.w = [w2q[3]]
                    for dc in range(16):
                        tv2 = k.op(TE, lambda e, tl=tl, dc=dc: e.matmul(
                            psA[2 + tl % 2][:],
                            ff_qa[:, dc * TOK + tl * 128: dc * TOK + tl * 128 + 128],
                            ff_w2[:, dc * 512:(dc + 1) * 512],
                            start=(dc == 0), stop=(dc == 15)),
                            waits=([w2q[dc // 4]] + (w2b_.read() + pb2.write()
                                   if dc == 0 else [])), inc=(dc == 15))
                    w2b_.r.append(tv2)
                    pb2.write(tv2)
                    sh = k.op(SC, lambda e, tl=tl: e.activation(
                        ff_sl[:], psA[tl % 2][:], AF.Silu, scale=wsc[:, 0:1]),
                        waits=pa.read((k.sem[SC], k.sem[SC].n + 1)) + B("ff_sl").write()
                        + B("wsc").read())
                    B("ff_sl").write(sh)
                    vh = k.op(VE, lambda e, tl=tl: e.scalar_tensor_tensor(
                        ff_h[:], psA[2 + tl % 2][:], wsc[:, 1:2], ff_sl[:],
                        ALU.mult, ALU.mult),
                        waits=pb2.read((k.sem[VE], k.sem[VE].n + 1))
                        + B("ff_sl").read((k.sem[VE], k.sem[VE].n + 1))
                        + B("ff_h").write() + B("wsc").read())
                    B("ff_h").write(vh)
                    hv1 = k.op(VE, lambda e: e.tensor_reduce(
                        scr2[:, 0:8], ff_h[:].rearrange("p (gg c) -> p gg c", gg=8),
                        AX.X, ALU.max, apply_absolute_value=True),
                        waits=B("ff_h").read((k.sem[VE], k.sem[VE].n + 1))
                        + B("qs2").write())
                    hv2 = k.op(VE, lambda e: e.tensor_scalar(
                        scr2[:, 8:16], scr2[:, 0:8], 1e-5, 1.0 / 127.0,
                        ALU.max, ALU.mult), waits=[hv1])
                    hv3 = k.op(VE, lambda e: e.reciprocal(scr2[:, 0:8], scr2[:, 8:16]),
                               waits=[hv2])
                    B("qs2").write(hv3)
                    qhib = B("ff_qih")
                    hv4 = k.op(VE, lambda e: e.tensor_tensor(
                        ff_qih[:].rearrange("p (gg c) -> p gg c", gg=8),
                        ff_h[:].rearrange("p (gg c) -> p gg c", gg=8),
                        scr2[:, 0:8].rearrange("p (gg u) -> p gg u", u=1)
                        .broadcast_to([128, 8, 64]), ALU.mult),
                        waits=[hv3] + B("ff_h").read((k.sem[VE], k.sem[VE].n + 1))
                        + qhib.write())
                    qhib.write(hv4)
                    qt = ff_qt if (tl % 2 == 0) else ff_qt2
                    qtb = B("ff_qt%d" % (tl % 2))
                    hv5 = k.op(VE, lambda e, qt=qt: e.tensor_tensor(
                        qt[:].rearrange("p (gg c) -> p gg c", gg=8),
                        ff_qih[:].rearrange("p (gg c) -> p gg c", gg=8),
                        scr2[:, 8:16].rearrange("p (gg u) -> p gg u", u=1)
                        .broadcast_to([128, 8, 64]), ALU.mult),
                        waits=[hv4] + qtb.write()
                        + B("qs2").read((k.sem[VE], k.sem[VE].n + 1)))
                    qtb.write(hv5)
                    pend_q.append((si, tl, qt, qtb))
                    if len(pend_q) > 1:
                        emit_qh_transpose(*pend_q.pop(0))
            while pend_q:
                emit_qh_transpose(*pend_q.pop(0))
            for dms in range(4):
                lv0 = k.dma(SY, ff_w3[0][:], w3T[dms][g][:, 0:4096],
                            waits=w3b[0].write())
                w3b[0].write(lv0)
                lv1 = k.dma(SY, ff_w3[1][:], w3T[dms][g][:, 4096:8192],
                            waits=w3b[1].write())
                w3b[1].write(lv1)
                for tl in range(NTT):
                    pa = B("psA%d" % (tl % 4))
                    for c in range(16):
                        hh = c // 8
                        tv = k.op(TE, lambda e, tl=tl, c=c, hh=hh: e.matmul(
                            psA[tl % 4][:],
                            ff_qh[:, c * TOK + tl * 128: c * TOK + tl * 128 + 128],
                            ff_w3[hh][:, (c % 8) * 512:(c % 8 + 1) * 512],
                            start=(c == 0), stop=(c == 15)),
                            waits=(w3b[hh].read() + qhb.read() + pa.write())
                            if c in (0, 8) else None, inc=(c in (7, 15)))
                        if c in (7, 15):
                            w3b[hh].r.append(tv)
                    qhb.r.append(tv)
                    pa.write(tv)
                    xb = B("x%d" % tl)
                    vv = k.op(VE, lambda e, tl=tl, dms=dms: e.scalar_tensor_tensor(
                        x_sb[:, tl * D + dms * 512: tl * D + dms * 512 + 512],
                        psA[tl % 4][:], wsc[:, 2:3],
                        x_sb[:, tl * D + dms * 512: tl * D + dms * 512 + 512],
                        ALU.mult, ALU.add),
                        waits=pa.read((k.sem[VE], k.sem[VE].n + 1)) + xb.write()
                        + B("wsc").read())
                    xb.write(vv)

    def emit_attn_b():
        lnb = B("ab_ln")
        for t in range(NTT):
            emit_ln(x_sb[:, t * D:(t + 1) * D], B("x%d" % t), [(scrB[:], B("scrB"))])
            transpose16(lambda dc: scrB[:, dc * 128:(dc + 1) * 128],
                        lambda rnd, t=t: ab_ln[:].rearrange("p (dc t) -> p dc t", dc=16)
                        [:, rnd * 8:(rnd + 1) * 8, t * 128:(t + 1) * 128],
                        B("scrB").read(), lnb)
            B("scrB").r.append((k.sem[VE], k.sem[VE].n))

        wb = B("ab_w")
        def proj_block(wblk, ocb, out_fn, out_buf):
            ws = wb.write()
            bq = {}
            for qq_ in range(4):
                lv = k.dma(SY, ab_w[:, qq_ * 2048:(qq_ + 1) * 2048],
                           wblk[ocb][:, qq_ * 2048:(qq_ + 1) * 2048],
                           waits=ws if qq_ == 0 else None)
                bq[qq_] = lv
            wb.write(lv)
            wb.w = [bq[3]]
            for oc4 in range(4):
                for ts in range(2):
                    pa = B("psA%d" % (2 * (oc4 % 2) + ts))
                    for dc in range(16):
                        tv = k.op(TE, lambda e, oc4=oc4, ts=ts, dc=dc: e.matmul(
                            psA[2 * (oc4 % 2) + ts][:],
                            ab_w[:, dc * 512 + oc4 * 128: dc * 512 + oc4 * 128 + 128],
                            ab_ln[:, dc * TOK + ts * 512: dc * TOK + ts * 512 + 512],
                            start=(dc == 0), stop=(dc == 15)),
                            waits=([bq[dc // 4]] + (wb.read() + lnb.read() + pa.write()
                                   if dc == 0 else [])), inc=(dc == 15))
                    wb.r.append(tv); lnb.r.append(tv)
                    pa.write(tv)
                    vv = k.op(VE, lambda e, oc4=oc4, ts=ts: e.tensor_copy(
                        out_fn(ocb * 4 + oc4, ts), psA[2 * (oc4 % 2) + ts][:]),
                        waits=pa.read((k.sem[VE], k.sem[VE].n + 1)) + out_buf.write())
                    out_buf.w.append(vv)

        cosb, sinb = B("ab_cos"), B("ab_sin")
        def rope_pair(raw, ri, rj, c, cp, dst_fn, dst_buf, rb_):
            lc1 = k.dma(SY, ab_cos[:, 0:TOK], cos_in[c * 128:(c + 1) * 128, :],
                        waits=cosb.write())
            lc2 = k.dma(SY, ab_cos[:, TOK:2 * TOK], cos_in[cp * 128:(cp + 1) * 128, :])
            cosb.write(lc2)
            ls1 = k.dma(SY, ab_sin[:, 0:TOK], sin_in[c * 128:(c + 1) * 128, :],
                        waits=sinb.write())
            ls2 = k.dma(SY, ab_sin[:, TOK:2 * TOK], sin_in[cp * 128:(cp + 1) * 128, :])
            sinb.write(ls2)
            for (cc, half, oth, sgn) in ((c, 0, rj, -1.0), (cp, 1, ri, 1.0)):
                me = ri if cc == c else rj
                v1 = k.op(VE, lambda e, me=me, half=half: e.tensor_tensor(
                    scrA[:, 0:TOK], raw[:, me * TOK:(me + 1) * TOK],
                    ab_cos[:, half * TOK:(half + 1) * TOK], ALU.mult),
                    waits=cosb.read((k.sem[VE], k.sem[VE].n + 1))
                    + rb_.read((k.sem[VE], k.sem[VE].n + 1)) + B("scrAr").write())
                v2 = k.op(VE, lambda e, oth=oth, half=half: e.tensor_tensor(
                    scrA[:, TOK:2 * TOK], raw[:, oth * TOK:(oth + 1) * TOK],
                    ab_sin[:, half * TOK:(half + 1) * TOK], ALU.mult),
                    waits=sinb.read((k.sem[VE], k.sem[VE].n + 1))
                    + rb_.read((k.sem[VE], k.sem[VE].n + 1)))
                v3 = k.op(VE, lambda e, cc=cc, sgn=sgn: e.scalar_tensor_tensor(
                    dst_fn(cc), scrA[:, TOK:2 * TOK], sgn, scrA[:, 0:TOK],
                    ALU.mult, ALU.add),
                    waits=[v1, v2] + dst_buf.write())
                B("scrAr").write(v3)
                dst_buf.w.append(v3)

        qb = B("ab_q")
        qrb = B("ab_qr")
        for half in range(2):
            ocbs = (0, 2) if half == 0 else (1, 3)
            for i, ocb in enumerate(ocbs):
                proj_block(wqbT, ocb,
                           lambda oc, ts, i=i: ab_qr[:, (i * 4 + oc % 4) * TOK + ts * 512:
                                                     (i * 4 + oc % 4) * TOK + ts * 512 + 512],
                           qrb)
            for j in range(4):
                c = half * 4 + j
                cp = c + 8
                rope_pair(ab_qr, j, 4 + j, c, cp,
                          lambda cc: ab_q[:, cc * TOK:(cc + 1) * TOK], qb, qrb)
            qrb.r = qb.w[-8:]
            qrb.w = []
        kst = B("kstage")
        for half in range(2):
            ocbs = (0, 2) if half == 0 else (1, 3)
            for i, ocb in enumerate(ocbs):
                proj_block(wkbT, ocb,
                           lambda oc, ts, i=i: ab_qr[:, (i * 4 + oc % 4) * TOK + ts * 512:
                                                     (i * 4 + oc % 4) * TOK + ts * 512 + 512],
                           qrb)
            for j in range(4):
                c = half * 4 + j
                cp = c + 8

                def kdst(cc, c=c):
                    return (scrB[:, 0:TOK] if cc == c else scrB2[:, 0:TOK])
                rope_pair(ab_qr, j, 4 + j, c, cp, kdst, kst, qrb)
                d1 = k.dma(SY, kv_pack[c * 128:(c + 1) * 128, :], scrB[:, 0:TOK],
                           waits=kst.write())
                d2 = k.dma(SY, kv_pack[cp * 128:(cp + 1) * 128, :], scrB2[:, 0:TOK])
                kst.r = [d1, d2]
                kst.w = []
                B("scrB").r.append(d1)
                B("scrB2").r.append(d2)
            qrb.r.append((k.sem[SY], k.sem[SY].n))
            qrb.w = []
        for ocb in range(4):
            ws = wb.write()
            bq = {}
            for qq_ in range(4):
                lv = k.dma(SY, ab_w[:, qq_ * 2048:(qq_ + 1) * 2048],
                           wvbT[ocb][:, qq_ * 2048:(qq_ + 1) * 2048],
                           waits=ws if qq_ == 0 else None)
                bq[qq_] = lv
            wb.write(lv)
            wb.w = [bq[3]]
            for tt in range(NTT):
                pa = B("psA%d" % (tt % 4))
                for dc in range(16):
                    tv = k.op(TE, lambda e, tt=tt, dc=dc: e.matmul(
                        psA[tt % 4][:],
                        ab_ln[:, dc * TOK + tt * 128: dc * TOK + tt * 128 + 128],
                        ab_w[:, dc * 512:(dc + 1) * 512],
                        start=(dc == 0), stop=(dc == 15)),
                        waits=([bq[dc // 4]] + (wb.read() + lnb.read() + pa.write()
                               if dc == 0 else [])), inc=(dc == 15))
                wb.r.append(tv); lnb.r.append(tv)
                pa.write(tv)
                sv = k.op(SC, lambda e, tt=tt: e.copy(scrB[:, 1536:2048], psA[tt % 4][:]),
                          waits=pa.read((k.sem[SC], k.sem[SC].n + 1)) + B("vstage").write())
                B("vstage").write(sv)
                r0 = D + tt * 256 + ocb // 2
                dv = k.dma(SY,
                           kv_pack[r0:r0 + 255:2,
                                   (ocb % 2) * 512:(ocb % 2) * 512 + 512],
                           scrB[:, 1536:2048],
                           waits=B("vstage").read((k.sem[SY], k.sem[SY].n + 16)))
        kv_done = (k.sem[SY], k.sem[SY].n)

        k.sem_cc.n += 1
        ccv = k.sem_cc.n
        kvd = kv_done

        def _cc(eng, ctx, kvd=kvd, ccv=ccv):
            if ctx.get(kvd[0].h.num, -1) < kvd[1]:
                eng.wait_ge(kvd[0].h, kvd[1])
                ctx[kvd[0].h.num] = kvd[1]
            eng.collective_compute(
                "AllGather", ALU.bypass,
                replica_groups=[[0, 1], [2, 3], [4, 5], [6, 7]],
                ins=[kv_pack], outs=[kv_g]).then_inc(k.sem_cc.h, 1)
        k.items["gpsimd"].append(_cc)
        cc_done = (k.sem_cc, ccv)


        qdone_gate = [(k.sem[TE], k.sem[TE].n), (k.sem[VE], k.sem[VE].n)]

        eb = [B("ab_exp0"), B("ab_exp1")]
        cb = B("ab_ctx")

        def prologue(h):
            hp = h % 2
            kvt = ab_kv[hp]
            kvb = B("ab_kv%d" % hp)
            ws_ = kvb.write()
            l1 = k.dma(SY, kvt[:, 0:TOK], kv_g[h * 128:(h + 1) * 128, :],
                       waits=ws_ + [cc_done])
            l2 = k.dma(SY, kvt[:, TOK:SEQ], kv_g[2 * D + h * 128: 2 * D + h * 128 + 128, :])
            for a in range(16):
                half, al = a // 8, a % 8
                r0 = half * 2 * D + D + al * 256 + h // 8
                l3 = k.dma(SY, kvt[:, SEQ + a * 128: SEQ + (a + 1) * 128],
                           kv_g[r0:r0 + 255:2, (h % 8) * 128:(h % 8) * 128 + 128])
            kvb.write(l3)
            v1 = k.op(VE, lambda e, kvt=kvt: e.tensor_scalar_min(
                scrA[:], kvt[:, 0:SEQ], 0.0),
                waits=kvb.read((k.sem[VE], k.sem[VE].n + 1)) + B("scrAr").write())
            B("scrAr").write(v1)
            s1 = k.op(SC, lambda e: e.activation(scrB[:], scrA[:], AF.Exp),
                      waits=B("scrAr").read((k.sem[SC], k.sem[SC].n + 1)) + B("scrB").write())
            B("scrB").write(s1)
            kfb = B("ab_kf")
            v2 = k.op(VE, lambda e, kvt=kvt: e.scalar_tensor_tensor(
                ab_kf[:], kvt[:, 0:SEQ], 0.0, scrB[:], ALU.max, ALU.add),
                waits=[s1] + kvb.read((k.sem[VE], k.sem[VE].n + 1)) + kfb.write()
                + B("scrB").read((k.sem[VE], k.sem[VE].n + 1)))
            kfb.write(v2)
            v3 = k.op(VE, lambda e: e.tensor_reduce(scr2[:, 0:1], ab_kf[:], AX.X, ALU.add),
                      waits=kfb.read((k.sem[VE], k.sem[VE].n + 1)) + B("ntf").write())
            B("ntf").write(v3)
            v4 = k.op(VE, lambda e, hp=hp: e.tensor_copy(ntb[:, hp:hp + 1], scr2[:, 0:1]),
                      waits=B("ntf").read((k.sem[VE], k.sem[VE].n + 1)) + B("ntb%d" % hp).write())
            B("ntb%d" % hp).write(v4)
            kftb = B("ab_kft")
            transpose16(lambda dc: ab_kf[:, dc * 128:(dc + 1) * 128],
                        lambda rnd: ab_kft[:].rearrange("p (dc t) -> p dc t", dc=16)
                        [:, rnd * 8:(rnd + 1) * 8, :], kfb.read(), kftb)
            kfb.r.append((k.sem[TE], k.sem[TE].n))
            v5 = k.op(VE, lambda e, h=h: e.tensor_scalar_min(
                scrA[:, 0:TOK], ab_q[:, h * TOK:(h + 1) * TOK], 0.0),
                waits=qb.read((k.sem[VE], k.sem[VE].n + 1)) + B("scrAr").write())
            B("scrAr").write(v5)
            s2 = k.op(SC, lambda e: e.activation(scrB[:, 0:TOK], scrA[:, 0:TOK], AF.Exp),
                      waits=B("scrAr").read((k.sem[SC], k.sem[SC].n + 1)) + B("scrB").write())
            B("scrB").write(s2)
            qfb = B("ab_qf%d" % hp)
            v6 = k.op(VE, lambda e, h=h, hp=hp: e.scalar_tensor_tensor(
                ab_qf[hp][:], ab_q[:, h * TOK:(h + 1) * TOK], 0.0, scrB[:, 0:TOK],
                ALU.max, ALU.add),
                waits=[s2] + qb.read((k.sem[VE], k.sem[VE].n + 1)) + qfb.write()
                + B("scrB").read((k.sem[VE], k.sem[VE].n + 1)))
            qfb.write(v6)
            pm = B("psE")
            for st_ in range(16):
                tv = k.op(TE, lambda e, st_=st_, kvt=kvt: e.matmul(
                    psE[:, 0:128], ab_kft[:, st_ * 128:(st_ + 1) * 128],
                    kvt[:, SEQ + st_ * 128: SEQ + (st_ + 1) * 128],
                    start=(st_ == 0), stop=(st_ == 15)),
                    waits=(kftb.read() + kvb.read() + pm.write()) if st_ == 0 else None,
                    inc=(st_ == 15))
            kftb.r.append(tv); kvb.r.append(tv)
            pm.write(tv)
            sm = k.op(SC, lambda e, hp=hp: e.copy(ab_mem[hp][:], psE[:, 0:128]),
                      waits=pm.read((k.sem[SC], k.sem[SC].n + 1)) + B("ab_mem%d" % hp).write())
            B("ab_mem%d" % hp).write(sm)

        def scores_b(i):
            h, qq = divmod(i, 4)
            hp, p = h % 2, i % 2
            kvt = ab_kv[hp]
            kvb = B("ab_kv%d" % hp)
            exp_t = ab_exp[p]
            qof = qq * 256
            for kt in range(16):
                pa = B("psA%d" % (kt % 2))
                tv = k.op(TE, lambda e, kt=kt, h=h, qof=qof, kvt=kvt: e.matmul(
                    psA[kt % 2][:, 0:256], kvt[:, kt * 128:(kt + 1) * 128],
                    ab_q[:, h * TOK + qof: h * TOK + qof + 256],
                    start=True, stop=True),
                    waits=pa.write() + (kvb.read() if kt == 0 else []))
                if kt == 0:
                    kvb.r.append(tv)
                pa.write(tv)
                sv = k.op(SC, lambda e, kt=kt, exp_t=exp_t: e.activation(
                    exp_t[:, kt * 256:(kt + 1) * 256], psA[kt % 2][:, 0:256], AF.Exp,
                    bias=cst[:, 1:2], scale=cst[:, 2:3]),
                    waits=pa.read((k.sem[SC], k.sem[SC].n + 1)) + eb[p].write()
                    + B("cst").read())
                eb[p].w.append(sv)
            kvb.r.append(tv)

        def tail_b(i):
            h, qq = divmod(i, 4)
            hp, p = h % 2, i % 2
            kvt = ab_kv[hp]
            kvb = B("ab_kv%d" % hp)
            exp_t = ab_exp[p]
            qfb = B("ab_qf%d" % hp)
            qof = qq * 256
            pd0, pd1 = B("psD0"), B("psD1")
            for kt in range(16):
                tv = k.op(TE, lambda e, kt=kt, exp_t=exp_t: e.matmul(
                    psD[0][:, 0:256], ones128[:], exp_t[:, kt * 256:(kt + 1) * 256],
                    start=(kt == 0), stop=(kt == 15)),
                    waits=(eb[p].read() + pd0.write() + B("ones").read())
                    if kt == 0 else None, inc=(kt == 15))
            eb[p].r.append(tv)
            pd0.write(tv)
            tl_ = k.op(TE, lambda e, qof=qof, hp=hp: e.matmul(
                psD[1][:, 0:256], ntb[:, hp:hp + 1], ab_qf[hp][:, qof:qof + 256],
                start=True, stop=True),
                waits=B("ntb%d" % hp).read() + qfb.read() + pd1.write())
            B("ntb%d" % hp).r.append(tl_); qfb.r.append(tl_)
            pd1.write(tl_)
            pc = B("psA2")
            for kt in range(16):
                tv = k.op(TE, lambda e, kt=kt, kvt=kvt, exp_t=exp_t: e.matmul(
                    psA[2][:, 0:256], kvt[:, SEQ + kt * 128: SEQ + (kt + 1) * 128],
                    exp_t[:, kt * 256:(kt + 1) * 256],
                    start=(kt == 0), stop=(kt == 15)),
                    waits=(eb[p].read() + pc.write() + kvb.read()) if kt == 0 else None,
                    inc=(kt == 15))
            eb[p].r.append(tv); kvb.r.append(tv)
            pc.write(tv)
            pmo = B("psA3")
            tv = k.op(TE, lambda e, qof=qof, hp=hp: e.matmul(
                psA[3][:, 0:256], ab_mem[hp][:], ab_qf[hp][:, qof:qof + 256],
                start=True, stop=True),
                waits=B("ab_mem%d" % hp).read() + qfb.read() + pmo.write())
            B("ab_mem%d" % hp).r.append(tv); qfb.r.append(tv)
            pmo.write(tv)
            vr1 = k.op(VE, lambda e: e.reciprocal(rowscr[0:1, 0:256], psD[0][0:1, 0:256]),
                       waits=pd0.read((k.sem[VE], k.sem[VE].n + 1)) + B("rowscr").write())
            vr2 = k.op(VE, lambda e: e.reciprocal(rowscr[0:1, 256:512], psD[1][0:1, 0:256]),
                       waits=pd1.read((k.sem[VE], k.sem[VE].n + 1)))
            B("rowscr").write(vr2)
            pa0, pa1 = B("psA0"), B("psA1")
            tb1 = k.op(TE, lambda e: e.matmul(psA[0][:, 0:256], ones1f[:],
                                              rowscr[0:1, 0:256], start=True, stop=True),
                       waits=[vr1, vr2] + pa0.write())
            B("rowscr").r.append(tb1)
            pa0.write(tb1)
            tb2 = k.op(TE, lambda e: e.matmul(psA[1][:, 0:256], ones1f[:],
                                              rowscr[0:1, 256:512],
                                              start=True, stop=True),
                       waits=pa1.write())
            B("rowscr").r.append(tb2)
            pa1.write(tb2)
            sc1 = k.op(SC, lambda e: e.copy(scrA[:, 1024:1280], psA[0][:, 0:256]),
                       waits=pa0.read((k.sem[SC], k.sem[SC].n + 1)) + B("scrAr").write())
            sc2 = k.op(SC, lambda e: e.copy(scrA[:, 1280:1536], psA[1][:, 0:256]),
                       waits=pa1.read((k.sem[SC], k.sem[SC].n + 1)))
            B("scrAr").write(sc2)
            v1_ = k.op(VE, lambda e: e.tensor_tensor(scrA[:, 0:256], psA[3][:, 0:256],
                                                     scrA[:, 1280:1536], ALU.mult),
                       waits=pmo.read((k.sem[VE], k.sem[VE].n + 1)) + [sc1, sc2])
            v2_ = k.op(VE, lambda e: e.tensor_tensor(scrA[:, 256:512], psA[2][:, 0:256],
                                                     scrA[:, 1024:1280], ALU.mult),
                       waits=pc.read((k.sem[VE], k.sem[VE].n + 1)))
            v3_ = k.op(VE, lambda e: e.tensor_sub(scrA[:, 0:256], scrA[:, 0:256],
                                                  scrA[:, 256:512]),
                       waits=[v1_, v2_])
            v4_ = k.op(VE, lambda e, h=h, qof=qof: e.scalar_tensor_tensor(
                ab_ctx[:, h * TOK + qof: h * TOK + qof + 256],
                scrA[:, 0:256], g_b[:, h:h + 1], scrA[:, 256:512],
                ALU.mult, ALU.add),
                waits=[v3_] + cb.write() + B("g_b").read())
            B("scrAr").write(v4_)
            cb.w.append(v4_)

        prologue(0)
        scores_b(0)
        for i in range(NH * 4):
            h, qq = divmod(i, 4)
            if qq == 0 and h + 1 < NH:
                prologue(h + 1)
            if i + 1 < NH * 4:
                scores_b(i + 1)
            tail_b(i)

        w3b_ = B("ab_w3")
        head_gate = [(k.sem[TE], k.sem[TE].n), (k.sem[VE], k.sem[VE].n),
                     (k.sem[SC], k.sem[SC].n)]
        for ocb in range(4):
            ws_ = w3b_.write() + (head_gate if ocb == 0 else [])
            bq3 = {}
            for qq_ in range(4):
                lv = k.dma(SY, ab_w3[:, qq_ * 2048:(qq_ + 1) * 2048],
                           wobT[ocb][:, qq_ * 2048:(qq_ + 1) * 2048],
                           waits=ws_ if qq_ == 0 else None)
                bq3[qq_] = lv
            w3b_.write(lv)
            w3b_.w = [bq3[3]]
            for tt in range(NTT):
                pa = B("psA%d" % (tt % 4))
                for dc in range(16):
                    tv = k.op(TE, lambda e, tt=tt, dc=dc: e.matmul(
                        psA[tt % 4][:],
                        ab_ctx[:, dc * TOK + tt * 128: dc * TOK + tt * 128 + 128],
                        ab_w3[:, dc * 512:(dc + 1) * 512],
                        start=(dc == 0), stop=(dc == 15)),
                        waits=([bq3[dc // 4]] + (w3b_.read() + cb.read() + pa.write()
                               if dc == 0 else [])), inc=(dc == 15))
                w3b_.r.append(tv); cb.r.append(tv)
                pa.write(tv)
                xb = B("x%d" % tt)
                vv = k.op(VE, lambda e, tt=tt, ocb=ocb: e.tensor_add(
                    x_sb[:, tt * D + ocb * 512: tt * D + ocb * 512 + 512],
                    x_sb[:, tt * D + ocb * 512: tt * D + ocb * 512 + 512],
                    psA[tt % 4][:]),
                    waits=pa.read((k.sem[VE], k.sem[VE].n + 1)) + xb.write())
                xb.write(vv)

    if dbg:
        dbg_k = nc.dram_tensor("dbg_k", [128, 4 * SEQ], BF16, kind="ExternalOutput").ap()
        dbg_v = nc.dram_tensor("dbg_v", [128, 16 * 512], BF16, kind="ExternalOutput").ap()
        dbg_q = nc.dram_tensor("dbg_q", [128, 16 * TOK], BF16, kind="ExternalOutput").ap()
        dbg_c = nc.dram_tensor("dbg_c", [128, 16 * TOK], BF16, kind="ExternalOutput").ap()
        dbg_g = nc.dram_tensor("dbg_g", [4 * D, TOK], BF16, kind="ExternalOutput").ap()

    emit_attn_a2()
    if dbg and n_sub == 1:
        k.barrier()
        k.dma(SY, dbg_k, aa_k[:])
        k.dma(SY, dbg_v, aa_v[:])
        k.dma(SY, dbg_q, aa_q[:])
        k.dma(SY, dbg_c, aa_ctx[:])
    k.barrier()
    if n_sub >= 2:
        emit_ffn()
        k.barrier()
    if n_sub >= 3:
        emit_attn_b()
        if dbg and n_sub == 3:
            k.barrier()
            k.dma(SY, dbg_q, ab_q[:])
            k.dma(SY, dbg_c, ab_ctx[:])
            for i in range(8):
                k.dma(SY, dbg_g[i * TOK:(i + 1) * TOK, :],
                      kv_g[i * TOK:(i + 1) * TOK, :])
        k.barrier()
    if n_sub >= 4:
        emit_ffn()
        k.barrier()

    last = None
    for t in range(NTT):
        last = k.dma(SY, y[t * 128:(t + 1) * 128, :], x_sb[:, t * D:(t + 1) * D],
                     waits=B("x%d" % t).write())
    k.raw("gpsimd", lambda e, v=last[1]: e.wait_ge(last[0].h, v))

    k.replay()
    return nc


def _tables():
    ident = np.eye(128, dtype=np.float32)
    inv_freq = (1.0 / (10000.0 ** (np.arange(0, D, 2, dtype=np.float32) / np.float32(D)))
                ).astype(np.float32)
    return ident, inv_freq


def _prep_src_for_core(inputs, c):
    w1 = np.asarray(inputs["L0_ffn_w1"]); w2 = np.asarray(inputs["L0_ffn_w2"])
    w3 = np.asarray(inputs["L0_ffn_w3"])
    att = [np.asarray(inputs["L0_attn_b_Wq"]), np.asarray(inputs["L0_attn_b_Wk"]),
           np.asarray(inputs["L0_attn_b_Wv"]), np.asarray(inputs["L0_attn_b_Wo"])]
    pans = []
    for b in range(8 * c, 8 * c + 8):
        if b < 16:
            for rb in range(4 * b, 4 * b + 4):
                pans.append(w1[rb * 128:(rb + 1) * 128, :])
        elif b < 32:
            ds = b - 16
            for rb in range(4 * ds, 4 * ds + 4):
                pans.append(w2[rb * 128:(rb + 1) * 128, :])
        elif b < 48:
            g, dms = divmod(b - 32, 4)
            for rb in range(4 * dms, 4 * dms + 4):
                pans.append(w3[rb * 128:(rb + 1) * 128, g * D:(g + 1) * D])
        else:
            wi, ocb = divmod(b - 48, 4)
            for rb in range(4 * ocb, 4 * ocb + 4):
                pans.append(att[wi][rb * 128:(rb + 1) * 128, :])
    return np.ascontiguousarray(np.stack(pans, 0))


def kernel(**inputs):
    x = np.asarray(inputs["L0_attn_a_x"])
    ident, inv_freq = _tables()
    nc = build_program(4)
    in_maps = []
    for c in range(8):
        b, h = c // 2, c % 2
        off = h * TOK
        pos = np.arange(off, off + TOK, dtype=np.float32)
        freqs = pos[:, None] * inv_freq[None, :]
        emb = np.concatenate([freqs, freqs], axis=1)
        cos_t = np.ascontiguousarray(np.cos(emb).T.astype(np.float32))
        sin_t = np.ascontiguousarray(np.sin(emb).T.astype(np.float32))
        kglob = np.concatenate([np.arange(off, off + TOK),
                                np.arange(TOK - off, 2 * TOK - off)])
        qglob = (np.arange(TOK) + off)
        maskT = (kglob[:, None] <= qglob[None, :]).astype(ml_dtypes.float8_e4m3fn)
        m = {
            "x_own": np.ascontiguousarray(x[b, off:off + TOK]),
            "x_oth": np.ascontiguousarray(x[b, TOK - off:2 * TOK - off]),
            "wq_a": np.asarray(inputs["L0_attn_a_Wq"]),
            "wk_a": np.asarray(inputs["L0_attn_a_Wk"]),
            "wv_a": np.asarray(inputs["L0_attn_a_Wv"]),
            "wo_a": np.asarray(inputs["L0_attn_a_Wo"]),
            "bo_a": np.asarray(inputs["L0_attn_a_bo"]).reshape(1, D),
            "wq_b": np.asarray(inputs["L0_attn_b_Wq"]),
            "wk_b": np.asarray(inputs["L0_attn_b_Wk"]),
            "wv_b": np.asarray(inputs["L0_attn_b_Wv"]),
            "wo_b": np.asarray(inputs["L0_attn_b_Wo"]),
            "gate": np.asarray(inputs["L0_attn_b_gate"]).reshape(1, NH),
            "w1": np.asarray(inputs["L0_ffn_w1"]),
            "w2": np.asarray(inputs["L0_ffn_w2"]),
            "w3": np.asarray(inputs["L0_ffn_w3"]),
            "ident": ident,
            "maskT": maskT,
            "cos_t": cos_t,
            "sin_t": sin_t,
            "prep_src": _prep_src_for_core(inputs, c),
            "sel": np.full((128, 1), 1.0 if c < 6 else 0.0, np.float32),
        }
        in_maps.append(m)
    res = run_bass_kernel_spmd(nc, in_maps, core_ids=list(range(8)))
    out = np.empty((4, SEQ, D), dtype=np.float32)
    for c in range(8):
        b, h = c // 2, c % 2
        out[b, h * TOK:(h + 1) * TOK] = res.results[c]["y"]
    return out


# version 12
# speedup vs baseline: 1.0026x; 1.0026x over previous
import math
import numpy as np
import ml_dtypes
import concourse.bass as bass
import concourse.mybir as mybir
from concourse.bass_utils import run_bass_kernel_spmd

F32 = mybir.dt.float32
BF16 = mybir.dt.bfloat16
FP8 = mybir.dt.float8e4
I32 = mybir.dt.int32
AF = mybir.ActivationFunctionType
ALU = mybir.AluOpType
AX = mybir.AxisListType

D = 2048
DFF = 8192
NH = 16
HD = 128
TOK = 1024
NTT = 8
SEQ = 2048
EXP_C = 32.0
ISQ = 1.0 / math.sqrt(128.0)

SB_LO = 16512
SB_HI = 229344


def _merge(waits):
    d = {}
    for s, v in waits:
        if v > 0 and (s not in d or v > d[s]):
            d[s] = v
    return [(s, v) for s, v in d.items()]


class Sem:
    def __init__(self, nc, name):
        self.h = nc.alloc_semaphore(name)
        self.n = 0


class Buf:

    def __init__(self):
        self.w = []
        self.r = []

    def read(self, my_sv=None):
        if my_sv is not None:
            self.r.append(my_sv)
        return list(self.w)

    def write(self, my_sv=None):
        w = self.r + self.w
        if my_sv is not None:
            self.w = [my_sv]
            self.r = []
        return w


class K:

    ENGS = ("sync", "gpsimd", "tensor", "vector", "scalar")

    def __init__(self, nc):
        self.nc = nc
        self.items = {e: [] for e in self.ENGS}
        self.sem = {e: Sem(nc, "s_" + e) for e in self.ENGS}
        self.sem_cc = Sem(nc, "s_cc")

    def _wrap(self, eng_name, fn, waits, sem, inc):
        if inc:
            sem.n += 1 if eng_name not in ("sync", "gpsimd") else 16
        val = sem.n
        waits = _merge(waits or [])

        def f(eng, ctx):
            for (s, v) in waits:
                if ctx.get(s.h.num, -1) < v:
                    eng.wait_ge(s.h, v)
                    ctx[s.h.num] = v
            ins = fn(eng)
            if inc:
                k = 16 if eng_name in ("sync", "gpsimd") else 1
                ins.then_inc(sem.h, k)
        self.items[eng_name].append(f)
        return (sem, val)

    def dma(self, q, out_ap, in_ap, waits=None):
        return self._wrap(q, lambda e: e.dma_start(out=out_ap, in_=in_ap),
                          waits, self.sem[q], True)

    def op(self, eng, fn, waits=None, inc=True):
        return self._wrap(eng, fn, waits, self.sem[eng], inc)

    def raw(self, eng, fn):
        self.items[eng].append(lambda e, ctx: fn(e))

    def barrier(self):
        snap = [(s, s.n) for s in self.sem.values()] + [(self.sem_cc, self.sem_cc.n)]
        for e in self.ENGS:
            mine = self.sem[e]
            waits = _merge([(s, v) for (s, v) in snap if s is not mine])

            def f(eng, ctx, waits=waits):
                for (s, v) in waits:
                    if ctx.get(s.h.num, -1) < v:
                        eng.wait_ge(s.h, v)
                        ctx[s.h.num] = v
            self.items[e].append(f)

    def replay(self):
        nc = self.nc
        with nc.Block() as blk:
            for e in self.ENGS:
                def make(e):
                    def body(eng):
                        ctx = {}
                        for f in self.items[e]:
                            f(eng, ctx)
                    return body
                getattr(blk, e)(make(e))


def build_program(n_sub=4, dbg=False):
    nc = bass.Bass("TRN2", target_bir_lowering=False, num_devices=8)
    k = K(nc)

    din = {}
    def dram_in(name, shape, dtype=F32):
        din[name] = nc.dram_tensor(name, shape, dtype, kind="ExternalInput").ap()
        return din[name]

    x_own = dram_in("x_own", [TOK, D])
    x_oth = dram_in("x_oth", [TOK, D])
    wq_a = dram_in("wq_a", [D, D])
    wk_a = dram_in("wk_a", [D, 512])
    wv_a = dram_in("wv_a", [D, 512])
    wo_a = dram_in("wo_a", [D, D])
    bo_a = dram_in("bo_a", [1, D])
    wq_b = dram_in("wq_b", [D, D])
    wk_b = dram_in("wk_b", [D, D])
    wv_b = dram_in("wv_b", [D, D])
    wo_b = dram_in("wo_b", [D, D])
    gate = dram_in("gate", [1, NH])
    w1 = dram_in("w1", [DFF, D])
    w2 = dram_in("w2", [DFF, D])
    w3 = dram_in("w3", [D, DFF])
    ident_in = dram_in("ident", [128, 128])
    prep_src = dram_in("prep_src", [32, 128, D])
    sel_in = dram_in("sel", [128, 1])
    maskT_in = dram_in("maskT", [SEQ, TOK], FP8)
    cos_in = dram_in("cos_t", [D, TOK])
    sin_in = dram_in("sin_t", [D, TOK])
    y = nc.dram_tensor("y", [TOK, D], F32, kind="ExternalOutput").ap()

    PRANK = 8200
    prep_in = nc.dram_tensor("prep_in", [PRANK, 1024], BF16, kind="Internal").ap()
    prep_all = nc.dram_tensor("prep_all", [8 * PRANK, 1024], BF16, kind="Internal").ap()

    def blk_view(b, hh):
        r, lb = b // 8, b % 8
        r0 = (2 * lb + hh) * 4096 + r * 512
        return prep_all[r0:r0 + 512, :].rearrange("(p a) j -> p (a j)", p=64)

    def blk2(b):
        return [blk_view(b, 0), blk_view(b, 1)]

    w1T = [blk2(ds) for ds in range(16)]
    w2T = [blk2(16 + ds) for ds in range(16)]
    w3T = [[blk2(32 + g * 4 + dms) for g in range(4)] for dms in range(4)]
    wqbT = [blk2(48 + i) for i in range(4)]
    wkbT = [blk2(52 + i) for i in range(4)]
    wvbT = [blk2(56 + i) for i in range(4)]
    wobT = [blk2(60 + i) for i in range(4)]
    kv_pack = nc.dram_tensor("kv_pack", [2 * D, TOK], BF16, kind="Internal").ap()
    kv_g = nc.dram_tensor("kv_g", [4 * D, TOK], BF16, kind="Internal").ap()

    used = []
    def S(name, shape, dtype, off):
        nb = int(np.prod(shape[1:])) * mybir.dt.size(dtype)
        assert off % 32 == 0 and off >= SB_LO and off + nb <= SB_HI, (name, off, nb)
        used.append((name, off, off + nb))
        return nc.alloc_sbuf_tensor_at(name, shape, dtype, offset=off)

    o = SB_LO
    def P(name, shape, dtype):
        nonlocal o
        t = S(name, shape, dtype, o)
        o += (int(np.prod(shape[1:])) * mybir.dt.size(dtype) + 31) & ~31
        return t

    x_sb = P("x_sb", [128, NTT * D], F32)
    ident = P("identb", [128, 128], BF16)
    ones128 = P("ones128", [128, 1], BF16)
    ones128f = P("ones128f", [128, 1], F32)
    ones1f = P("ones1f", [1, 128], F32)
    bo_b = P("bo_b", [128, D], BF16)
    g_b = P("g_b", [128, NH], F32)
    wsc = P("wsc", [128, 4], F32)
    scr = P("scr", [128, 16], F32)
    cst = P("cst", [128, 4], F32)
    scr2 = P("scr2", [128, 64], F32)
    sumacc = P("sumacc", [128, 4], F32)
    ntb = P("ntb", [128, 2], BF16)
    rowscr = P("rowscr", [1, TOK], F32)
    scrA = P("scrA", [128, D], F32)
    scrB = P("scrB", [128, D], BF16)
    scrB2 = P("scrB2", [128, D], BF16)
    A = (o + 31) & ~31
    ARENA_SZ = SB_HI - A
    assert ARENA_SZ >= 116 * 1024, (A, ARENA_SZ)

    pr_pan2 = S("pr_pan2", [128, D], F32, A)
    pr_sgn2 = S("pr_sgn2", [128, D], BF16, A + 0x2000)
    pr_asm = S("pr_asm", [128, 16 * 512], BF16, A + 0x3000)
    ln_f = S("ln_f", [128, 16 * TOK], BF16, A)
    aa_q = S("aa_q", [128, 16 * TOK], BF16, A + 0x8000)
    aa_k = S("aa_k", [128, 4 * SEQ], BF16, A + 0x10000)
    aa_v = S("aa_v", [128, 16 * 512], BF16, A + 0x14000)
    aa_lnfo = S("aa_lnfo", [128, 16 * 128], BF16, A + 0x18000)
    aa_w = S("aa_w", [128, 16 * 512], BF16, A + 0x19000)
    aa_ctx = S("aa_ctx", [128, 16 * TOK], BF16, A)
    aa_mask = S("aa_mask", [128, 16 * 256], FP8, A + 0x18000)
    aa_exp = [S("aa_exp%d" % i, [128, 16 * 256], BF16, A + 0x19000 + i * 0x2000)
              for i in range(2)]
    aa_w2 = S("aa_w2", [128, 16 * 512], BF16, A + 0x8000)
    ff_qa = S("ff_qa", [128, 16 * TOK], BF16, A)
    ff_qh = S("ff_qh", [128, 16 * TOK], BF16, A + 0x8000)
    ff_w1 = S("ff_w1", [128, 16 * 512], BF16, A + 0x10000)
    ff_w2 = S("ff_w2", [128, 16 * 512], BF16, A + 0x14000)
    ff_w3 = [S("ff_w3_%d" % i, [128, 8 * 512], BF16, A + 0x18000 + i * 0x2000)
             for i in range(2)]
    ff_ln = S("ff_ln", [128, D], F32, A + 0x8000)
    ff_qi = S("ff_qi", [128, D], I32, A + 0xA000)
    ff_h = scrA[:, 0:512]
    ff_sl = scrA[:, 512:1024]
    ff_qih = scrA[:, 1024:1536].bitcast(I32)
    ff_qt = scrB2[:, 0:512]
    ff_qt2 = scrB2[:, 512:1024]
    ab_ln = S("ab_ln", [128, 16 * TOK], BF16, A)
    ab_q = S("ab_q", [128, 16 * TOK], BF16, A + 0x8000)
    ab_qr = S("ab_qr", [128, 8 * TOK], BF16, A + 0x10000)
    ab_w = S("ab_w", [128, 16 * 512], BF16, A + 0x14000)
    ab_cos = S("ab_cos", [128, 2 * TOK], F32, A + 0x18000)
    ab_sin = S("ab_sin", [128, 2 * TOK], F32, A + 0x1A000)
    ab_ctx = S("ab_ctx", [128, 16 * TOK], BF16, A)
    ab_kf = S("ab_kf", [128, SEQ], BF16, A + 0x10000)
    ab_kft = S("ab_kft", [128, SEQ], BF16, A + 0x11000)
    ab_qf = [S("ab_qf%d" % i, [128, TOK], BF16, A + 0x12000 + i * 0x800)
             for i in range(2)]
    ab_mem = [S("ab_mem%d" % i, [128, 128], BF16, A + 0x1B000 + i * 0x100)
              for i in range(2)]
    ab_exp = [S("ab_exp%d" % i, [128, 16 * 256], BF16, A + 0x13000 + i * 0x2000)
              for i in range(2)]
    ab_kv = [S("ab_kv%d" % i, [128, 2 * SEQ], BF16, A + 0x17000 + i * 0x2000)
             for i in range(2)]
    ab_w3 = S("ab_w3", [128, 16 * 512], BF16, A + 0x10000)

    psA = [nc.alloc_psum_tensor("psA%d" % i, [128, 512], F32) for i in range(4)]
    psT = nc.alloc_psum_tensor("psT", [128, 1024], BF16)
    psE = nc.alloc_psum_tensor("psE", [128, 512], F32)
    psD = [nc.alloc_psum_tensor("psD%d" % i, [1, 512], F32) for i in range(2)]

    bufs = {}
    def B(name):
        if name not in bufs:
            bufs[name] = Buf()
        return bufs[name]

    SY, GP, TE, VE, SC = "sync", "gpsimd", "tensor", "vector", "scalar"

    for t in range(NTT):
        sv = k.dma(SY, x_sb[:, t * D:(t + 1) * D], x_own[t * 128:(t + 1) * 128, :])
        B("x%d" % t).write(sv)
    sv = k.dma(GP, ident[:], ident_in)
    B("ident").write(sv)
    sv = k.op(VE, lambda e: e.memset(ones128[:], 1.0))
    B("ones").write(sv)
    sv = k.op(VE, lambda e: e.memset(ones128f[:], 1.0))
    B("ones").write(sv)
    sv = k.op(VE, lambda e: e.memset(ones1f[:], 1.0))
    B("ones").write(sv)
    k.op(VE, lambda e: e.memset(cst[:, 0:1], 1e-5))
    k.op(VE, lambda e: e.memset(cst[:, 1:2], -EXP_C))
    sv = k.op(VE, lambda e: e.memset(cst[:, 2:3], ISQ))
    B("cst").write(sv)
    sv = k.dma(SY, rowscr[0:1, 0:NH], gate)
    B("rowscr").write(sv)
    sv = k.op(SC, lambda e: e.activation(rowscr[0:1, 0:NH], rowscr[0:1, 0:NH], AF.Sigmoid),
              waits=B("rowscr").write())
    B("rowscr").write(sv)
    sv = k.op(TE, lambda e: e.matmul(psE[:, 0:NH], ones1f[:], rowscr[0:1, 0:NH],
                                     start=True, stop=True),
              waits=B("rowscr").read((k.sem[TE], k.sem[TE].n + 1))
              + B("ones").read() + B("psE").write())
    B("psE").write(sv)
    sv = k.op(VE, lambda e: e.tensor_copy(g_b[:], psE[:, 0:NH]),
              waits=B("psE").read((k.sem[VE], k.sem[VE].n + 1)))
    B("g_b").write(sv)
    for hh in range(2):
        sv = k.dma(SY, rowscr[0:1, :], bo_a[0:1, hh * TOK:(hh + 1) * TOK],
                   waits=B("rowscr").write())
        B("rowscr").write(sv)
        for i in range(2):
            sv = k.op(TE, lambda e, i=i: e.matmul(psA[i][:], ones1f[:],
                                                  rowscr[0:1, i * 512:(i + 1) * 512],
                                                  start=True, stop=True),
                      waits=B("rowscr").read((k.sem[TE], k.sem[TE].n + 1))
                      + B("ones").read() + B("psA%d" % i).write())
            B("psA%d" % i).write(sv)
            sv = k.op(VE, lambda e, i=i, hh=hh: e.tensor_copy(
                bo_b[:, hh * TOK + i * 512: hh * TOK + i * 512 + 512], psA[i][:]),
                      waits=B("psA%d" % i).read((k.sem[VE], k.sem[VE].n + 1)))
            B("bo_b").write(sv)
    k.barrier()

    sv = k.op(VE, lambda e: e.memset(sumacc[:], 0.0))
    B("sumacc").write(sv)
    selb = B("sel")
    sv = k.dma(SY, scr[:, 12:13], sel_in)
    selb.write(sv)
    for lb in range(8):
        for pi in range(4):
            u = lb * 4 + pi
            pan = scrA if u % 2 == 0 else pr_pan2
            sgn = scrB if u % 2 == 0 else pr_sgn2
            pb = B("prep_pan%d" % (u % 2))
            sb_ = B("prep_sgn%d" % (u % 2))
            lv = k.dma(SY, pan[:], prep_src[u], waits=pb.write())
            pb.write(lv)
            s1 = k.op(SC, lambda e, pan=pan, sgn=sgn: e.sign(sgn[:], pan[:]),
                      waits=pb.read((k.sem[SC], k.sem[SC].n + 1)) + sb_.write())
            v1 = k.op(VE, lambda e, pan=pan, sgn=sgn: e.tensor_sub(
                sgn[:], sgn[:], pan[:]),
                waits=[s1] + pb.read((k.sem[VE], k.sem[VE].n + 1)))
            v2 = k.op(VE, lambda e, pan=pan, sgn=sgn: e.scalar_tensor_tensor(
                sgn[:], sgn[:], scr[:, 12:13], pan[:], ALU.mult, ALU.add),
                waits=selb.read((k.sem[VE], k.sem[VE].n + 1))
                + pb.read((k.sem[VE], k.sem[VE].n + 1)))
            sb_.write(v2)
            rv = k.op(VE, lambda e, pan=pan: e.tensor_reduce(
                scr2[:, 63:64], pan[:], AX.X, ALU.add, apply_absolute_value=True),
                waits=pb.read((k.sem[VE], k.sem[VE].n + 1)) + B("scr2r").write())
            B("scr2r").write(rv)
            av = k.op(VE, lambda e: e.tensor_tensor(
                sumacc[:, 0:1], sumacc[:, 0:1], scr2[:, 63:64], ALU.add),
                waits=B("scr2r").read((k.sem[VE], k.sem[VE].n + 1)) + B("sumacc").write())
            B("sumacc").write(av)
            for rnd in range(2):
                for j in range(8):
                    dc = rnd * 8 + j
                    tv = k.op(TE, lambda e, sgn=sgn, dc=dc, j=j: e.transpose(
                        psT[:, j * 128:(j + 1) * 128], sgn[:, dc * 128:(dc + 1) * 128],
                        ident[:]),
                        waits=(sb_.read() + B("ident").read() + B("psT").write()
                               + B("asm").write())
                        if j == 0 else None, inc=(j == 7))
                if rnd == 1:
                    sb_.read((k.sem[TE], tv[1]))
                B("psT").write(tv)
                vv = k.op(VE, lambda e, pi=pi, rnd=rnd: e.tensor_copy(
                    pr_asm[:].rearrange("p (dc f j) -> p dc f j", dc=16, f=4)
                    [:, rnd * 8:(rnd + 1) * 8, pi, :],
                    psT[:].rearrange("p (dc j) -> p dc j", dc=8)),
                    waits=B("psT").read((k.sem[VE], k.sem[VE].n + 1)))
                B("asm").w.append(vv)
        wv = k.dma(SY, prep_in[lb * 1024:(lb + 1) * 1024, :]
                   .rearrange("(p a) j -> p (a j)", p=128),
                   pr_asm[:], waits=B("asm").write())
        B("asm").r.append(wv)
        B("asm").w = []
    tv = k.op(TE, lambda e: e.matmul(psD[0][0:1, 0:1], ones128f[:], sumacc[:, 0:1],
                                     start=True, stop=True),
              waits=B("sumacc").read() + B("ones").read() + B("psD0").write())
    B("psD0").write(tv)
    vv = k.op(VE, lambda e: e.tensor_copy(rowscr[0:1, 0:1], psD[0][0:1, 0:1]),
              waits=B("psD0").read((k.sem[VE], k.sem[VE].n + 1)) + B("rowscr").write())
    B("rowscr").write(vv)
    tw = k.dma(SY, prep_in[8192:8193, 0:2], rowscr[0:1, 0:1].bitcast(BF16),
               waits=B("rowscr").read((k.sem[SY], k.sem[SY].n + 16)))
    k.barrier()
    for c in range(17):
        k.sem_cc.n += 1
        if c < 16:
            iap = prep_in[c * 512:(c + 1) * 512, :]
            oap = prep_all[c * 4096:(c + 1) * 4096, :]
        else:
            iap = prep_in[8192:8200, :]
            oap = prep_all[65536:65600, :]

        def _ccp(eng, ctx, iap=iap, oap=oap):
            eng.collective_compute(
                "AllGather", ALU.bypass,
                replica_groups=[[0, 1, 2, 3, 4, 5, 6, 7]],
                ins=[iap], outs=[oap]).then_inc(k.sem_cc.h, 1)
        k.items["gpsimd"].append(_ccp)
    cc0_done = (k.sem_cc, k.sem_cc.n)
    for r in range(6):
        lv = k.dma(SY, scrB[0:1, 16 + 2 * r:16 + 2 * r + 2],
                   prep_all[65536 + r * 8:65536 + r * 8 + 1, 0:2],
                   waits=[cc0_done] if r == 0 else None)
    ld_tails = (k.sem[SY], k.sem[SY].n)
    f32v = scrB[0:1, 16:28].bitcast(F32)
    v1 = k.op(VE, lambda e: e.tensor_add(
        rowscr[0:1, 0:3], f32v[0:1, 0:6:2], f32v[0:1, 1:6:2]),
        waits=[ld_tails] + B("rowscr").write())
    v2 = k.op(VE, lambda e: e.tensor_scalar(
        rowscr[0:1, 0:3], rowscr[0:1, 0:3], 1.0 / (DFF * D), None, ALU.mult),
        waits=[v1])
    B("rowscr").write(v2)
    tv = k.op(TE, lambda e: e.matmul(psE[:, 0:3], ones1f[:], rowscr[0:1, 0:3],
                                     start=True, stop=True),
              waits=B("rowscr").read((k.sem[TE], k.sem[TE].n + 1)) + B("psE").write())
    B("psE").write(tv)
    vv = k.op(VE, lambda e: e.tensor_copy(wsc[:, 0:3], psE[:, 0:3]),
              waits=B("psE").read((k.sem[VE], k.sem[VE].n + 1)))
    B("wsc").write(vv)
    k.barrier()

    def emit_ln(x_ap, x_buf, outs):
        stats = B("lnstats")
        vv = k.op(VE, lambda e: e.tensor_reduce(scr[:, 0:1], x_ap, AX.X, ALU.add),
                  waits=x_buf.read((k.sem[VE], k.sem[VE].n + 1)) + stats.write())
        sv = k.op(SC, lambda e: e.activation(scrB2[:], x_ap, AF.Square,
                                             accum_out=scr[:, 1:2]),
                  waits=x_buf.read((k.sem[SC], k.sem[SC].n + 1)) + B("scrB2").write())
        B("scrB2").write(sv)
        v2 = k.op(VE, lambda e: e.tensor_scalar(scr[:, 2:3], scr[:, 0:1], 1.0 / D,
                                                None, ALU.mult),
                  waits=[vv])
        v3 = k.op(VE, lambda e: e.tensor_scalar(scr[:, 3:4], scr[:, 1:2], 1.0 / D,
                                                None, ALU.mult),
                  waits=[sv])
        v4a = k.op(VE, lambda e: e.tensor_mul(scr[:, 4:5], scr[:, 2:3], scr[:, 2:3]),
                   waits=[v2])
        v4 = k.op(VE, lambda e: e.tensor_sub(scr[:, 8:9], scr[:, 3:4], scr[:, 4:5]),
                  waits=[v3, v4a])
        s2 = k.op(SC, lambda e: e.activation(scr[:, 5:6], scr[:, 8:9], AF.Sqrt,
                                             bias=cst[:, 0:1]),
                  waits=[v4] + B("cst").read())
        v5 = k.op(VE, lambda e: e.reciprocal(scr[:, 6:7], scr[:, 5:6]), waits=[s2])
        v6 = k.op(VE, lambda e: e.scalar_tensor_tensor(
            scr[:, 7:8], scr[:, 2:3], -1.0, scr[:, 6:7], ALU.mult, ALU.mult), waits=[v5])
        last = []
        for (ap, obuf) in outs:
            ov = k.op(SC, lambda e, ap=ap: e.activation(ap, x_ap, AF.Identity,
                                                        bias=scr[:, 7:8], scale=scr[:, 6:7]),
                      waits=[v6] + x_buf.read((k.sem[SC], k.sem[SC].n + 1))
                      + (obuf.write() if obuf else []))
            if obuf:
                obuf.write(ov)
            stats.r = [ov]
            last.append(ov)
        return last[-1]

    def transpose16(src_fn, dst_view_fn, src_waits, dst_buf):
        for rnd in range(2):
            for j in range(8):
                dc = rnd * 8 + j
                tv = k.op(TE, lambda e, dc=dc, j=j: e.transpose(
                    psT[:, j * 128:(j + 1) * 128], src_fn(dc), ident[:]),
                    waits=(src_waits + B("ident").read() + B("psT").write()) if j == 0
                    else None, inc=(j == 7))
            B("psT").write(tv)
            vv = k.op(VE, lambda e, rnd=rnd: e.tensor_copy(
                dst_view_fn(rnd), psT[:].rearrange("p (dc j) -> p dc j", dc=8)),
                waits=B("psT").read((k.sem[VE], k.sem[VE].n + 1)) + dst_buf.write())
            dst_buf.w.append(vv)
        return vv


    def emit_attn_a2():
        lnf_b = B("ln_f")
        for t in range(NTT):
            emit_ln(x_sb[:, t * D:(t + 1) * D], B("x%d" % t), [(scrB[:], B("scrB"))])
            transpose16(lambda dc: scrB[:, dc * 128:(dc + 1) * 128],
                        lambda rnd, t=t: ln_f[:].rearrange("p (dc t) -> p dc t", dc=16)
                        [:, rnd * 8:(rnd + 1) * 8, t * 128:(t + 1) * 128],
                        B("scrB").read(), lnf_b)
            B("scrB").r.append((k.sem[VE], k.sem[VE].n))
        kbuf, vbuf, wbuf = B("aa_k"), B("aa_v"), B("aa_w")
        lnfo_b = B("aa_lnfo")
        xo_b = B("xo")

        wld = {}

        def load_wcol(dram_w, ocb):
            ws = wbuf.write()
            for dc in range(16):
                lv = k.dma(GP, aa_w[:, dc * 512:(dc + 1) * 512],
                           dram_w[dc * 128:(dc + 1) * 128, ocb * 512:(ocb + 1) * 512],
                           waits=ws if dc == 0 else None)
                wld[dc] = lv
            wbuf.write(lv)
            wbuf.w = [wld[15]]

        load_wcol(wk_a, 0)
        for oc in range(4):
            for ts in range(2):
                pa = B("psA%d" % (2 * (oc % 2) + ts))
                for dc in range(16):
                    tv = k.op(TE, lambda e, oc=oc, ts=ts, dc=dc: e.matmul(
                        psA[2 * (oc % 2) + ts][:],
                        aa_w[:, dc * 512 + oc * 128: dc * 512 + oc * 128 + 128],
                        ln_f[:, dc * TOK + ts * 512: dc * TOK + ts * 512 + 512],
                        start=(dc == 0), stop=(dc == 15)),
                        waits=([wld[dc]] + (wbuf.read() + lnf_b.read() + pa.write()
                               if dc == 0 else [])), inc=(dc == 15))
                wbuf.r.append(tv); lnf_b.r.append(tv)
                pa.write(tv)
                vv = k.op(VE, lambda e, oc=oc, ts=ts: e.tensor_copy(
                    aa_k[:, oc * SEQ + ts * 512: oc * SEQ + ts * 512 + 512],
                    psA[2 * (oc % 2) + ts][:]),
                    waits=pa.read((k.sem[VE], k.sem[VE].n + 1)) + kbuf.write())
                kbuf.w.append(vv)
        for t in range(NTT):
            lv = k.dma(SY, scrA[:], x_oth[t * 128:(t + 1) * 128, :], waits=xo_b.write())
            xo_b.write(lv)
            emit_ln(scrA[:], xo_b, [(scrB[:], B("scrB"))])
            transpose16(lambda dc: scrB[:, dc * 128:(dc + 1) * 128],
                        lambda rnd: aa_lnfo[:].rearrange("p (dc t) -> p dc t", dc=16)
                        [:, rnd * 8:(rnd + 1) * 8, :],
                        B("scrB").read(), lnfo_b)
            B("scrB").r.append((k.sem[VE], k.sem[VE].n))
            for oc in range(4):
                pa = B("psA%d" % (oc % 2))
                for dc in range(16):
                    tv = k.op(TE, lambda e, oc=oc, dc=dc: e.matmul(
                        psA[oc % 2][:, 0:128],
                        aa_w[:, dc * 512 + oc * 128: dc * 512 + oc * 128 + 128],
                        aa_lnfo[:, dc * 128:(dc + 1) * 128],
                        start=(dc == 0), stop=(dc == 15)),
                        waits=(lnfo_b.read() + pa.write()) if dc == 0 else None,
                        inc=(dc == 15))
                lnfo_b.r.append(tv)
                wbuf.r.append(tv)
                pa.write(tv)
                vv = k.op(VE, lambda e, oc=oc, t=t: e.tensor_copy(
                    aa_k[:, oc * SEQ + TOK + t * 128: oc * SEQ + TOK + t * 128 + 128],
                    psA[oc % 2][:, 0:128]),
                    waits=pa.read((k.sem[VE], k.sem[VE].n + 1)) + kbuf.write())
                kbuf.w.append(vv)
        load_wcol(wv_a, 0)
        for tt in range(NTT):
            pa = B("psA%d" % (tt % 4))
            for dc in range(16):
                tv = k.op(TE, lambda e, tt=tt, dc=dc: e.matmul(
                    psA[tt % 4][:],
                    ln_f[:, dc * TOK + tt * 128: dc * TOK + tt * 128 + 128],
                    aa_w[:, dc * 512:(dc + 1) * 512],
                    start=(dc == 0), stop=(dc == 15)),
                    waits=(wbuf.read() + lnf_b.read() + pa.write()) if dc == 0 else None,
                    inc=(dc == 15))
            wbuf.r.append(tv); lnf_b.r.append(tv)
            pa.write(tv)
            vv = k.op(VE, lambda e, tt=tt: e.tensor_copy(
                aa_v[:, tt * 512:(tt + 1) * 512], psA[tt % 4][:]),
                waits=pa.read((k.sem[VE], k.sem[VE].n + 1)) + vbuf.write())
            vbuf.w.append(vv)
        for t in range(NTT):
            lv = k.dma(SY, scrA[:], x_oth[t * 128:(t + 1) * 128, :], waits=xo_b.write())
            xo_b.write(lv)
            emit_ln(scrA[:], xo_b, [(scrB[:], B("scrB"))])
            transpose16(lambda dc: scrB[:, dc * 128:(dc + 1) * 128],
                        lambda rnd: aa_lnfo[:].rearrange("p (dc t) -> p dc t", dc=16)
                        [:, rnd * 8:(rnd + 1) * 8, :],
                        B("scrB").read(), lnfo_b)
            B("scrB").r.append((k.sem[VE], k.sem[VE].n))
            pa = B("psA%d" % (t % 2))
            for dc in range(16):
                tv = k.op(TE, lambda e, t=t, dc=dc: e.matmul(
                    psA[t % 2][:],
                    aa_lnfo[:, dc * 128:(dc + 1) * 128],
                    aa_w[:, dc * 512:(dc + 1) * 512],
                    start=(dc == 0), stop=(dc == 15)),
                    waits=(lnfo_b.read() + pa.write()) if dc == 0 else None,
                    inc=(dc == 15))
            lnfo_b.r.append(tv)
            wbuf.r.append(tv)
            pa.write(tv)
            vv = k.op(VE, lambda e, t=t: e.tensor_copy(
                aa_v[:, (NTT + t) * 512:(NTT + t + 1) * 512], psA[t % 2][:]),
                waits=pa.read((k.sem[VE], k.sem[VE].n + 1)) + vbuf.write())
            vbuf.w.append(vv)
        qb = B("aa_q")
        for ocb in range(4):
            load_wcol(wq_a, ocb)
            for oc4 in range(4):
                for ts in range(2):
                    pa = B("psA%d" % (2 * (oc4 % 2) + ts))
                    for dc in range(16):
                        tv = k.op(TE, lambda e, oc4=oc4, ts=ts, dc=dc: e.matmul(
                            psA[2 * (oc4 % 2) + ts][:],
                            aa_w[:, dc * 512 + oc4 * 128: dc * 512 + oc4 * 128 + 128],
                            ln_f[:, dc * TOK + ts * 512: dc * TOK + ts * 512 + 512],
                            start=(dc == 0), stop=(dc == 15)),
                            waits=([wld[dc]] + (wbuf.read() + lnf_b.read()
                                   + pa.write() if dc == 0 else [])), inc=(dc == 15))
                    wbuf.r.append(tv); lnf_b.r.append(tv)
                    pa.write(tv)
                    oc = ocb * 4 + oc4
                    vv = k.op(VE, lambda e, oc=oc, ts=ts, oc4=oc4: e.tensor_copy(
                        aa_q[:, oc * TOK + ts * 512: oc * TOK + ts * 512 + 512],
                        psA[2 * (oc4 % 2) + ts][:]),
                        waits=pa.read((k.sem[VE], k.sem[VE].n + 1)) + qb.write())
                    qb.w.append(vv)

        eb = [B("aa_exp0"), B("aa_exp1")]
        cb = B("aa_ctx")
        proj_gate = [(k.sem[TE], k.sem[TE].n), (k.sem[VE], k.sem[VE].n)]
        mb = B("aa_mask")
        cur_mask_qq = [-1]

        def load_mask(qq):
            if cur_mask_qq[0] != qq:
                lv = k.dma(SY, aa_mask[:].rearrange("p (c t) -> p c t", c=16),
                           maskT_in.rearrange("(c p) t -> p c t", p=128)
                           [:, :, qq * 256:(qq + 1) * 256],
                           waits=mb.write() + (proj_gate if qq == 0 else []))
                mb.write(lv)
                cur_mask_qq[0] = qq

        def emit_scores(i):
            qq, h = divmod(i, NH)
            g = h // 4
            p = i % 2
            load_mask(qq)
            q_ap = aa_q[:, h * TOK + qq * 256: h * TOK + qq * 256 + 256]
            exp_t = aa_exp[0] if p == 0 else aa_exp[1]
            for kt in range(16):
                pa = B("psA%d" % (kt % 2))
                tv = k.op(TE, lambda e, g=g, kt=kt, q_ap=q_ap: e.matmul(
                    psA[kt % 2][:, 0:256],
                    aa_k[:, g * SEQ + kt * 128: g * SEQ + kt * 128 + 128],
                    q_ap, start=True, stop=True),
                    waits=pa.write() + (kbuf.read() + qb.read() if i == 0 and kt == 0
                                        else []))
                pa.write(tv)
                sv = k.op(SC, lambda e, kt=kt, exp_t=exp_t: e.activation(
                    exp_t[:, kt * 256:(kt + 1) * 256], psA[kt % 2][:, 0:256],
                    AF.Exp, bias=cst[:, 1:2], scale=cst[:, 2:3]),
                    waits=pa.read((k.sem[SC], k.sem[SC].n + 1)) + eb[p].write()
                    + B("cst").read())
                eb[p].w.append(sv)
            if i == 0:
                kbuf.r.append(tv); qb.r.append(tv)
            mv = k.op(VE, lambda e, exp_t=exp_t: e.tensor_tensor(
                exp_t[:], exp_t[:], aa_mask[:], ALU.mult),
                waits=eb[p].write() + mb.read((k.sem[VE], k.sem[VE].n + 1)))
            eb[p].write(mv)

        def emit_tail(i):
            qq, h = divmod(i, NH)
            g = h // 4
            p = i % 2
            exp_t = aa_exp[0] if p == 0 else aa_exp[1]
            pd = B("psD0")
            for kt in range(16):
                tv = k.op(TE, lambda e, kt=kt, exp_t=exp_t: e.matmul(
                    psD[0][:, 0:256], ones128[:], exp_t[:, kt * 256:(kt + 1) * 256],
                    start=(kt == 0), stop=(kt == 15)),
                    waits=(eb[p].read() + pd.write() + B("ones").read())
                    if kt == 0 else None, inc=(kt == 15))
            eb[p].r.append(tv)
            pd.write(tv)
            vr = k.op(VE, lambda e: e.reciprocal(rowscr[0:1, 0:256], psD[0][0:1, 0:256]),
                      waits=pd.read((k.sem[VE], k.sem[VE].n + 1)) + B("rowscr").write())
            B("rowscr").write(vr)
            pc = B("psA2")
            for kt in range(16):
                tv = k.op(TE, lambda e, kt=kt, g=g, exp_t=exp_t: e.matmul(
                    psA[2][:, 0:256],
                    aa_v[:, kt * 512 + g * 128: kt * 512 + g * 128 + 128],
                    exp_t[:, kt * 256:(kt + 1) * 256],
                    start=(kt == 0), stop=(kt == 15)),
                    waits=(eb[p].read() + pc.write() + vbuf.read()) if kt == 0 else None,
                    inc=(kt == 15))
            eb[p].r.append(tv); vbuf.r.append(tv)
            pc.write(tv)
            pe = B("psE")
            tb = k.op(TE, lambda e: e.matmul(psE[:, 0:256], ones1f[:],
                                             rowscr[0:1, 0:256], start=True, stop=True),
                      waits=B("rowscr").read((k.sem[TE], k.sem[TE].n + 1)) + pe.write())
            pe.write(tb)
            sre = k.op(SC, lambda e: e.copy(scrA[:, 0:256], psE[:, 0:256]),
                       waits=pe.read((k.sem[SC], k.sem[SC].n + 1)) + B("scrAh").write())
            B("scrAh").write(sre)
            vv = k.op(VE, lambda e, h=h, qq=qq: e.tensor_tensor(
                aa_ctx[:, h * TOK + qq * 256: h * TOK + qq * 256 + 256],
                psA[2][:, 0:256], scrA[:, 0:256], ALU.mult),
                waits=pc.read((k.sem[VE], k.sem[VE].n + 1))
                + B("scrAh").read((k.sem[VE], k.sem[VE].n + 1)) + cb.write())
            cb.w.append(vv)

        emit_scores(0)
        for i in range(4 * NH):
            if i + 1 < 4 * NH:
                emit_scores(i + 1)
            emit_tail(i)

        for t in range(NTT):
            xb = B("x%d" % t)
            vv = k.op(VE, lambda e, t=t: e.tensor_add(
                x_sb[:, t * D:(t + 1) * D], x_sb[:, t * D:(t + 1) * D], bo_b[:]),
                waits=xb.write() + B("bo_b").read())
            xb.write(vv)
        w2b = B("aa_w2")
        head_gate = [(k.sem[TE], k.sem[TE].n), (k.sem[SC], k.sem[SC].n)]
        wld2 = {}
        for ocb in range(4):
            ws = w2b.write() + (head_gate if ocb == 0 else [])
            for dc in range(16):
                lv = k.dma(GP, aa_w2[:, dc * 512:(dc + 1) * 512],
                           wo_a[dc * 128:(dc + 1) * 128, ocb * 512:(ocb + 1) * 512],
                           waits=ws if dc == 0 else None)
                wld2[dc] = lv
            w2b.write(lv)
            w2b.w = [wld2[15]]
            for tt in range(NTT):
                pa = B("psA%d" % (tt % 4))
                for dc in range(16):
                    tv = k.op(TE, lambda e, tt=tt, dc=dc: e.matmul(
                        psA[tt % 4][:],
                        aa_ctx[:, dc * TOK + tt * 128: dc * TOK + tt * 128 + 128],
                        aa_w2[:, dc * 512:(dc + 1) * 512],
                        start=(dc == 0), stop=(dc == 15)),
                        waits=([wld2[dc]] + (w2b.read() + cb.read() + pa.write()
                               if dc == 0 else [])), inc=(dc == 15))
                w2b.r.append(tv); cb.r.append(tv)
                pa.write(tv)
                xb = B("x%d" % tt)
                vv = k.op(VE, lambda e, tt=tt, ocb=ocb: e.tensor_add(
                    x_sb[:, tt * D + ocb * 512: tt * D + ocb * 512 + 512],
                    x_sb[:, tt * D + ocb * 512: tt * D + ocb * 512 + 512],
                    psA[tt % 4][:]),
                    waits=pa.read((k.sem[VE], k.sem[VE].n + 1)) + xb.write())
                xb.write(vv)

    def emit_ffn():
        qab, qhb = B("ff_qa"), B("ff_qh")
        w1b, w2b_ = B("ff_w1"), B("ff_w2")
        w3b = [B("ff_w30"), B("ff_w31")]
        pend_q = []

        def emit_qh_transpose(si, tl, qt, qtb):
            for i in range(4):
                tv = k.op(TE, lambda e, i=i, qt=qt: e.transpose(
                    psT[:, i * 128:(i + 1) * 128],
                    qt[:, i * 128:(i + 1) * 128], ident[:]),
                    waits=(qtb.read() + B("psT").write()) if i == 0 else None,
                    inc=(i == 3))
            qtb.r.append((k.sem[TE], tv[1]))
            B("psT").write(tv)
            vq = k.op(VE, lambda e, si=si, tl=tl: e.tensor_copy(
                ff_qh[:].rearrange("p (c t) -> p c t", c=16)
                [:, 4 * si:4 * si + 4, tl * 128:(tl + 1) * 128],
                psT[:, 0:512].rearrange("p (c t) -> p c t", c=4)),
                waits=B("psT").read((k.sem[VE], k.sem[VE].n + 1)) + qhb.write())
            qhb.w.append(vq)
        for t in range(NTT):
            emit_ln(x_sb[:, t * D:(t + 1) * D], B("x%d" % t),
                    [(ff_ln[:], B("ff_ln"))])
            flb = B("ff_ln")
            v1 = k.op(VE, lambda e: e.tensor_reduce(
                scr2[:, 0:32], ff_ln[:].rearrange("p (g c) -> p g c", g=32),
                AX.X, ALU.max, apply_absolute_value=True),
                waits=flb.read((k.sem[VE], k.sem[VE].n + 1)) + B("qs").write())
            v2 = k.op(VE, lambda e: e.tensor_scalar(
                scr2[:, 32:64], scr2[:, 0:32], 1e-5, 1.0 / 127.0, ALU.max, ALU.mult),
                waits=[v1])
            v3 = k.op(VE, lambda e: e.reciprocal(scr2[:, 0:32], scr2[:, 32:64]),
                      waits=[v2])
            B("qs").write(v3)
            qib = B("ff_qi")
            v4 = k.op(VE, lambda e: e.tensor_tensor(
                ff_qi[:].rearrange("p (g c) -> p g c", g=32),
                ff_ln[:].rearrange("p (g c) -> p g c", g=32),
                scr2[:, 0:32].rearrange("p (g u) -> p g u", u=1)
                .broadcast_to([128, 32, 64]), ALU.mult),
                waits=[v3] + flb.read((k.sem[VE], k.sem[VE].n + 1)) + qib.write())
            qib.write(v4)
            v5 = k.op(VE, lambda e: e.tensor_tensor(
                scrB[:].rearrange("p (g c) -> p g c", g=32),
                ff_qi[:].rearrange("p (g c) -> p g c", g=32),
                scr2[:, 32:64].rearrange("p (g u) -> p g u", u=1)
                .broadcast_to([128, 32, 64]), ALU.mult),
                waits=[v4] + B("scrB").write() + B("qs").read((k.sem[VE], k.sem[VE].n + 1)))
            B("scrB").write(v5)
            transpose16(lambda dc: scrB[:, dc * 128:(dc + 1) * 128],
                        lambda rnd, t=t: ff_qa[:].rearrange("p (dc t) -> p dc t", dc=16)
                        [:, rnd * 8:(rnd + 1) * 8, t * 128:(t + 1) * 128],
                        B("scrB").read(), qab)
            B("scrB").r.append((k.sem[VE], k.sem[VE].n))

        for g in range(4):
            for si in range(4):
                sl = g * 4 + si
                w1q = {}
                ws = w1b.write()
                for hh_ in range(2):
                    l1 = k.dma(SY, ff_w1[hh_ * 64:(hh_ + 1) * 64, :],
                               w1T[sl][hh_], waits=ws if hh_ == 0 else None)
                    w1q[hh_] = l1
                w1b.write(l1)
                w1b.w = [w1q[1]]
                w2q = {}
                l2 = None
                for tl in range(NTT):
                    pa = B("psA%d" % (tl % 2))
                    pb2 = B("psA%d" % (2 + tl % 2))
                    for dc in range(16):
                        tv = k.op(TE, lambda e, tl=tl, dc=dc: e.matmul(
                            psA[tl % 2][:],
                            ff_qa[:, dc * TOK + tl * 128: dc * TOK + tl * 128 + 128],
                            ff_w1[:, dc * 512:(dc + 1) * 512],
                            start=(dc == 0), stop=(dc == 15)),
                            waits=([w1q[1]] + (w1b.read() + qab.read()
                                   + pa.write() if dc == 0 else [])), inc=(dc == 15))
                    w1b.r.append(tv); qab.r.append(tv)
                    pa.write(tv)
                    if tl == 0:
                        ws = w2b_.write()
                        for hh_ in range(2):
                            l2 = k.dma(SY, ff_w2[hh_ * 64:(hh_ + 1) * 64, :],
                                       w2T[sl][hh_], waits=ws if hh_ == 0 else None)
                            w2q[hh_] = l2
                        w2b_.write(l2)
                        w2b_.w = [w2q[1]]
                    for dc in range(16):
                        tv2 = k.op(TE, lambda e, tl=tl, dc=dc: e.matmul(
                            psA[2 + tl % 2][:],
                            ff_qa[:, dc * TOK + tl * 128: dc * TOK + tl * 128 + 128],
                            ff_w2[:, dc * 512:(dc + 1) * 512],
                            start=(dc == 0), stop=(dc == 15)),
                            waits=([w2q[1]] + (w2b_.read() + pb2.write()
                                   if dc == 0 else [])), inc=(dc == 15))
                    w2b_.r.append(tv2)
                    pb2.write(tv2)
                    sh = k.op(SC, lambda e, tl=tl: e.activation(
                        ff_sl[:], psA[tl % 2][:], AF.Silu, scale=wsc[:, 0:1]),
                        waits=pa.read((k.sem[SC], k.sem[SC].n + 1)) + B("ff_sl").write()
                        + B("wsc").read())
                    B("ff_sl").write(sh)
                    vh = k.op(VE, lambda e, tl=tl: e.scalar_tensor_tensor(
                        ff_h[:], psA[2 + tl % 2][:], wsc[:, 1:2], ff_sl[:],
                        ALU.mult, ALU.mult),
                        waits=pb2.read((k.sem[VE], k.sem[VE].n + 1))
                        + B("ff_sl").read((k.sem[VE], k.sem[VE].n + 1))
                        + B("ff_h").write() + B("wsc").read())
                    B("ff_h").write(vh)
                    hv1 = k.op(VE, lambda e: e.tensor_reduce(
                        scr2[:, 0:8], ff_h[:].rearrange("p (gg c) -> p gg c", gg=8),
                        AX.X, ALU.max, apply_absolute_value=True),
                        waits=B("ff_h").read((k.sem[VE], k.sem[VE].n + 1))
                        + B("qs2").write())
                    hv2 = k.op(VE, lambda e: e.tensor_scalar(
                        scr2[:, 8:16], scr2[:, 0:8], 1e-5, 1.0 / 127.0,
                        ALU.max, ALU.mult), waits=[hv1])
                    hv3 = k.op(VE, lambda e: e.reciprocal(scr2[:, 0:8], scr2[:, 8:16]),
                               waits=[hv2])
                    B("qs2").write(hv3)
                    qhib = B("ff_qih")
                    hv4 = k.op(VE, lambda e: e.tensor_tensor(
                        ff_qih[:].rearrange("p (gg c) -> p gg c", gg=8),
                        ff_h[:].rearrange("p (gg c) -> p gg c", gg=8),
                        scr2[:, 0:8].rearrange("p (gg u) -> p gg u", u=1)
                        .broadcast_to([128, 8, 64]), ALU.mult),
                        waits=[hv3] + B("ff_h").read((k.sem[VE], k.sem[VE].n + 1))
                        + qhib.write())
                    qhib.write(hv4)
                    qt = ff_qt if (tl % 2 == 0) else ff_qt2
                    qtb = B("ff_qt%d" % (tl % 2))
                    hv5 = k.op(VE, lambda e, qt=qt: e.tensor_tensor(
                        qt[:].rearrange("p (gg c) -> p gg c", gg=8),
                        ff_qih[:].rearrange("p (gg c) -> p gg c", gg=8),
                        scr2[:, 8:16].rearrange("p (gg u) -> p gg u", u=1)
                        .broadcast_to([128, 8, 64]), ALU.mult),
                        waits=[hv4] + qtb.write()
                        + B("qs2").read((k.sem[VE], k.sem[VE].n + 1)))
                    qtb.write(hv5)
                    pend_q.append((si, tl, qt, qtb))
                    if len(pend_q) > 1:
                        emit_qh_transpose(*pend_q.pop(0))
            while pend_q:
                emit_qh_transpose(*pend_q.pop(0))
            for dms in range(4):
                ws3 = w3b[0].write() + w3b[1].write()
                lv0 = k.dma(SY, ff_w3[0][0:64, :],
                            w3T[dms][g][0][:, 0:4096], waits=ws3)
                lv0b = k.dma(SY, ff_w3[1][0:64, :], w3T[dms][g][0][:, 4096:8192])
                lv1 = k.dma(SY, ff_w3[0][64:128, :], w3T[dms][g][1][:, 0:4096])
                lv1b = k.dma(SY, ff_w3[1][64:128, :], w3T[dms][g][1][:, 4096:8192])
                w3b[0].write(lv1b)
                w3b[1].write(lv1b)
                for tl in range(NTT):
                    pa = B("psA%d" % (tl % 4))
                    for c in range(16):
                        hh = c // 8
                        tv = k.op(TE, lambda e, tl=tl, c=c, hh=hh: e.matmul(
                            psA[tl % 4][:],
                            ff_qh[:, c * TOK + tl * 128: c * TOK + tl * 128 + 128],
                            ff_w3[hh][:, (c % 8) * 512:(c % 8 + 1) * 512],
                            start=(c == 0), stop=(c == 15)),
                            waits=(w3b[hh].read() + qhb.read() + pa.write())
                            if c in (0, 8) else None, inc=(c in (7, 15)))
                        if c in (7, 15):
                            w3b[hh].r.append(tv)
                    qhb.r.append(tv)
                    pa.write(tv)
                    xb = B("x%d" % tl)
                    vv = k.op(VE, lambda e, tl=tl, dms=dms: e.scalar_tensor_tensor(
                        x_sb[:, tl * D + dms * 512: tl * D + dms * 512 + 512],
                        psA[tl % 4][:], wsc[:, 2:3],
                        x_sb[:, tl * D + dms * 512: tl * D + dms * 512 + 512],
                        ALU.mult, ALU.add),
                        waits=pa.read((k.sem[VE], k.sem[VE].n + 1)) + xb.write()
                        + B("wsc").read())
                    xb.write(vv)

    def emit_attn_b():
        lnb = B("ab_ln")
        for t in range(NTT):
            emit_ln(x_sb[:, t * D:(t + 1) * D], B("x%d" % t), [(scrB[:], B("scrB"))])
            transpose16(lambda dc: scrB[:, dc * 128:(dc + 1) * 128],
                        lambda rnd, t=t: ab_ln[:].rearrange("p (dc t) -> p dc t", dc=16)
                        [:, rnd * 8:(rnd + 1) * 8, t * 128:(t + 1) * 128],
                        B("scrB").read(), lnb)
            B("scrB").r.append((k.sem[VE], k.sem[VE].n))

        wb = B("ab_w")
        def proj_block(wblk, ocb, out_fn, out_buf):
            ws = wb.write()
            bq = {}
            for hh_ in range(2):
                lv = k.dma(SY, ab_w[hh_ * 64:(hh_ + 1) * 64, :],
                           wblk[ocb][hh_], waits=ws if hh_ == 0 else None)
                bq[hh_] = lv
            wb.write(lv)
            wb.w = [bq[1]]
            for oc4 in range(4):
                for ts in range(2):
                    pa = B("psA%d" % (2 * (oc4 % 2) + ts))
                    for dc in range(16):
                        tv = k.op(TE, lambda e, oc4=oc4, ts=ts, dc=dc: e.matmul(
                            psA[2 * (oc4 % 2) + ts][:],
                            ab_w[:, dc * 512 + oc4 * 128: dc * 512 + oc4 * 128 + 128],
                            ab_ln[:, dc * TOK + ts * 512: dc * TOK + ts * 512 + 512],
                            start=(dc == 0), stop=(dc == 15)),
                            waits=([bq[1]] + (wb.read() + lnb.read() + pa.write()
                                   if dc == 0 else [])), inc=(dc == 15))
                    wb.r.append(tv); lnb.r.append(tv)
                    pa.write(tv)
                    vv = k.op(VE, lambda e, oc4=oc4, ts=ts: e.tensor_copy(
                        out_fn(ocb * 4 + oc4, ts), psA[2 * (oc4 % 2) + ts][:]),
                        waits=pa.read((k.sem[VE], k.sem[VE].n + 1)) + out_buf.write())
                    out_buf.w.append(vv)

        cosb, sinb = B("ab_cos"), B("ab_sin")
        def rope_pair(raw, ri, rj, c, cp, dst_fn, dst_buf, rb_):
            lc1 = k.dma(SY, ab_cos[:, 0:TOK], cos_in[c * 128:(c + 1) * 128, :],
                        waits=cosb.write())
            lc2 = k.dma(SY, ab_cos[:, TOK:2 * TOK], cos_in[cp * 128:(cp + 1) * 128, :])
            cosb.write(lc2)
            ls1 = k.dma(SY, ab_sin[:, 0:TOK], sin_in[c * 128:(c + 1) * 128, :],
                        waits=sinb.write())
            ls2 = k.dma(SY, ab_sin[:, TOK:2 * TOK], sin_in[cp * 128:(cp + 1) * 128, :])
            sinb.write(ls2)
            for (cc, half, oth, sgn) in ((c, 0, rj, -1.0), (cp, 1, ri, 1.0)):
                me = ri if cc == c else rj
                v1 = k.op(VE, lambda e, me=me, half=half: e.tensor_tensor(
                    scrA[:, 0:TOK], raw[:, me * TOK:(me + 1) * TOK],
                    ab_cos[:, half * TOK:(half + 1) * TOK], ALU.mult),
                    waits=cosb.read((k.sem[VE], k.sem[VE].n + 1))
                    + rb_.read((k.sem[VE], k.sem[VE].n + 1)) + B("scrAr").write())
                v2 = k.op(VE, lambda e, oth=oth, half=half: e.tensor_tensor(
                    scrA[:, TOK:2 * TOK], raw[:, oth * TOK:(oth + 1) * TOK],
                    ab_sin[:, half * TOK:(half + 1) * TOK], ALU.mult),
                    waits=sinb.read((k.sem[VE], k.sem[VE].n + 1))
                    + rb_.read((k.sem[VE], k.sem[VE].n + 1)))
                v3 = k.op(VE, lambda e, cc=cc, sgn=sgn: e.scalar_tensor_tensor(
                    dst_fn(cc), scrA[:, TOK:2 * TOK], sgn, scrA[:, 0:TOK],
                    ALU.mult, ALU.add),
                    waits=[v1, v2] + dst_buf.write())
                B("scrAr").write(v3)
                dst_buf.w.append(v3)

        qb = B("ab_q")
        qrb = B("ab_qr")
        for half in range(2):
            ocbs = (0, 2) if half == 0 else (1, 3)
            for i, ocb in enumerate(ocbs):
                proj_block(wqbT, ocb,
                           lambda oc, ts, i=i: ab_qr[:, (i * 4 + oc % 4) * TOK + ts * 512:
                                                     (i * 4 + oc % 4) * TOK + ts * 512 + 512],
                           qrb)
            for j in range(4):
                c = half * 4 + j
                cp = c + 8
                rope_pair(ab_qr, j, 4 + j, c, cp,
                          lambda cc: ab_q[:, cc * TOK:(cc + 1) * TOK], qb, qrb)
            qrb.r = qb.w[-8:]
            qrb.w = []
        kst = B("kstage")
        for half in range(2):
            ocbs = (0, 2) if half == 0 else (1, 3)
            for i, ocb in enumerate(ocbs):
                proj_block(wkbT, ocb,
                           lambda oc, ts, i=i: ab_qr[:, (i * 4 + oc % 4) * TOK + ts * 512:
                                                     (i * 4 + oc % 4) * TOK + ts * 512 + 512],
                           qrb)
            for j in range(4):
                c = half * 4 + j
                cp = c + 8

                def kdst(cc, c=c):
                    return (scrB[:, 0:TOK] if cc == c else scrB2[:, 0:TOK])
                rope_pair(ab_qr, j, 4 + j, c, cp, kdst, kst, qrb)
                d1 = k.dma(SY, kv_pack[c * 128:(c + 1) * 128, :], scrB[:, 0:TOK],
                           waits=kst.write())
                d2 = k.dma(SY, kv_pack[cp * 128:(cp + 1) * 128, :], scrB2[:, 0:TOK])
                kst.r = [d1, d2]
                kst.w = []
                B("scrB").r.append(d1)
                B("scrB2").r.append(d2)
            qrb.r.append((k.sem[SY], k.sem[SY].n))
            qrb.w = []
        for ocb in range(4):
            ws = wb.write()
            bq = {}
            for hh_ in range(2):
                lv = k.dma(SY, ab_w[hh_ * 64:(hh_ + 1) * 64, :],
                           wvbT[ocb][hh_], waits=ws if hh_ == 0 else None)
                bq[hh_] = lv
            wb.write(lv)
            wb.w = [bq[1]]
            for tt in range(NTT):
                pa = B("psA%d" % (tt % 4))
                for dc in range(16):
                    tv = k.op(TE, lambda e, tt=tt, dc=dc: e.matmul(
                        psA[tt % 4][:],
                        ab_ln[:, dc * TOK + tt * 128: dc * TOK + tt * 128 + 128],
                        ab_w[:, dc * 512:(dc + 1) * 512],
                        start=(dc == 0), stop=(dc == 15)),
                        waits=([bq[1]] + (wb.read() + lnb.read() + pa.write()
                               if dc == 0 else [])), inc=(dc == 15))
                wb.r.append(tv); lnb.r.append(tv)
                pa.write(tv)
                sv = k.op(SC, lambda e, tt=tt: e.copy(scrB[:, 1536:2048], psA[tt % 4][:]),
                          waits=pa.read((k.sem[SC], k.sem[SC].n + 1)) + B("vstage").write())
                B("vstage").write(sv)
                r0 = D + tt * 256 + ocb // 2
                dv = k.dma(SY,
                           kv_pack[r0:r0 + 255:2,
                                   (ocb % 2) * 512:(ocb % 2) * 512 + 512],
                           scrB[:, 1536:2048],
                           waits=B("vstage").read((k.sem[SY], k.sem[SY].n + 16)))
        kv_done = (k.sem[SY], k.sem[SY].n)

        k.sem_cc.n += 1
        ccv = k.sem_cc.n
        kvd = kv_done

        def _cc(eng, ctx, kvd=kvd, ccv=ccv):
            if ctx.get(kvd[0].h.num, -1) < kvd[1]:
                eng.wait_ge(kvd[0].h, kvd[1])
                ctx[kvd[0].h.num] = kvd[1]
            eng.collective_compute(
                "AllGather", ALU.bypass,
                replica_groups=[[0, 1], [2, 3], [4, 5], [6, 7]],
                ins=[kv_pack], outs=[kv_g]).then_inc(k.sem_cc.h, 1)
        k.items["gpsimd"].append(_cc)
        cc_done = (k.sem_cc, ccv)


        qdone_gate = [(k.sem[TE], k.sem[TE].n), (k.sem[VE], k.sem[VE].n)]

        eb = [B("ab_exp0"), B("ab_exp1")]
        cb = B("ab_ctx")

        def prologue(h):
            hp = h % 2
            kvt = ab_kv[hp]
            kvb = B("ab_kv%d" % hp)
            ws_ = kvb.write()
            l1 = k.dma(SY, kvt[:, 0:TOK], kv_g[h * 128:(h + 1) * 128, :],
                       waits=ws_ + [cc_done])
            l2 = k.dma(SY, kvt[:, TOK:SEQ], kv_g[2 * D + h * 128: 2 * D + h * 128 + 128, :])
            for a in range(16):
                half, al = a // 8, a % 8
                r0 = half * 2 * D + D + al * 256 + h // 8
                l3 = k.dma(SY, kvt[:, SEQ + a * 128: SEQ + (a + 1) * 128],
                           kv_g[r0:r0 + 255:2, (h % 8) * 128:(h % 8) * 128 + 128])
            kvb.write(l3)
            v1 = k.op(VE, lambda e, kvt=kvt: e.tensor_scalar_min(
                scrA[:], kvt[:, 0:SEQ], 0.0),
                waits=kvb.read((k.sem[VE], k.sem[VE].n + 1)) + B("scrAr").write())
            B("scrAr").write(v1)
            s1 = k.op(SC, lambda e: e.activation(scrB[:], scrA[:], AF.Exp),
                      waits=B("scrAr").read((k.sem[SC], k.sem[SC].n + 1)) + B("scrB").write())
            B("scrB").write(s1)
            kfb = B("ab_kf")
            v2 = k.op(VE, lambda e, kvt=kvt: e.scalar_tensor_tensor(
                ab_kf[:], kvt[:, 0:SEQ], 0.0, scrB[:], ALU.max, ALU.add),
                waits=[s1] + kvb.read((k.sem[VE], k.sem[VE].n + 1)) + kfb.write()
                + B("scrB").read((k.sem[VE], k.sem[VE].n + 1)))
            kfb.write(v2)
            v3 = k.op(VE, lambda e: e.tensor_reduce(scr2[:, 0:1], ab_kf[:], AX.X, ALU.add),
                      waits=kfb.read((k.sem[VE], k.sem[VE].n + 1)) + B("ntf").write())
            B("ntf").write(v3)
            v4 = k.op(VE, lambda e, hp=hp: e.tensor_copy(ntb[:, hp:hp + 1], scr2[:, 0:1]),
                      waits=B("ntf").read((k.sem[VE], k.sem[VE].n + 1)) + B("ntb%d" % hp).write())
            B("ntb%d" % hp).write(v4)
            kftb = B("ab_kft")
            transpose16(lambda dc: ab_kf[:, dc * 128:(dc + 1) * 128],
                        lambda rnd: ab_kft[:].rearrange("p (dc t) -> p dc t", dc=16)
                        [:, rnd * 8:(rnd + 1) * 8, :], kfb.read(), kftb)
            kfb.r.append((k.sem[TE], k.sem[TE].n))
            v5 = k.op(VE, lambda e, h=h: e.tensor_scalar_min(
                scrA[:, 0:TOK], ab_q[:, h * TOK:(h + 1) * TOK], 0.0),
                waits=qb.read((k.sem[VE], k.sem[VE].n + 1)) + B("scrAr").write())
            B("scrAr").write(v5)
            s2 = k.op(SC, lambda e: e.activation(scrB[:, 0:TOK], scrA[:, 0:TOK], AF.Exp),
                      waits=B("scrAr").read((k.sem[SC], k.sem[SC].n + 1)) + B("scrB").write())
            B("scrB").write(s2)
            qfb = B("ab_qf%d" % hp)
            v6 = k.op(VE, lambda e, h=h, hp=hp: e.scalar_tensor_tensor(
                ab_qf[hp][:], ab_q[:, h * TOK:(h + 1) * TOK], 0.0, scrB[:, 0:TOK],
                ALU.max, ALU.add),
                waits=[s2] + qb.read((k.sem[VE], k.sem[VE].n + 1)) + qfb.write()
                + B("scrB").read((k.sem[VE], k.sem[VE].n + 1)))
            qfb.write(v6)
            pm = B("psE")
            for st_ in range(16):
                tv = k.op(TE, lambda e, st_=st_, kvt=kvt: e.matmul(
                    psE[:, 0:128], ab_kft[:, st_ * 128:(st_ + 1) * 128],
                    kvt[:, SEQ + st_ * 128: SEQ + (st_ + 1) * 128],
                    start=(st_ == 0), stop=(st_ == 15)),
                    waits=(kftb.read() + kvb.read() + pm.write()) if st_ == 0 else None,
                    inc=(st_ == 15))
            kftb.r.append(tv); kvb.r.append(tv)
            pm.write(tv)
            sm = k.op(SC, lambda e, hp=hp: e.copy(ab_mem[hp][:], psE[:, 0:128]),
                      waits=pm.read((k.sem[SC], k.sem[SC].n + 1)) + B("ab_mem%d" % hp).write())
            B("ab_mem%d" % hp).write(sm)

        def scores_b(i):
            h, qq = divmod(i, 4)
            hp, p = h % 2, i % 2
            kvt = ab_kv[hp]
            kvb = B("ab_kv%d" % hp)
            exp_t = ab_exp[p]
            qof = qq * 256
            for kt in range(16):
                pa = B("psA%d" % (kt % 2))
                tv = k.op(TE, lambda e, kt=kt, h=h, qof=qof, kvt=kvt: e.matmul(
                    psA[kt % 2][:, 0:256], kvt[:, kt * 128:(kt + 1) * 128],
                    ab_q[:, h * TOK + qof: h * TOK + qof + 256],
                    start=True, stop=True),
                    waits=pa.write() + (kvb.read() if kt == 0 else []))
                if kt == 0:
                    kvb.r.append(tv)
                pa.write(tv)
                sv = k.op(SC, lambda e, kt=kt, exp_t=exp_t: e.activation(
                    exp_t[:, kt * 256:(kt + 1) * 256], psA[kt % 2][:, 0:256], AF.Exp,
                    bias=cst[:, 1:2], scale=cst[:, 2:3]),
                    waits=pa.read((k.sem[SC], k.sem[SC].n + 1)) + eb[p].write()
                    + B("cst").read())
                eb[p].w.append(sv)
            kvb.r.append(tv)

        def tail_b(i):
            h, qq = divmod(i, 4)
            hp, p = h % 2, i % 2
            kvt = ab_kv[hp]
            kvb = B("ab_kv%d" % hp)
            exp_t = ab_exp[p]
            qfb = B("ab_qf%d" % hp)
            qof = qq * 256
            pd0, pd1 = B("psD0"), B("psD1")
            for kt in range(16):
                tv = k.op(TE, lambda e, kt=kt, exp_t=exp_t: e.matmul(
                    psD[0][:, 0:256], ones128[:], exp_t[:, kt * 256:(kt + 1) * 256],
                    start=(kt == 0), stop=(kt == 15)),
                    waits=(eb[p].read() + pd0.write() + B("ones").read())
                    if kt == 0 else None, inc=(kt == 15))
            eb[p].r.append(tv)
            pd0.write(tv)
            tl_ = k.op(TE, lambda e, qof=qof, hp=hp: e.matmul(
                psD[1][:, 0:256], ntb[:, hp:hp + 1], ab_qf[hp][:, qof:qof + 256],
                start=True, stop=True),
                waits=B("ntb%d" % hp).read() + qfb.read() + pd1.write())
            B("ntb%d" % hp).r.append(tl_); qfb.r.append(tl_)
            pd1.write(tl_)
            pc = B("psA2")
            for kt in range(16):
                tv = k.op(TE, lambda e, kt=kt, kvt=kvt, exp_t=exp_t: e.matmul(
                    psA[2][:, 0:256], kvt[:, SEQ + kt * 128: SEQ + (kt + 1) * 128],
                    exp_t[:, kt * 256:(kt + 1) * 256],
                    start=(kt == 0), stop=(kt == 15)),
                    waits=(eb[p].read() + pc.write() + kvb.read()) if kt == 0 else None,
                    inc=(kt == 15))
            eb[p].r.append(tv); kvb.r.append(tv)
            pc.write(tv)
            pmo = B("psA3")
            tv = k.op(TE, lambda e, qof=qof, hp=hp: e.matmul(
                psA[3][:, 0:256], ab_mem[hp][:], ab_qf[hp][:, qof:qof + 256],
                start=True, stop=True),
                waits=B("ab_mem%d" % hp).read() + qfb.read() + pmo.write())
            B("ab_mem%d" % hp).r.append(tv); qfb.r.append(tv)
            pmo.write(tv)
            vr1 = k.op(VE, lambda e: e.reciprocal(rowscr[0:1, 0:256], psD[0][0:1, 0:256]),
                       waits=pd0.read((k.sem[VE], k.sem[VE].n + 1)) + B("rowscr").write())
            vr2 = k.op(VE, lambda e: e.reciprocal(rowscr[0:1, 256:512], psD[1][0:1, 0:256]),
                       waits=pd1.read((k.sem[VE], k.sem[VE].n + 1)))
            B("rowscr").write(vr2)
            pa0, pa1 = B("psA0"), B("psA1")
            tb1 = k.op(TE, lambda e: e.matmul(psA[0][:, 0:256], ones1f[:],
                                              rowscr[0:1, 0:256], start=True, stop=True),
                       waits=[vr1, vr2] + pa0.write())
            B("rowscr").r.append(tb1)
            pa0.write(tb1)
            tb2 = k.op(TE, lambda e: e.matmul(psA[1][:, 0:256], ones1f[:],
                                              rowscr[0:1, 256:512],
                                              start=True, stop=True),
                       waits=pa1.write())
            B("rowscr").r.append(tb2)
            pa1.write(tb2)
            sc1 = k.op(SC, lambda e: e.copy(scrA[:, 1024:1280], psA[0][:, 0:256]),
                       waits=pa0.read((k.sem[SC], k.sem[SC].n + 1)) + B("scrAr").write())
            sc2 = k.op(SC, lambda e: e.copy(scrA[:, 1280:1536], psA[1][:, 0:256]),
                       waits=pa1.read((k.sem[SC], k.sem[SC].n + 1)))
            B("scrAr").write(sc2)
            v1_ = k.op(VE, lambda e: e.tensor_tensor(scrA[:, 0:256], psA[3][:, 0:256],
                                                     scrA[:, 1280:1536], ALU.mult),
                       waits=pmo.read((k.sem[VE], k.sem[VE].n + 1)) + [sc1, sc2])
            v2_ = k.op(VE, lambda e: e.tensor_tensor(scrA[:, 256:512], psA[2][:, 0:256],
                                                     scrA[:, 1024:1280], ALU.mult),
                       waits=pc.read((k.sem[VE], k.sem[VE].n + 1)))
            v3_ = k.op(VE, lambda e: e.tensor_sub(scrA[:, 0:256], scrA[:, 0:256],
                                                  scrA[:, 256:512]),
                       waits=[v1_, v2_])
            v4_ = k.op(VE, lambda e, h=h, qof=qof: e.scalar_tensor_tensor(
                ab_ctx[:, h * TOK + qof: h * TOK + qof + 256],
                scrA[:, 0:256], g_b[:, h:h + 1], scrA[:, 256:512],
                ALU.mult, ALU.add),
                waits=[v3_] + cb.write() + B("g_b").read())
            B("scrAr").write(v4_)
            cb.w.append(v4_)

        prologue(0)
        scores_b(0)
        for i in range(NH * 4):
            h, qq = divmod(i, 4)
            if qq == 0 and h + 1 < NH:
                prologue(h + 1)
            if i + 1 < NH * 4:
                scores_b(i + 1)
            tail_b(i)

        w3b_ = B("ab_w3")
        head_gate = [(k.sem[TE], k.sem[TE].n), (k.sem[VE], k.sem[VE].n),
                     (k.sem[SC], k.sem[SC].n)]
        for ocb in range(4):
            ws_ = w3b_.write() + (head_gate if ocb == 0 else [])
            bq3 = {}
            for hh_ in range(2):
                lv = k.dma(SY, ab_w3[hh_ * 64:(hh_ + 1) * 64, :],
                           wobT[ocb][hh_], waits=ws_ if hh_ == 0 else None)
                bq3[hh_] = lv
            w3b_.write(lv)
            w3b_.w = [bq3[1]]
            for tt in range(NTT):
                pa = B("psA%d" % (tt % 4))
                for dc in range(16):
                    tv = k.op(TE, lambda e, tt=tt, dc=dc: e.matmul(
                        psA[tt % 4][:],
                        ab_ctx[:, dc * TOK + tt * 128: dc * TOK + tt * 128 + 128],
                        ab_w3[:, dc * 512:(dc + 1) * 512],
                        start=(dc == 0), stop=(dc == 15)),
                        waits=([bq3[1]] + (w3b_.read() + cb.read() + pa.write()
                               if dc == 0 else [])), inc=(dc == 15))
                w3b_.r.append(tv); cb.r.append(tv)
                pa.write(tv)
                xb = B("x%d" % tt)
                vv = k.op(VE, lambda e, tt=tt, ocb=ocb: e.tensor_add(
                    x_sb[:, tt * D + ocb * 512: tt * D + ocb * 512 + 512],
                    x_sb[:, tt * D + ocb * 512: tt * D + ocb * 512 + 512],
                    psA[tt % 4][:]),
                    waits=pa.read((k.sem[VE], k.sem[VE].n + 1)) + xb.write())
                xb.write(vv)

    if dbg:
        dbg_k = nc.dram_tensor("dbg_k", [128, 4 * SEQ], BF16, kind="ExternalOutput").ap()
        dbg_v = nc.dram_tensor("dbg_v", [128, 16 * 512], BF16, kind="ExternalOutput").ap()
        dbg_q = nc.dram_tensor("dbg_q", [128, 16 * TOK], BF16, kind="ExternalOutput").ap()
        dbg_c = nc.dram_tensor("dbg_c", [128, 16 * TOK], BF16, kind="ExternalOutput").ap()
        dbg_g = nc.dram_tensor("dbg_g", [4 * D, TOK], BF16, kind="ExternalOutput").ap()

    emit_attn_a2()
    if dbg and n_sub == 1:
        k.barrier()
        k.dma(SY, dbg_k, aa_k[:])
        k.dma(SY, dbg_v, aa_v[:])
        k.dma(SY, dbg_q, aa_q[:])
        k.dma(SY, dbg_c, aa_ctx[:])
    k.barrier()
    if n_sub >= 2:
        emit_ffn()
        k.barrier()
    if n_sub >= 3:
        emit_attn_b()
        if dbg and n_sub == 3:
            k.barrier()
            k.dma(SY, dbg_q, ab_q[:])
            k.dma(SY, dbg_c, ab_ctx[:])
            for i in range(8):
                k.dma(SY, dbg_g[i * TOK:(i + 1) * TOK, :],
                      kv_g[i * TOK:(i + 1) * TOK, :])
        k.barrier()
    if n_sub >= 4:
        emit_ffn()
        k.barrier()

    last = None
    for t in range(NTT):
        last = k.dma(SY, y[t * 128:(t + 1) * 128, :], x_sb[:, t * D:(t + 1) * D],
                     waits=B("x%d" % t).write())
    k.raw("gpsimd", lambda e, v=last[1]: e.wait_ge(last[0].h, v))

    k.replay()
    return nc


def _tables():
    ident = np.eye(128, dtype=np.float32)
    inv_freq = (1.0 / (10000.0 ** (np.arange(0, D, 2, dtype=np.float32) / np.float32(D)))
                ).astype(np.float32)
    return ident, inv_freq


def _prep_src_for_core(inputs, c):
    w1 = np.asarray(inputs["L0_ffn_w1"]); w2 = np.asarray(inputs["L0_ffn_w2"])
    w3 = np.asarray(inputs["L0_ffn_w3"])
    att = [np.asarray(inputs["L0_attn_b_Wq"]), np.asarray(inputs["L0_attn_b_Wk"]),
           np.asarray(inputs["L0_attn_b_Wv"]), np.asarray(inputs["L0_attn_b_Wo"])]
    pans = []
    for b in range(8 * c, 8 * c + 8):
        if b < 16:
            for rb in range(4 * b, 4 * b + 4):
                pans.append(w1[rb * 128:(rb + 1) * 128, :])
        elif b < 32:
            ds = b - 16
            for rb in range(4 * ds, 4 * ds + 4):
                pans.append(w2[rb * 128:(rb + 1) * 128, :])
        elif b < 48:
            g, dms = divmod(b - 32, 4)
            for rb in range(4 * dms, 4 * dms + 4):
                pans.append(w3[rb * 128:(rb + 1) * 128, g * D:(g + 1) * D])
        else:
            wi, ocb = divmod(b - 48, 4)
            for rb in range(4 * ocb, 4 * ocb + 4):
                pans.append(att[wi][rb * 128:(rb + 1) * 128, :])
    return np.ascontiguousarray(np.stack(pans, 0))


def kernel(**inputs):
    x = np.asarray(inputs["L0_attn_a_x"])
    ident, inv_freq = _tables()
    nc = build_program(4)
    in_maps = []
    for c in range(8):
        b, h = c // 2, c % 2
        off = h * TOK
        pos = np.arange(off, off + TOK, dtype=np.float32)
        freqs = pos[:, None] * inv_freq[None, :]
        emb = np.concatenate([freqs, freqs], axis=1)
        cos_t = np.ascontiguousarray(np.cos(emb).T.astype(np.float32))
        sin_t = np.ascontiguousarray(np.sin(emb).T.astype(np.float32))
        kglob = np.concatenate([np.arange(off, off + TOK),
                                np.arange(TOK - off, 2 * TOK - off)])
        qglob = (np.arange(TOK) + off)
        maskT = (kglob[:, None] <= qglob[None, :]).astype(ml_dtypes.float8_e4m3fn)
        m = {
            "x_own": np.ascontiguousarray(x[b, off:off + TOK]),
            "x_oth": np.ascontiguousarray(x[b, TOK - off:2 * TOK - off]),
            "wq_a": np.asarray(inputs["L0_attn_a_Wq"]),
            "wk_a": np.asarray(inputs["L0_attn_a_Wk"]),
            "wv_a": np.asarray(inputs["L0_attn_a_Wv"]),
            "wo_a": np.asarray(inputs["L0_attn_a_Wo"]),
            "bo_a": np.asarray(inputs["L0_attn_a_bo"]).reshape(1, D),
            "wq_b": np.asarray(inputs["L0_attn_b_Wq"]),
            "wk_b": np.asarray(inputs["L0_attn_b_Wk"]),
            "wv_b": np.asarray(inputs["L0_attn_b_Wv"]),
            "wo_b": np.asarray(inputs["L0_attn_b_Wo"]),
            "gate": np.asarray(inputs["L0_attn_b_gate"]).reshape(1, NH),
            "w1": np.asarray(inputs["L0_ffn_w1"]),
            "w2": np.asarray(inputs["L0_ffn_w2"]),
            "w3": np.asarray(inputs["L0_ffn_w3"]),
            "ident": ident,
            "maskT": maskT,
            "cos_t": cos_t,
            "sin_t": sin_t,
            "prep_src": _prep_src_for_core(inputs, c),
            "sel": np.full((128, 1), 1.0 if c < 6 else 0.0, np.float32),
        }
        in_maps.append(m)
    res = run_bass_kernel_spmd(nc, in_maps, core_ids=list(range(8)))
    out = np.empty((4, SEQ, D), dtype=np.float32)
    for c in range(8):
        b, h = c // 2, c % 2
        out[b, h * TOK:(h + 1) * TOK] = res.results[c]["y"]
    return out


# version 13
# speedup vs baseline: 1.0497x; 1.0470x over previous
import math
import numpy as np
import ml_dtypes
import concourse.bass as bass
import concourse.mybir as mybir
from concourse.bass_utils import run_bass_kernel_spmd

F32 = mybir.dt.float32
BF16 = mybir.dt.bfloat16
FP8 = mybir.dt.float8e4
I32 = mybir.dt.int32
AF = mybir.ActivationFunctionType
ALU = mybir.AluOpType
AX = mybir.AxisListType

D = 2048
DFF = 8192
NH = 16
HD = 128
TOK = 1024
NTT = 8
SEQ = 2048
EXP_C = 32.0
ISQ = 1.0 / math.sqrt(128.0)

SB_LO = 16512
SB_HI = 229344


def _merge(waits):
    d = {}
    for s, v in waits:
        if v > 0 and (s not in d or v > d[s]):
            d[s] = v
    return [(s, v) for s, v in d.items()]


class Sem:
    def __init__(self, nc, name):
        self.h = nc.alloc_semaphore(name)
        self.n = 0


class Buf:

    def __init__(self):
        self.w = []
        self.r = []

    def read(self, my_sv=None):
        if my_sv is not None:
            self.r.append(my_sv)
        return list(self.w)

    def write(self, my_sv=None):
        w = self.r + self.w
        if my_sv is not None:
            self.w = [my_sv]
            self.r = []
        return w


class K:

    ENGS = ("sync", "gpsimd", "tensor", "vector", "scalar")

    def __init__(self, nc):
        self.nc = nc
        self.items = {e: [] for e in self.ENGS}
        self.sem = {e: Sem(nc, "s_" + e) for e in self.ENGS}
        self.sem_cc = Sem(nc, "s_cc")

    def _wrap(self, eng_name, fn, waits, sem, inc):
        if inc:
            sem.n += 1 if eng_name not in ("sync", "gpsimd") else 16
        val = sem.n
        waits = _merge(waits or [])

        def f(eng, ctx):
            for (s, v) in waits:
                if ctx.get(s.h.num, -1) < v:
                    eng.wait_ge(s.h, v)
                    ctx[s.h.num] = v
            ins = fn(eng)
            if inc:
                k = 16 if eng_name in ("sync", "gpsimd") else 1
                ins.then_inc(sem.h, k)
        self.items[eng_name].append(f)
        return (sem, val)

    def dma(self, q, out_ap, in_ap, waits=None):
        return self._wrap(q, lambda e: e.dma_start(out=out_ap, in_=in_ap),
                          waits, self.sem[q], True)

    def op(self, eng, fn, waits=None, inc=True):
        return self._wrap(eng, fn, waits, self.sem[eng], inc)

    def raw(self, eng, fn):
        self.items[eng].append(lambda e, ctx: fn(e))

    def barrier(self):
        snap = [(s, s.n) for s in self.sem.values()] + [(self.sem_cc, self.sem_cc.n)]
        for e in self.ENGS:
            mine = self.sem[e]
            waits = _merge([(s, v) for (s, v) in snap if s is not mine])

            def f(eng, ctx, waits=waits):
                for (s, v) in waits:
                    if ctx.get(s.h.num, -1) < v:
                        eng.wait_ge(s.h, v)
                        ctx[s.h.num] = v
            self.items[e].append(f)

    def replay(self):
        nc = self.nc
        with nc.Block() as blk:
            for e in self.ENGS:
                def make(e):
                    def body(eng):
                        ctx = {}
                        for f in self.items[e]:
                            f(eng, ctx)
                    return body
                getattr(blk, e)(make(e))


def build_program(n_sub=4, dbg=False):
    nc = bass.Bass("TRN2", target_bir_lowering=False, num_devices=8)
    k = K(nc)

    din = {}
    def dram_in(name, shape, dtype=F32):
        din[name] = nc.dram_tensor(name, shape, dtype, kind="ExternalInput").ap()
        return din[name]

    x_own = dram_in("x_own", [TOK, D])
    x_oth = dram_in("x_oth", [TOK, D])
    wq_a = dram_in("wq_a", [D, D])
    wk_a = dram_in("wk_a", [D, 512])
    wv_a = dram_in("wv_a", [D, 512])
    wo_a = dram_in("wo_a", [D, D])
    bo_a = dram_in("bo_a", [1, D])
    wq_b = dram_in("wq_b", [D, D])
    wk_b = dram_in("wk_b", [D, D])
    wv_b = dram_in("wv_b", [D, D])
    wo_b = dram_in("wo_b", [D, D])
    gate = dram_in("gate", [1, NH])
    w1 = dram_in("w1", [DFF, D])
    w2 = dram_in("w2", [DFF, D])
    w3 = dram_in("w3", [D, DFF])
    ident_in = dram_in("ident", [128, 128])
    prep_src = dram_in("prep_src", [32, 128, D])
    sel_in = dram_in("sel", [128, 1])
    maskT_in = dram_in("maskT", [SEQ, TOK], FP8)
    cos_in = dram_in("cos_t", [D, TOK])
    sin_in = dram_in("sin_t", [D, TOK])
    y = nc.dram_tensor("y", [TOK, D], F32, kind="ExternalOutput").ap()

    PRANK = 8200
    prep_in = nc.dram_tensor("prep_in", [PRANK, 1024], BF16, kind="Internal").ap()
    prep_all = nc.dram_tensor("prep_all", [8 * PRANK, 1024], BF16, kind="Internal").ap()

    def blk_view(b):
        r0 = (b // 8) * PRANK + (b % 8) * 1024
        return prep_all[r0:r0 + 1024, :].rearrange("(p a) j -> p (a j)", p=128)

    w1T = [blk_view(ds) for ds in range(16)]
    w2T = [blk_view(16 + ds) for ds in range(16)]
    w3T = [[blk_view(32 + g * 4 + dms) for g in range(4)] for dms in range(4)]
    wqbT = [blk_view(48 + i) for i in range(4)]
    wkbT = [blk_view(52 + i) for i in range(4)]
    wvbT = [blk_view(56 + i) for i in range(4)]
    wobT = [blk_view(60 + i) for i in range(4)]
    kv_pack = nc.dram_tensor("kv_pack", [2 * D, TOK], BF16, kind="Internal").ap()
    kv_g = nc.dram_tensor("kv_g", [4 * D, TOK], BF16, kind="Internal").ap()

    used = []
    def S(name, shape, dtype, off):
        nb = int(np.prod(shape[1:])) * mybir.dt.size(dtype)
        assert off % 32 == 0 and off >= SB_LO and off + nb <= SB_HI, (name, off, nb)
        used.append((name, off, off + nb))
        return nc.alloc_sbuf_tensor_at(name, shape, dtype, offset=off)

    o = SB_LO
    def P(name, shape, dtype):
        nonlocal o
        t = S(name, shape, dtype, o)
        o += (int(np.prod(shape[1:])) * mybir.dt.size(dtype) + 31) & ~31
        return t

    x_sb = P("x_sb", [128, NTT * D], F32)
    ident = P("identb", [128, 128], BF16)
    ones128 = P("ones128", [128, 1], BF16)
    ones128f = P("ones128f", [128, 1], F32)
    ones1f = P("ones1f", [1, 128], F32)
    bo_b = P("bo_b", [128, D], BF16)
    g_b = P("g_b", [128, NH], F32)
    wsc = P("wsc", [128, 4], F32)
    scr = P("scr", [128, 16], F32)
    cst = P("cst", [128, 4], F32)
    scr2 = P("scr2", [128, 64], F32)
    sumacc = P("sumacc", [128, 4], F32)
    ntb = P("ntb", [128, 2], BF16)
    rowscr = P("rowscr", [1, TOK], F32)
    scrA = P("scrA", [128, D], F32)
    scrB = P("scrB", [128, D], BF16)
    scrB2 = P("scrB2", [128, D], BF16)
    A = (o + 31) & ~31
    ARENA_SZ = SB_HI - A
    assert ARENA_SZ >= 116 * 1024, (A, ARENA_SZ)

    pr_pan2 = S("pr_pan2", [128, D], F32, A)
    pr_sgn2 = S("pr_sgn2", [128, D], BF16, A + 0x2000)
    pr_asm = S("pr_asm", [128, 16 * 512], BF16, A + 0x3000)
    ln_f = S("ln_f", [128, 16 * TOK], BF16, A)
    aa_q = S("aa_q", [128, 16 * TOK], BF16, A + 0x8000)
    aa_k = S("aa_k", [128, 4 * SEQ], BF16, A + 0x10000)
    aa_v = S("aa_v", [128, 16 * 512], BF16, A + 0x14000)
    aa_lnfo = S("aa_lnfo", [128, 16 * 128], BF16, A + 0x18000)
    aa_w = S("aa_w", [128, 16 * 512], BF16, A + 0x19000)
    aa_ctx = S("aa_ctx", [128, 16 * TOK], BF16, A)
    aa_mask = S("aa_mask", [128, 16 * 256], FP8, A + 0x18000)
    aa_exp = [S("aa_exp%d" % i, [128, 16 * 256], BF16, A + 0x19000 + i * 0x2000)
              for i in range(2)]
    aa_w2 = S("aa_w2", [128, 16 * 512], BF16, A + 0x8000)
    ff_qa = S("ff_qa", [128, 16 * TOK], BF16, A)
    ff_qh = S("ff_qh", [128, 16 * TOK], BF16, A + 0x8000)
    ff_w1 = S("ff_w1", [128, 16 * 512], BF16, A + 0x10000)
    ff_w2 = S("ff_w2", [128, 16 * 512], BF16, A + 0x14000)
    ff_w3 = [S("ff_w3_%d" % i, [128, 8 * 512], BF16, A + 0x18000 + i * 0x2000)
             for i in range(2)]
    ff_ln = S("ff_ln", [128, D], F32, A + 0x8000)
    ff_qi = S("ff_qi", [128, D], I32, A + 0xA000)
    ff_h = scrA[:, 0:512]
    ff_sl = scrA[:, 512:1024]
    ff_qih = scrA[:, 1024:1536].bitcast(I32)
    ff_qt = scrB2[:, 0:512]
    ff_qt2 = scrB2[:, 512:1024]
    ab_ln = S("ab_ln", [128, 16 * TOK], BF16, A)
    ab_q = S("ab_q", [128, 16 * TOK], BF16, A + 0x8000)
    ab_qr = S("ab_qr", [128, 8 * TOK], BF16, A + 0x10000)
    ab_w = S("ab_w", [128, 16 * 512], BF16, A + 0x14000)
    ab_cos = S("ab_cos", [128, 2 * TOK], F32, A + 0x18000)
    ab_sin = S("ab_sin", [128, 2 * TOK], F32, A + 0x1A000)
    ab_ctx = S("ab_ctx", [128, 16 * TOK], BF16, A)
    ab_kf = S("ab_kf", [128, SEQ], BF16, A + 0x10000)
    ab_kft = S("ab_kft", [128, SEQ], BF16, A + 0x11000)
    ab_qf = [S("ab_qf%d" % i, [128, TOK], BF16, A + 0x12000 + i * 0x800)
             for i in range(2)]
    ab_mem = [S("ab_mem%d" % i, [128, 128], BF16, A + 0x1B000 + i * 0x100)
              for i in range(2)]
    ab_exp = [S("ab_exp%d" % i, [128, 16 * 256], BF16, A + 0x13000 + i * 0x2000)
              for i in range(2)]
    ab_kv = [S("ab_kv%d" % i, [128, 2 * SEQ], BF16, A + 0x17000 + i * 0x2000)
             for i in range(2)]
    ab_w3 = S("ab_w3", [128, 16 * 512], BF16, A + 0x10000)

    psA = [nc.alloc_psum_tensor("psA%d" % i, [128, 512], F32) for i in range(4)]
    psT = nc.alloc_psum_tensor("psT", [128, 1024], BF16)
    psE = nc.alloc_psum_tensor("psE", [128, 512], F32)
    psD = [nc.alloc_psum_tensor("psD%d" % i, [1, 512], F32) for i in range(2)]

    bufs = {}
    def B(name):
        if name not in bufs:
            bufs[name] = Buf()
        return bufs[name]

    SY, GP, TE, VE, SC = "sync", "gpsimd", "tensor", "vector", "scalar"

    for t in range(NTT):
        sv = k.dma(SY, x_sb[:, t * D:(t + 1) * D], x_own[t * 128:(t + 1) * 128, :])
        B("x%d" % t).write(sv)
    sv = k.dma(GP, ident[:], ident_in)
    B("ident").write(sv)
    sv = k.op(VE, lambda e: e.memset(ones128[:], 1.0))
    B("ones").write(sv)
    sv = k.op(VE, lambda e: e.memset(ones128f[:], 1.0))
    B("ones").write(sv)
    sv = k.op(VE, lambda e: e.memset(ones1f[:], 1.0))
    B("ones").write(sv)
    k.op(VE, lambda e: e.memset(cst[:, 0:1], 1e-5))
    k.op(VE, lambda e: e.memset(cst[:, 1:2], -EXP_C))
    sv = k.op(VE, lambda e: e.memset(cst[:, 2:3], ISQ))
    B("cst").write(sv)
    sv = k.dma(SY, rowscr[0:1, 0:NH], gate)
    B("rowscr").write(sv)
    sv = k.op(SC, lambda e: e.activation(rowscr[0:1, 0:NH], rowscr[0:1, 0:NH], AF.Sigmoid),
              waits=B("rowscr").write())
    B("rowscr").write(sv)
    sv = k.op(TE, lambda e: e.matmul(psE[:, 0:NH], ones1f[:], rowscr[0:1, 0:NH],
                                     start=True, stop=True),
              waits=B("rowscr").read((k.sem[TE], k.sem[TE].n + 1))
              + B("ones").read() + B("psE").write())
    B("psE").write(sv)
    sv = k.op(VE, lambda e: e.tensor_copy(g_b[:], psE[:, 0:NH]),
              waits=B("psE").read((k.sem[VE], k.sem[VE].n + 1)))
    B("g_b").write(sv)
    for hh in range(2):
        sv = k.dma(SY, rowscr[0:1, :], bo_a[0:1, hh * TOK:(hh + 1) * TOK],
                   waits=B("rowscr").write())
        B("rowscr").write(sv)
        for i in range(2):
            sv = k.op(TE, lambda e, i=i: e.matmul(psA[i][:], ones1f[:],
                                                  rowscr[0:1, i * 512:(i + 1) * 512],
                                                  start=True, stop=True),
                      waits=B("rowscr").read((k.sem[TE], k.sem[TE].n + 1))
                      + B("ones").read() + B("psA%d" % i).write())
            B("psA%d" % i).write(sv)
            sv = k.op(VE, lambda e, i=i, hh=hh: e.tensor_copy(
                bo_b[:, hh * TOK + i * 512: hh * TOK + i * 512 + 512], psA[i][:]),
                      waits=B("psA%d" % i).read((k.sem[VE], k.sem[VE].n + 1)))
            B("bo_b").write(sv)
    k.barrier()

    sv = k.op(VE, lambda e: e.memset(sumacc[:], 0.0))
    B("sumacc").write(sv)
    selb = B("sel")
    sv = k.dma(SY, scr[:, 12:13], sel_in)
    selb.write(sv)
    for lb in range(8):
        for pi in range(4):
            u = lb * 4 + pi
            pan = scrA if u % 2 == 0 else pr_pan2
            sgn = scrB if u % 2 == 0 else pr_sgn2
            pb = B("prep_pan%d" % (u % 2))
            sb_ = B("prep_sgn%d" % (u % 2))
            lv = k.dma(SY, pan[:], prep_src[u], waits=pb.write())
            pb.write(lv)
            s1 = k.op(SC, lambda e, pan=pan, sgn=sgn: e.sign(sgn[:], pan[:]),
                      waits=pb.read((k.sem[SC], k.sem[SC].n + 1)) + sb_.write())
            v1 = k.op(VE, lambda e, pan=pan, sgn=sgn: e.tensor_sub(
                sgn[:], sgn[:], pan[:]),
                waits=[s1] + pb.read((k.sem[VE], k.sem[VE].n + 1)))
            v2 = k.op(VE, lambda e, pan=pan, sgn=sgn: e.scalar_tensor_tensor(
                sgn[:], sgn[:], scr[:, 12:13], pan[:], ALU.mult, ALU.add),
                waits=selb.read((k.sem[VE], k.sem[VE].n + 1))
                + pb.read((k.sem[VE], k.sem[VE].n + 1)))
            sb_.write(v2)
            rv = k.op(VE, lambda e, pan=pan: e.tensor_reduce(
                scr2[:, 63:64], pan[:], AX.X, ALU.add, apply_absolute_value=True),
                waits=pb.read((k.sem[VE], k.sem[VE].n + 1)) + B("scr2r").write())
            B("scr2r").write(rv)
            av = k.op(VE, lambda e: e.tensor_tensor(
                sumacc[:, 0:1], sumacc[:, 0:1], scr2[:, 63:64], ALU.add),
                waits=B("scr2r").read((k.sem[VE], k.sem[VE].n + 1)) + B("sumacc").write())
            B("sumacc").write(av)
            for rnd in range(2):
                for j in range(8):
                    dc = rnd * 8 + j
                    tv = k.op(TE, lambda e, sgn=sgn, dc=dc, j=j: e.transpose(
                        psT[:, j * 128:(j + 1) * 128], sgn[:, dc * 128:(dc + 1) * 128],
                        ident[:]),
                        waits=(sb_.read() + B("ident").read() + B("psT").write()
                               + B("asm").write())
                        if j == 0 else None, inc=(j == 7))
                if rnd == 1:
                    sb_.read((k.sem[TE], tv[1]))
                B("psT").write(tv)
                vv = k.op(VE, lambda e, pi=pi, rnd=rnd: e.tensor_copy(
                    pr_asm[:].rearrange("p (dc f j) -> p dc f j", dc=16, f=4)
                    [:, rnd * 8:(rnd + 1) * 8, pi, :],
                    psT[:].rearrange("p (dc j) -> p dc j", dc=8)),
                    waits=B("psT").read((k.sem[VE], k.sem[VE].n + 1)))
                B("asm").w.append(vv)
        wv = k.dma(SY, prep_in[lb * 1024:(lb + 1) * 1024, :]
                   .rearrange("(p a) j -> p (a j)", p=128),
                   pr_asm[:], waits=B("asm").write())
        B("asm").r.append(wv)
        B("asm").w = []
    tv = k.op(TE, lambda e: e.matmul(psD[0][0:1, 0:1], ones128f[:], sumacc[:, 0:1],
                                     start=True, stop=True),
              waits=B("sumacc").read() + B("ones").read() + B("psD0").write())
    B("psD0").write(tv)
    vv = k.op(VE, lambda e: e.tensor_copy(rowscr[0:1, 0:1], psD[0][0:1, 0:1]),
              waits=B("psD0").read((k.sem[VE], k.sem[VE].n + 1)) + B("rowscr").write())
    B("rowscr").write(vv)
    tw = k.dma(SY, prep_in[8192:8193, 0:2], rowscr[0:1, 0:1].bitcast(BF16),
               waits=B("rowscr").read((k.sem[SY], k.sem[SY].n + 16)))
    k.barrier()
    k.sem_cc.n += 1
    ccv0 = k.sem_cc.n

    def _ccp(eng, ctx, ccv0=ccv0):
        eng.collective_compute(
            "AllGather", ALU.bypass,
            replica_groups=[[0, 1, 2, 3, 4, 5, 6, 7]],
            ins=[prep_in], outs=[prep_all]).then_inc(k.sem_cc.h, 1)
    k.items["gpsimd"].append(_ccp)
    cc0_done = (k.sem_cc, ccv0)
    for r in range(6):
        lv = k.dma(SY, scrB[0:1, 16 + 2 * r:16 + 2 * r + 2],
                   prep_all[r * PRANK + 8192:r * PRANK + 8193, 0:2],
                   waits=[cc0_done] if r == 0 else None)
    ld_tails = (k.sem[SY], k.sem[SY].n)
    f32v = scrB[0:1, 16:28].bitcast(F32)
    v1 = k.op(VE, lambda e: e.tensor_add(
        rowscr[0:1, 0:3], f32v[0:1, 0:6:2], f32v[0:1, 1:6:2]),
        waits=[ld_tails] + B("rowscr").write())
    v2 = k.op(VE, lambda e: e.tensor_scalar(
        rowscr[0:1, 0:3], rowscr[0:1, 0:3], 1.0 / (DFF * D), None, ALU.mult),
        waits=[v1])
    B("rowscr").write(v2)
    tv = k.op(TE, lambda e: e.matmul(psE[:, 0:3], ones1f[:], rowscr[0:1, 0:3],
                                     start=True, stop=True),
              waits=B("rowscr").read((k.sem[TE], k.sem[TE].n + 1)) + B("psE").write())
    B("psE").write(tv)
    vv = k.op(VE, lambda e: e.tensor_copy(wsc[:, 0:3], psE[:, 0:3]),
              waits=B("psE").read((k.sem[VE], k.sem[VE].n + 1)))
    B("wsc").write(vv)
    k.barrier()

    def emit_ln(x_ap, x_buf, outs):
        stats = B("lnstats")
        vv = k.op(VE, lambda e: e.tensor_reduce(scr[:, 0:1], x_ap, AX.X, ALU.add),
                  waits=x_buf.read((k.sem[VE], k.sem[VE].n + 1)) + stats.write())
        sv = k.op(SC, lambda e: e.activation(scrB2[:], x_ap, AF.Square,
                                             accum_out=scr[:, 1:2]),
                  waits=x_buf.read((k.sem[SC], k.sem[SC].n + 1)) + B("scrB2").write())
        B("scrB2").write(sv)
        v2 = k.op(VE, lambda e: e.tensor_scalar(scr[:, 2:3], scr[:, 0:1], 1.0 / D,
                                                None, ALU.mult),
                  waits=[vv])
        v3 = k.op(VE, lambda e: e.tensor_scalar(scr[:, 3:4], scr[:, 1:2], 1.0 / D,
                                                None, ALU.mult),
                  waits=[sv])
        v4a = k.op(VE, lambda e: e.tensor_mul(scr[:, 4:5], scr[:, 2:3], scr[:, 2:3]),
                   waits=[v2])
        v4 = k.op(VE, lambda e: e.tensor_sub(scr[:, 8:9], scr[:, 3:4], scr[:, 4:5]),
                  waits=[v3, v4a])
        s2 = k.op(SC, lambda e: e.activation(scr[:, 5:6], scr[:, 8:9], AF.Sqrt,
                                             bias=cst[:, 0:1]),
                  waits=[v4] + B("cst").read())
        v5 = k.op(VE, lambda e: e.reciprocal(scr[:, 6:7], scr[:, 5:6]), waits=[s2])
        v6 = k.op(VE, lambda e: e.scalar_tensor_tensor(
            scr[:, 7:8], scr[:, 2:3], -1.0, scr[:, 6:7], ALU.mult, ALU.mult), waits=[v5])
        last = []
        for (ap, obuf) in outs:
            ov = k.op(SC, lambda e, ap=ap: e.activation(ap, x_ap, AF.Identity,
                                                        bias=scr[:, 7:8], scale=scr[:, 6:7]),
                      waits=[v6] + x_buf.read((k.sem[SC], k.sem[SC].n + 1))
                      + (obuf.write() if obuf else []))
            if obuf:
                obuf.write(ov)
            stats.r = [ov]
            last.append(ov)
        return last[-1]

    def transpose16(src_fn, dst_view_fn, src_waits, dst_buf):
        for rnd in range(2):
            for j in range(8):
                dc = rnd * 8 + j
                tv = k.op(TE, lambda e, dc=dc, j=j: e.transpose(
                    psT[:, j * 128:(j + 1) * 128], src_fn(dc), ident[:]),
                    waits=(src_waits + B("ident").read() + B("psT").write()) if j == 0
                    else None, inc=(j == 7))
            B("psT").write(tv)
            vv = k.op(VE, lambda e, rnd=rnd: e.tensor_copy(
                dst_view_fn(rnd), psT[:].rearrange("p (dc j) -> p dc j", dc=8)),
                waits=B("psT").read((k.sem[VE], k.sem[VE].n + 1)) + dst_buf.write())
            dst_buf.w.append(vv)
        return vv


    def emit_attn_a2():
        lnf_b = B("ln_f")
        for t in range(NTT):
            emit_ln(x_sb[:, t * D:(t + 1) * D], B("x%d" % t), [(scrB[:], B("scrB"))])
            transpose16(lambda dc: scrB[:, dc * 128:(dc + 1) * 128],
                        lambda rnd, t=t: ln_f[:].rearrange("p (dc t) -> p dc t", dc=16)
                        [:, rnd * 8:(rnd + 1) * 8, t * 128:(t + 1) * 128],
                        B("scrB").read(), lnf_b)
            B("scrB").r.append((k.sem[VE], k.sem[VE].n))
        kbuf, vbuf, wbuf = B("aa_k"), B("aa_v"), B("aa_w")
        lnfo_b = B("aa_lnfo")
        xo_b = B("xo")

        wld = {}

        def load_wcol(dram_w, ocb):
            ws = wbuf.write()
            for dc in range(16):
                lv = k.dma(GP, aa_w[:, dc * 512:(dc + 1) * 512],
                           dram_w[dc * 128:(dc + 1) * 128, ocb * 512:(ocb + 1) * 512],
                           waits=ws if dc == 0 else None)
                wld[dc] = lv
            wbuf.write(lv)
            wbuf.w = [wld[15]]

        load_wcol(wk_a, 0)
        for oc in range(4):
            for ts in range(2):
                pa = B("psA%d" % (2 * (oc % 2) + ts))
                for dc in range(16):
                    tv = k.op(TE, lambda e, oc=oc, ts=ts, dc=dc: e.matmul(
                        psA[2 * (oc % 2) + ts][:],
                        aa_w[:, dc * 512 + oc * 128: dc * 512 + oc * 128 + 128],
                        ln_f[:, dc * TOK + ts * 512: dc * TOK + ts * 512 + 512],
                        start=(dc == 0), stop=(dc == 15)),
                        waits=([wld[dc]] + (wbuf.read() + lnf_b.read() + pa.write()
                               if dc == 0 else [])), inc=(dc == 15))
                wbuf.r.append(tv); lnf_b.r.append(tv)
                pa.write(tv)
                vv = k.op(VE, lambda e, oc=oc, ts=ts: e.tensor_copy(
                    aa_k[:, oc * SEQ + ts * 512: oc * SEQ + ts * 512 + 512],
                    psA[2 * (oc % 2) + ts][:]),
                    waits=pa.read((k.sem[VE], k.sem[VE].n + 1)) + kbuf.write())
                kbuf.w.append(vv)
        for t in range(NTT):
            lv = k.dma(SY, scrA[:], x_oth[t * 128:(t + 1) * 128, :], waits=xo_b.write())
            xo_b.write(lv)
            emit_ln(scrA[:], xo_b, [(scrB[:], B("scrB"))])
            transpose16(lambda dc: scrB[:, dc * 128:(dc + 1) * 128],
                        lambda rnd: aa_lnfo[:].rearrange("p (dc t) -> p dc t", dc=16)
                        [:, rnd * 8:(rnd + 1) * 8, :],
                        B("scrB").read(), lnfo_b)
            B("scrB").r.append((k.sem[VE], k.sem[VE].n))
            for oc in range(4):
                pa = B("psA%d" % (oc % 2))
                for dc in range(16):
                    tv = k.op(TE, lambda e, oc=oc, dc=dc: e.matmul(
                        psA[oc % 2][:, 0:128],
                        aa_w[:, dc * 512 + oc * 128: dc * 512 + oc * 128 + 128],
                        aa_lnfo[:, dc * 128:(dc + 1) * 128],
                        start=(dc == 0), stop=(dc == 15)),
                        waits=(lnfo_b.read() + pa.write()) if dc == 0 else None,
                        inc=(dc == 15))
                lnfo_b.r.append(tv)
                wbuf.r.append(tv)
                pa.write(tv)
                vv = k.op(VE, lambda e, oc=oc, t=t: e.tensor_copy(
                    aa_k[:, oc * SEQ + TOK + t * 128: oc * SEQ + TOK + t * 128 + 128],
                    psA[oc % 2][:, 0:128]),
                    waits=pa.read((k.sem[VE], k.sem[VE].n + 1)) + kbuf.write())
                kbuf.w.append(vv)
        load_wcol(wv_a, 0)
        for tt in range(NTT):
            pa = B("psA%d" % (tt % 4))
            for dc in range(16):
                tv = k.op(TE, lambda e, tt=tt, dc=dc: e.matmul(
                    psA[tt % 4][:],
                    ln_f[:, dc * TOK + tt * 128: dc * TOK + tt * 128 + 128],
                    aa_w[:, dc * 512:(dc + 1) * 512],
                    start=(dc == 0), stop=(dc == 15)),
                    waits=(wbuf.read() + lnf_b.read() + pa.write()) if dc == 0 else None,
                    inc=(dc == 15))
            wbuf.r.append(tv); lnf_b.r.append(tv)
            pa.write(tv)
            vv = k.op(VE, lambda e, tt=tt: e.tensor_copy(
                aa_v[:, tt * 512:(tt + 1) * 512], psA[tt % 4][:]),
                waits=pa.read((k.sem[VE], k.sem[VE].n + 1)) + vbuf.write())
            vbuf.w.append(vv)
        for t in range(NTT):
            lv = k.dma(SY, scrA[:], x_oth[t * 128:(t + 1) * 128, :], waits=xo_b.write())
            xo_b.write(lv)
            emit_ln(scrA[:], xo_b, [(scrB[:], B("scrB"))])
            transpose16(lambda dc: scrB[:, dc * 128:(dc + 1) * 128],
                        lambda rnd: aa_lnfo[:].rearrange("p (dc t) -> p dc t", dc=16)
                        [:, rnd * 8:(rnd + 1) * 8, :],
                        B("scrB").read(), lnfo_b)
            B("scrB").r.append((k.sem[VE], k.sem[VE].n))
            pa = B("psA%d" % (t % 2))
            for dc in range(16):
                tv = k.op(TE, lambda e, t=t, dc=dc: e.matmul(
                    psA[t % 2][:],
                    aa_lnfo[:, dc * 128:(dc + 1) * 128],
                    aa_w[:, dc * 512:(dc + 1) * 512],
                    start=(dc == 0), stop=(dc == 15)),
                    waits=(lnfo_b.read() + pa.write()) if dc == 0 else None,
                    inc=(dc == 15))
            lnfo_b.r.append(tv)
            wbuf.r.append(tv)
            pa.write(tv)
            vv = k.op(VE, lambda e, t=t: e.tensor_copy(
                aa_v[:, (NTT + t) * 512:(NTT + t + 1) * 512], psA[t % 2][:]),
                waits=pa.read((k.sem[VE], k.sem[VE].n + 1)) + vbuf.write())
            vbuf.w.append(vv)
        qb = B("aa_q")
        for ocb in range(4):
            load_wcol(wq_a, ocb)
            for oc4 in range(4):
                for ts in range(2):
                    pa = B("psA%d" % (2 * (oc4 % 2) + ts))
                    for dc in range(16):
                        tv = k.op(TE, lambda e, oc4=oc4, ts=ts, dc=dc: e.matmul(
                            psA[2 * (oc4 % 2) + ts][:],
                            aa_w[:, dc * 512 + oc4 * 128: dc * 512 + oc4 * 128 + 128],
                            ln_f[:, dc * TOK + ts * 512: dc * TOK + ts * 512 + 512],
                            start=(dc == 0), stop=(dc == 15)),
                            waits=([wld[dc]] + (wbuf.read() + lnf_b.read()
                                   + pa.write() if dc == 0 else [])), inc=(dc == 15))
                    wbuf.r.append(tv); lnf_b.r.append(tv)
                    pa.write(tv)
                    oc = ocb * 4 + oc4
                    vv = k.op(VE, lambda e, oc=oc, ts=ts, oc4=oc4: e.tensor_copy(
                        aa_q[:, oc * TOK + ts * 512: oc * TOK + ts * 512 + 512],
                        psA[2 * (oc4 % 2) + ts][:]),
                        waits=pa.read((k.sem[VE], k.sem[VE].n + 1)) + qb.write())
                    qb.w.append(vv)

        eb = [B("aa_exp0"), B("aa_exp1")]
        cb = B("aa_ctx")
        proj_gate = [(k.sem[TE], k.sem[TE].n), (k.sem[VE], k.sem[VE].n)]
        mb = B("aa_mask")
        cur_mask_qq = [-1]

        def load_mask(qq):
            if cur_mask_qq[0] != qq:
                lv = k.dma(SY, aa_mask[:].rearrange("p (c t) -> p c t", c=16),
                           maskT_in.rearrange("(c p) t -> p c t", p=128)
                           [:, :, qq * 256:(qq + 1) * 256],
                           waits=mb.write() + (proj_gate if qq == 0 else []))
                mb.write(lv)
                cur_mask_qq[0] = qq

        def emit_scores(i):
            qq, h = divmod(i, NH)
            g = h // 4
            p = i % 2
            load_mask(qq)
            q_ap = aa_q[:, h * TOK + qq * 256: h * TOK + qq * 256 + 256]
            exp_t = aa_exp[0] if p == 0 else aa_exp[1]
            for kt in range(16):
                pa = B("psA%d" % (kt % 2))
                tv = k.op(TE, lambda e, g=g, kt=kt, q_ap=q_ap: e.matmul(
                    psA[kt % 2][:, 0:256],
                    aa_k[:, g * SEQ + kt * 128: g * SEQ + kt * 128 + 128],
                    q_ap, start=True, stop=True),
                    waits=pa.write() + (kbuf.read() + qb.read() if i == 0 and kt == 0
                                        else []))
                pa.write(tv)
                sv = k.op(SC, lambda e, kt=kt, exp_t=exp_t: e.activation(
                    exp_t[:, kt * 256:(kt + 1) * 256], psA[kt % 2][:, 0:256],
                    AF.Exp, bias=cst[:, 1:2], scale=cst[:, 2:3]),
                    waits=pa.read((k.sem[SC], k.sem[SC].n + 1)) + eb[p].write()
                    + B("cst").read())
                eb[p].w.append(sv)
            if i == 0:
                kbuf.r.append(tv); qb.r.append(tv)
            mv = k.op(VE, lambda e, exp_t=exp_t: e.tensor_tensor(
                exp_t[:], exp_t[:], aa_mask[:], ALU.mult),
                waits=eb[p].write() + mb.read((k.sem[VE], k.sem[VE].n + 1)))
            eb[p].write(mv)

        def emit_tail(i):
            qq, h = divmod(i, NH)
            g = h // 4
            p = i % 2
            exp_t = aa_exp[0] if p == 0 else aa_exp[1]
            pd = B("psD0")
            for kt in range(16):
                tv = k.op(TE, lambda e, kt=kt, exp_t=exp_t: e.matmul(
                    psD[0][:, 0:256], ones128[:], exp_t[:, kt * 256:(kt + 1) * 256],
                    start=(kt == 0), stop=(kt == 15)),
                    waits=(eb[p].read() + pd.write() + B("ones").read())
                    if kt == 0 else None, inc=(kt == 15))
            eb[p].r.append(tv)
            pd.write(tv)
            vr = k.op(VE, lambda e: e.reciprocal(rowscr[0:1, 0:256], psD[0][0:1, 0:256]),
                      waits=pd.read((k.sem[VE], k.sem[VE].n + 1)) + B("rowscr").write())
            B("rowscr").write(vr)
            pc = B("psA2")
            for kt in range(16):
                tv = k.op(TE, lambda e, kt=kt, g=g, exp_t=exp_t: e.matmul(
                    psA[2][:, 0:256],
                    aa_v[:, kt * 512 + g * 128: kt * 512 + g * 128 + 128],
                    exp_t[:, kt * 256:(kt + 1) * 256],
                    start=(kt == 0), stop=(kt == 15)),
                    waits=(eb[p].read() + pc.write() + vbuf.read()) if kt == 0 else None,
                    inc=(kt == 15))
            eb[p].r.append(tv); vbuf.r.append(tv)
            pc.write(tv)
            pe = B("psE")
            tb = k.op(TE, lambda e: e.matmul(psE[:, 0:256], ones1f[:],
                                             rowscr[0:1, 0:256], start=True, stop=True),
                      waits=B("rowscr").read((k.sem[TE], k.sem[TE].n + 1)) + pe.write())
            pe.write(tb)
            sre = k.op(SC, lambda e: e.copy(scrA[:, 0:256], psE[:, 0:256]),
                       waits=pe.read((k.sem[SC], k.sem[SC].n + 1)) + B("scrAh").write())
            B("scrAh").write(sre)
            vv = k.op(VE, lambda e, h=h, qq=qq: e.tensor_tensor(
                aa_ctx[:, h * TOK + qq * 256: h * TOK + qq * 256 + 256],
                psA[2][:, 0:256], scrA[:, 0:256], ALU.mult),
                waits=pc.read((k.sem[VE], k.sem[VE].n + 1))
                + B("scrAh").read((k.sem[VE], k.sem[VE].n + 1)) + cb.write())
            cb.w.append(vv)

        emit_scores(0)
        for i in range(4 * NH):
            if i + 1 < 4 * NH:
                emit_scores(i + 1)
            emit_tail(i)

        for t in range(NTT):
            xb = B("x%d" % t)
            vv = k.op(VE, lambda e, t=t: e.tensor_add(
                x_sb[:, t * D:(t + 1) * D], x_sb[:, t * D:(t + 1) * D], bo_b[:]),
                waits=xb.write() + B("bo_b").read())
            xb.write(vv)
        w2b = B("aa_w2")
        head_gate = [(k.sem[TE], k.sem[TE].n), (k.sem[SC], k.sem[SC].n)]
        wld2 = {}
        for ocb in range(4):
            ws = w2b.write() + (head_gate if ocb == 0 else [])
            for dc in range(16):
                lv = k.dma(GP, aa_w2[:, dc * 512:(dc + 1) * 512],
                           wo_a[dc * 128:(dc + 1) * 128, ocb * 512:(ocb + 1) * 512],
                           waits=ws if dc == 0 else None)
                wld2[dc] = lv
            w2b.write(lv)
            w2b.w = [wld2[15]]
            for tt in range(NTT):
                pa = B("psA%d" % (tt % 4))
                for dc in range(16):
                    tv = k.op(TE, lambda e, tt=tt, dc=dc: e.matmul(
                        psA[tt % 4][:],
                        aa_ctx[:, dc * TOK + tt * 128: dc * TOK + tt * 128 + 128],
                        aa_w2[:, dc * 512:(dc + 1) * 512],
                        start=(dc == 0), stop=(dc == 15)),
                        waits=([wld2[dc]] + (w2b.read() + cb.read() + pa.write()
                               if dc == 0 else [])), inc=(dc == 15))
                w2b.r.append(tv); cb.r.append(tv)
                pa.write(tv)
                xb = B("x%d" % tt)
                vv = k.op(VE, lambda e, tt=tt, ocb=ocb: e.tensor_add(
                    x_sb[:, tt * D + ocb * 512: tt * D + ocb * 512 + 512],
                    x_sb[:, tt * D + ocb * 512: tt * D + ocb * 512 + 512],
                    psA[tt % 4][:]),
                    waits=pa.read((k.sem[VE], k.sem[VE].n + 1)) + xb.write())
                xb.write(vv)

    def emit_ffn():
        qab, qhb = B("ff_qa"), B("ff_qh")
        w1b, w2b_ = B("ff_w1"), B("ff_w2")
        w3b = [B("ff_w30"), B("ff_w31")]
        pend_q = []

        def emit_qh_transpose(si, tl, qt, qtb):
            for i in range(4):
                tv = k.op(TE, lambda e, i=i, qt=qt: e.transpose(
                    psT[:, i * 128:(i + 1) * 128],
                    qt[:, i * 128:(i + 1) * 128], ident[:]),
                    waits=(qtb.read() + B("psT").write()) if i == 0 else None,
                    inc=(i == 3))
            qtb.r.append((k.sem[TE], tv[1]))
            B("psT").write(tv)
            vq = k.op(VE, lambda e, si=si, tl=tl: e.tensor_copy(
                ff_qh[:].rearrange("p (c t) -> p c t", c=16)
                [:, 4 * si:4 * si + 4, tl * 128:(tl + 1) * 128],
                psT[:, 0:512].rearrange("p (c t) -> p c t", c=4)),
                waits=B("psT").read((k.sem[VE], k.sem[VE].n + 1)) + qhb.write())
            qhb.w.append(vq)
        for t in range(NTT):
            emit_ln(x_sb[:, t * D:(t + 1) * D], B("x%d" % t),
                    [(ff_ln[:], B("ff_ln"))])
            flb = B("ff_ln")
            v1 = k.op(VE, lambda e: e.tensor_reduce(
                scr2[:, 0:32], ff_ln[:].rearrange("p (g c) -> p g c", g=32),
                AX.X, ALU.max, apply_absolute_value=True),
                waits=flb.read((k.sem[VE], k.sem[VE].n + 1)) + B("qs").write())
            v2 = k.op(VE, lambda e: e.tensor_scalar(
                scr2[:, 32:64], scr2[:, 0:32], 1e-5, 1.0 / 127.0, ALU.max, ALU.mult),
                waits=[v1])
            v3 = k.op(VE, lambda e: e.reciprocal(scr2[:, 0:32], scr2[:, 32:64]),
                      waits=[v2])
            B("qs").write(v3)
            qib = B("ff_qi")
            v4 = k.op(VE, lambda e: e.tensor_tensor(
                ff_qi[:].rearrange("p (g c) -> p g c", g=32),
                ff_ln[:].rearrange("p (g c) -> p g c", g=32),
                scr2[:, 0:32].rearrange("p (g u) -> p g u", u=1)
                .broadcast_to([128, 32, 64]), ALU.mult),
                waits=[v3] + flb.read((k.sem[VE], k.sem[VE].n + 1)) + qib.write())
            qib.write(v4)
            v5 = k.op(VE, lambda e: e.tensor_tensor(
                scrB[:].rearrange("p (g c) -> p g c", g=32),
                ff_qi[:].rearrange("p (g c) -> p g c", g=32),
                scr2[:, 32:64].rearrange("p (g u) -> p g u", u=1)
                .broadcast_to([128, 32, 64]), ALU.mult),
                waits=[v4] + B("scrB").write() + B("qs").read((k.sem[VE], k.sem[VE].n + 1)))
            B("scrB").write(v5)
            transpose16(lambda dc: scrB[:, dc * 128:(dc + 1) * 128],
                        lambda rnd, t=t: ff_qa[:].rearrange("p (dc t) -> p dc t", dc=16)
                        [:, rnd * 8:(rnd + 1) * 8, t * 128:(t + 1) * 128],
                        B("scrB").read(), qab)
            B("scrB").r.append((k.sem[VE], k.sem[VE].n))

        for g in range(4):
            for si in range(4):
                sl = g * 4 + si
                w1q = {}
                ws = w1b.write()
                for qq_ in range(4):
                    l1 = k.dma(SY, ff_w1[:, qq_ * 2048:(qq_ + 1) * 2048],
                               w1T[sl][:, qq_ * 2048:(qq_ + 1) * 2048],
                               waits=ws if qq_ == 0 else None)
                    w1q[qq_] = l1
                w1b.write(l1)
                w1b.w = [w1q[3]]
                w2q = {}
                l2 = None
                for tl in range(NTT):
                    pa = B("psA%d" % (tl % 2))
                    pb2 = B("psA%d" % (2 + tl % 2))
                    for dc in range(16):
                        tv = k.op(TE, lambda e, tl=tl, dc=dc: e.matmul(
                            psA[tl % 2][:],
                            ff_qa[:, dc * TOK + tl * 128: dc * TOK + tl * 128 + 128],
                            ff_w1[:, dc * 512:(dc + 1) * 512],
                            start=(dc == 0), stop=(dc == 15)),
                            waits=([w1q[dc // 4]] + (w1b.read() + qab.read()
                                   + pa.write() if dc == 0 else [])), inc=(dc == 15))
                    w1b.r.append(tv); qab.r.append(tv)
                    pa.write(tv)
                    if tl == 0:
                        ws = w2b_.write()
                        for qq_ in range(4):
                            l2 = k.dma(SY, ff_w2[:, qq_ * 2048:(qq_ + 1) * 2048],
                                       w2T[sl][:, qq_ * 2048:(qq_ + 1) * 2048],
                                       waits=ws if qq_ == 0 else None)
                            w2q[qq_] = l2
                        w2b_.write(l2)
                        w2b_.w = [w2q[3]]
                    for dc in range(16):
                        tv2 = k.op(TE, lambda e, tl=tl, dc=dc: e.matmul(
                            psA[2 + tl % 2][:],
                            ff_qa[:, dc * TOK + tl * 128: dc * TOK + tl * 128 + 128],
                            ff_w2[:, dc * 512:(dc + 1) * 512],
                            start=(dc == 0), stop=(dc == 15)),
                            waits=([w2q[dc // 4]] + (w2b_.read() + pb2.write()
                                   if dc == 0 else [])), inc=(dc == 15))
                    w2b_.r.append(tv2)
                    pb2.write(tv2)
                    sh = k.op(SC, lambda e, tl=tl: e.activation(
                        ff_sl[:], psA[tl % 2][:], AF.Silu, scale=wsc[:, 0:1]),
                        waits=pa.read((k.sem[SC], k.sem[SC].n + 1)) + B("ff_sl").write()
                        + B("wsc").read())
                    B("ff_sl").write(sh)
                    vh = k.op(VE, lambda e, tl=tl: e.scalar_tensor_tensor(
                        ff_h[:], psA[2 + tl % 2][:], wsc[:, 1:2], ff_sl[:],
                        ALU.mult, ALU.mult),
                        waits=pb2.read((k.sem[VE], k.sem[VE].n + 1))
                        + B("ff_sl").read((k.sem[VE], k.sem[VE].n + 1))
                        + B("ff_h").write() + B("wsc").read())
                    B("ff_h").write(vh)
                    hv1 = k.op(VE, lambda e: e.tensor_reduce(
                        scr2[:, 0:8], ff_h[:].rearrange("p (gg c) -> p gg c", gg=8),
                        AX.X, ALU.max, apply_absolute_value=True),
                        waits=B("ff_h").read((k.sem[VE], k.sem[VE].n + 1))
                        + B("qs2").write())
                    hv2 = k.op(VE, lambda e: e.tensor_scalar(
                        scr2[:, 8:16], scr2[:, 0:8], 1e-5, 1.0 / 127.0,
                        ALU.max, ALU.mult), waits=[hv1])
                    hv3 = k.op(VE, lambda e: e.reciprocal(scr2[:, 0:8], scr2[:, 8:16]),
                               waits=[hv2])
                    B("qs2").write(hv3)
                    qhib = B("ff_qih")
                    hv4 = k.op(VE, lambda e: e.tensor_tensor(
                        ff_qih[:].rearrange("p (gg c) -> p gg c", gg=8),
                        ff_h[:].rearrange("p (gg c) -> p gg c", gg=8),
                        scr2[:, 0:8].rearrange("p (gg u) -> p gg u", u=1)
                        .broadcast_to([128, 8, 64]), ALU.mult),
                        waits=[hv3] + B("ff_h").read((k.sem[VE], k.sem[VE].n + 1))
                        + qhib.write())
                    qhib.write(hv4)
                    qt = ff_qt if (tl % 2 == 0) else ff_qt2
                    qtb = B("ff_qt%d" % (tl % 2))
                    hv5 = k.op(VE, lambda e, qt=qt: e.tensor_tensor(
                        qt[:].rearrange("p (gg c) -> p gg c", gg=8),
                        ff_qih[:].rearrange("p (gg c) -> p gg c", gg=8),
                        scr2[:, 8:16].rearrange("p (gg u) -> p gg u", u=1)
                        .broadcast_to([128, 8, 64]), ALU.mult),
                        waits=[hv4] + qtb.write()
                        + B("qs2").read((k.sem[VE], k.sem[VE].n + 1)))
                    qtb.write(hv5)
                    pend_q.append((si, tl, qt, qtb))
                    if len(pend_q) > 1:
                        emit_qh_transpose(*pend_q.pop(0))
            while pend_q:
                emit_qh_transpose(*pend_q.pop(0))
            for dms in range(4):
                lv0 = k.dma(SY, ff_w3[0][:], w3T[dms][g][:, 0:4096],
                            waits=w3b[0].write())
                w3b[0].write(lv0)
                lv1 = k.dma(SY, ff_w3[1][:], w3T[dms][g][:, 4096:8192],
                            waits=w3b[1].write())
                w3b[1].write(lv1)
                for tl in range(NTT):
                    pa = B("psA%d" % (tl % 4))
                    for c in range(16):
                        hh = c // 8
                        tv = k.op(TE, lambda e, tl=tl, c=c, hh=hh: e.matmul(
                            psA[tl % 4][:],
                            ff_qh[:, c * TOK + tl * 128: c * TOK + tl * 128 + 128],
                            ff_w3[hh][:, (c % 8) * 512:(c % 8 + 1) * 512],
                            start=(c == 0), stop=(c == 15)),
                            waits=(w3b[hh].read() + qhb.read() + pa.write())
                            if c in (0, 8) else None, inc=(c in (7, 15)))
                        if c in (7, 15):
                            w3b[hh].r.append(tv)
                    qhb.r.append(tv)
                    pa.write(tv)
                    xb = B("x%d" % tl)
                    vv = k.op(VE, lambda e, tl=tl, dms=dms: e.scalar_tensor_tensor(
                        x_sb[:, tl * D + dms * 512: tl * D + dms * 512 + 512],
                        psA[tl % 4][:], wsc[:, 2:3],
                        x_sb[:, tl * D + dms * 512: tl * D + dms * 512 + 512],
                        ALU.mult, ALU.add),
                        waits=pa.read((k.sem[VE], k.sem[VE].n + 1)) + xb.write()
                        + B("wsc").read())
                    xb.write(vv)

    def emit_attn_b():
        lnb = B("ab_ln")
        for t in range(NTT):
            emit_ln(x_sb[:, t * D:(t + 1) * D], B("x%d" % t), [(scrB[:], B("scrB"))])
            transpose16(lambda dc: scrB[:, dc * 128:(dc + 1) * 128],
                        lambda rnd, t=t: ab_ln[:].rearrange("p (dc t) -> p dc t", dc=16)
                        [:, rnd * 8:(rnd + 1) * 8, t * 128:(t + 1) * 128],
                        B("scrB").read(), lnb)
            B("scrB").r.append((k.sem[VE], k.sem[VE].n))

        wb = B("ab_w")
        def proj_block(wblk, ocb, out_fn, out_buf):
            ws = wb.write()
            bq = {}
            for qq_ in range(4):
                lv = k.dma(SY, ab_w[:, qq_ * 2048:(qq_ + 1) * 2048],
                           wblk[ocb][:, qq_ * 2048:(qq_ + 1) * 2048],
                           waits=ws if qq_ == 0 else None)
                bq[qq_] = lv
            wb.write(lv)
            wb.w = [bq[3]]
            for oc4 in range(4):
                for ts in range(2):
                    pa = B("psA%d" % (2 * (oc4 % 2) + ts))
                    for dc in range(16):
                        tv = k.op(TE, lambda e, oc4=oc4, ts=ts, dc=dc: e.matmul(
                            psA[2 * (oc4 % 2) + ts][:],
                            ab_w[:, dc * 512 + oc4 * 128: dc * 512 + oc4 * 128 + 128],
                            ab_ln[:, dc * TOK + ts * 512: dc * TOK + ts * 512 + 512],
                            start=(dc == 0), stop=(dc == 15)),
                            waits=([bq[dc // 4]] + (wb.read() + lnb.read() + pa.write()
                                   if dc == 0 else [])), inc=(dc == 15))
                    wb.r.append(tv); lnb.r.append(tv)
                    pa.write(tv)
                    vv = k.op(VE, lambda e, oc4=oc4, ts=ts: e.tensor_copy(
                        out_fn(ocb * 4 + oc4, ts), psA[2 * (oc4 % 2) + ts][:]),
                        waits=pa.read((k.sem[VE], k.sem[VE].n + 1)) + out_buf.write())
                    out_buf.w.append(vv)

        cosb, sinb = B("ab_cos"), B("ab_sin")
        def rope_pair(raw, ri, rj, c, cp, dst_fn, dst_buf, rb_):
            lc1 = k.dma(SY, ab_cos[:, 0:TOK], cos_in[c * 128:(c + 1) * 128, :],
                        waits=cosb.write())
            lc2 = k.dma(SY, ab_cos[:, TOK:2 * TOK], cos_in[cp * 128:(cp + 1) * 128, :])
            cosb.write(lc2)
            ls1 = k.dma(SY, ab_sin[:, 0:TOK], sin_in[c * 128:(c + 1) * 128, :],
                        waits=sinb.write())
            ls2 = k.dma(SY, ab_sin[:, TOK:2 * TOK], sin_in[cp * 128:(cp + 1) * 128, :])
            sinb.write(ls2)
            for (cc, half, oth, sgn) in ((c, 0, rj, -1.0), (cp, 1, ri, 1.0)):
                me = ri if cc == c else rj
                v1 = k.op(VE, lambda e, me=me, half=half: e.tensor_tensor(
                    scrA[:, 0:TOK], raw[:, me * TOK:(me + 1) * TOK],
                    ab_cos[:, half * TOK:(half + 1) * TOK], ALU.mult),
                    waits=cosb.read((k.sem[VE], k.sem[VE].n + 1))
                    + rb_.read((k.sem[VE], k.sem[VE].n + 1)) + B("scrAr").write())
                v2 = k.op(VE, lambda e, oth=oth, half=half: e.tensor_tensor(
                    scrA[:, TOK:2 * TOK], raw[:, oth * TOK:(oth + 1) * TOK],
                    ab_sin[:, half * TOK:(half + 1) * TOK], ALU.mult),
                    waits=sinb.read((k.sem[VE], k.sem[VE].n + 1))
                    + rb_.read((k.sem[VE], k.sem[VE].n + 1)))
                v3 = k.op(VE, lambda e, cc=cc, sgn=sgn: e.scalar_tensor_tensor(
                    dst_fn(cc), scrA[:, TOK:2 * TOK], sgn, scrA[:, 0:TOK],
                    ALU.mult, ALU.add),
                    waits=[v1, v2] + dst_buf.write())
                B("scrAr").write(v3)
                dst_buf.w.append(v3)

        qb = B("ab_q")
        qrb = B("ab_qr")
        kst = B("kstage")

        def proj_half(wblk, half):
            for i, ocb in enumerate((0, 2) if half == 0 else (1, 3)):
                proj_block(wblk, ocb,
                           lambda oc, ts, i=i: ab_qr[:, (i * 4 + oc % 4) * TOK + ts * 512:
                                                     (i * 4 + oc % 4) * TOK + ts * 512 + 512],
                           qrb)

        def rope_q(half):
            for j in range(4):
                c = half * 4 + j
                rope_pair(ab_qr, j, 4 + j, c, c + 8,
                          lambda cc: ab_q[:, cc * TOK:(cc + 1) * TOK], qb, qrb)
            qrb.r = qb.w[-8:]
            qrb.w = []

        def rope_k(half):
            for j in range(4):
                c = half * 4 + j
                cp = c + 8

                def kdst(cc, c=c):
                    return (scrB[:, 0:TOK] if cc == c else scrB2[:, 0:TOK])
                rope_pair(ab_qr, j, 4 + j, c, cp, kdst, kst, qrb)
                d1 = k.dma(SY, kv_pack[c * 128:(c + 1) * 128, :], scrB[:, 0:TOK],
                           waits=kst.write())
                d2 = k.dma(SY, kv_pack[cp * 128:(cp + 1) * 128, :], scrB2[:, 0:TOK])
                kst.r = [d1, d2]
                kst.w = []
                B("scrB").r.append(d1)
                B("scrB2").r.append(d2)
            qrb.r.append((k.sem[SY], k.sem[SY].n))
            qrb.w = []

        def vproj(ocb):
            ws = wb.write()
            bq = {}
            for qq_ in range(4):
                lv = k.dma(SY, ab_w[:, qq_ * 2048:(qq_ + 1) * 2048],
                           wvbT[ocb][:, qq_ * 2048:(qq_ + 1) * 2048],
                           waits=ws if qq_ == 0 else None)
                bq[qq_] = lv
            wb.write(lv)
            wb.w = [bq[3]]
            for tt in range(NTT):
                pa = B("psA%d" % (tt % 4))
                for dc in range(16):
                    tv = k.op(TE, lambda e, tt=tt, dc=dc: e.matmul(
                        psA[tt % 4][:],
                        ab_ln[:, dc * TOK + tt * 128: dc * TOK + tt * 128 + 128],
                        ab_w[:, dc * 512:(dc + 1) * 512],
                        start=(dc == 0), stop=(dc == 15)),
                        waits=([bq[dc // 4]] + (wb.read() + lnb.read() + pa.write()
                               if dc == 0 else [])), inc=(dc == 15))
                wb.r.append(tv); lnb.r.append(tv)
                pa.write(tv)
                vslot = scrB[:, 1536:2048] if tt % 2 == 0 else scrB2[:, 1536:2048]
                vsb = B("vstage%d" % (tt % 2))
                sv = k.op(SC, lambda e, tt=tt, vslot=vslot: e.copy(vslot, psA[tt % 4][:]),
                          waits=pa.read((k.sem[SC], k.sem[SC].n + 1)) + vsb.write())
                vsb.write(sv)
                r0 = D + tt * 256 + ocb // 2
                dv = k.dma(SY,
                           kv_pack[r0:r0 + 255:2,
                                   (ocb % 2) * 512:(ocb % 2) * 512 + 512],
                           vslot, waits=vsb.read((k.sem[SY], k.sem[SY].n + 16)))
                B("scrB" if tt % 2 == 0 else "scrB2").r.append(dv)

        proj_half(wqbT, 0)
        rope_q(0)
        vproj(0)
        proj_half(wqbT, 1)
        rope_q(1)
        vproj(1)
        proj_half(wkbT, 0)
        rope_k(0)
        vproj(2)
        proj_half(wkbT, 1)
        rope_k(1)
        vproj(3)
        kv_done = (k.sem[SY], k.sem[SY].n)

        k.sem_cc.n += 1
        ccv = k.sem_cc.n
        kvd = kv_done

        def _cc(eng, ctx, kvd=kvd, ccv=ccv):
            if ctx.get(kvd[0].h.num, -1) < kvd[1]:
                eng.wait_ge(kvd[0].h, kvd[1])
                ctx[kvd[0].h.num] = kvd[1]
            eng.collective_compute(
                "AllGather", ALU.bypass,
                replica_groups=[[0, 1], [2, 3], [4, 5], [6, 7]],
                ins=[kv_pack], outs=[kv_g]).then_inc(k.sem_cc.h, 1)
        k.items["gpsimd"].append(_cc)
        cc_done = (k.sem_cc, ccv)


        qdone_gate = [(k.sem[TE], k.sem[TE].n), (k.sem[VE], k.sem[VE].n)]

        eb = [B("ab_exp0"), B("ab_exp1")]
        cb = B("ab_ctx")

        def prologue(h):
            hp = h % 2
            kvt = ab_kv[hp]
            kvb = B("ab_kv%d" % hp)
            ws_ = kvb.write()
            l1 = k.dma(SY, kvt[:, 0:TOK], kv_g[h * 128:(h + 1) * 128, :],
                       waits=ws_ + [cc_done])
            l2 = k.dma(SY, kvt[:, TOK:SEQ], kv_g[2 * D + h * 128: 2 * D + h * 128 + 128, :])
            for a in range(16):
                half, al = a // 8, a % 8
                r0 = half * 2 * D + D + al * 256 + h // 8
                l3 = k.dma(SY, kvt[:, SEQ + a * 128: SEQ + (a + 1) * 128],
                           kv_g[r0:r0 + 255:2, (h % 8) * 128:(h % 8) * 128 + 128])
            kvb.write(l3)
            v1 = k.op(VE, lambda e, kvt=kvt: e.tensor_scalar_min(
                scrA[:], kvt[:, 0:SEQ], 0.0),
                waits=kvb.read((k.sem[VE], k.sem[VE].n + 1)) + B("scrAr").write())
            B("scrAr").write(v1)
            s1 = k.op(SC, lambda e: e.activation(scrB[:], scrA[:], AF.Exp),
                      waits=B("scrAr").read((k.sem[SC], k.sem[SC].n + 1)) + B("scrB").write())
            B("scrB").write(s1)
            kfb = B("ab_kf")
            v2 = k.op(VE, lambda e, kvt=kvt: e.scalar_tensor_tensor(
                ab_kf[:], kvt[:, 0:SEQ], 0.0, scrB[:], ALU.max, ALU.add),
                waits=[s1] + kvb.read((k.sem[VE], k.sem[VE].n + 1)) + kfb.write()
                + B("scrB").read((k.sem[VE], k.sem[VE].n + 1)))
            kfb.write(v2)
            v3 = k.op(VE, lambda e: e.tensor_reduce(scr2[:, 0:1], ab_kf[:], AX.X, ALU.add),
                      waits=kfb.read((k.sem[VE], k.sem[VE].n + 1)) + B("ntf").write())
            B("ntf").write(v3)
            v4 = k.op(VE, lambda e, hp=hp: e.tensor_copy(ntb[:, hp:hp + 1], scr2[:, 0:1]),
                      waits=B("ntf").read((k.sem[VE], k.sem[VE].n + 1)) + B("ntb%d" % hp).write())
            B("ntb%d" % hp).write(v4)
            kftb = B("ab_kft")
            transpose16(lambda dc: ab_kf[:, dc * 128:(dc + 1) * 128],
                        lambda rnd: ab_kft[:].rearrange("p (dc t) -> p dc t", dc=16)
                        [:, rnd * 8:(rnd + 1) * 8, :], kfb.read(), kftb)
            kfb.r.append((k.sem[TE], k.sem[TE].n))
            v5 = k.op(VE, lambda e, h=h: e.tensor_scalar_min(
                scrA[:, 0:TOK], ab_q[:, h * TOK:(h + 1) * TOK], 0.0),
                waits=qb.read((k.sem[VE], k.sem[VE].n + 1)) + B("scrAr").write())
            B("scrAr").write(v5)
            s2 = k.op(SC, lambda e: e.activation(scrB[:, 0:TOK], scrA[:, 0:TOK], AF.Exp),
                      waits=B("scrAr").read((k.sem[SC], k.sem[SC].n + 1)) + B("scrB").write())
            B("scrB").write(s2)
            qfb = B("ab_qf%d" % hp)
            v6 = k.op(VE, lambda e, h=h, hp=hp: e.scalar_tensor_tensor(
                ab_qf[hp][:], ab_q[:, h * TOK:(h + 1) * TOK], 0.0, scrB[:, 0:TOK],
                ALU.max, ALU.add),
                waits=[s2] + qb.read((k.sem[VE], k.sem[VE].n + 1)) + qfb.write()
                + B("scrB").read((k.sem[VE], k.sem[VE].n + 1)))
            qfb.write(v6)
            pm = B("psE")
            for st_ in range(16):
                tv = k.op(TE, lambda e, st_=st_, kvt=kvt: e.matmul(
                    psE[:, 0:128], ab_kft[:, st_ * 128:(st_ + 1) * 128],
                    kvt[:, SEQ + st_ * 128: SEQ + (st_ + 1) * 128],
                    start=(st_ == 0), stop=(st_ == 15)),
                    waits=(kftb.read() + kvb.read() + pm.write()) if st_ == 0 else None,
                    inc=(st_ == 15))
            kftb.r.append(tv); kvb.r.append(tv)
            pm.write(tv)
            sm = k.op(SC, lambda e, hp=hp: e.copy(ab_mem[hp][:], psE[:, 0:128]),
                      waits=pm.read((k.sem[SC], k.sem[SC].n + 1)) + B("ab_mem%d" % hp).write())
            B("ab_mem%d" % hp).write(sm)

        def scores_b(i):
            h, qq = divmod(i, 4)
            hp, p = h % 2, i % 2
            kvt = ab_kv[hp]
            kvb = B("ab_kv%d" % hp)
            exp_t = ab_exp[p]
            qof = qq * 256
            for kt in range(16):
                pa = B("psA%d" % (kt % 2))
                tv = k.op(TE, lambda e, kt=kt, h=h, qof=qof, kvt=kvt: e.matmul(
                    psA[kt % 2][:, 0:256], kvt[:, kt * 128:(kt + 1) * 128],
                    ab_q[:, h * TOK + qof: h * TOK + qof + 256],
                    start=True, stop=True),
                    waits=pa.write() + (kvb.read() if kt == 0 else []))
                if kt == 0:
                    kvb.r.append(tv)
                pa.write(tv)
                sv = k.op(SC, lambda e, kt=kt, exp_t=exp_t: e.activation(
                    exp_t[:, kt * 256:(kt + 1) * 256], psA[kt % 2][:, 0:256], AF.Exp,
                    bias=cst[:, 1:2], scale=cst[:, 2:3]),
                    waits=pa.read((k.sem[SC], k.sem[SC].n + 1)) + eb[p].write()
                    + B("cst").read())
                eb[p].w.append(sv)
            kvb.r.append(tv)

        def tail_b(i):
            h, qq = divmod(i, 4)
            hp, p = h % 2, i % 2
            kvt = ab_kv[hp]
            kvb = B("ab_kv%d" % hp)
            exp_t = ab_exp[p]
            qfb = B("ab_qf%d" % hp)
            qof = qq * 256
            pd0, pd1 = B("psD0"), B("psD1")
            for kt in range(16):
                tv = k.op(TE, lambda e, kt=kt, exp_t=exp_t: e.matmul(
                    psD[0][:, 0:256], ones128[:], exp_t[:, kt * 256:(kt + 1) * 256],
                    start=(kt == 0), stop=(kt == 15)),
                    waits=(eb[p].read() + pd0.write() + B("ones").read())
                    if kt == 0 else None, inc=(kt == 15))
            eb[p].r.append(tv)
            pd0.write(tv)
            tl_ = k.op(TE, lambda e, qof=qof, hp=hp: e.matmul(
                psD[1][:, 0:256], ntb[:, hp:hp + 1], ab_qf[hp][:, qof:qof + 256],
                start=True, stop=True),
                waits=B("ntb%d" % hp).read() + qfb.read() + pd1.write())
            B("ntb%d" % hp).r.append(tl_); qfb.r.append(tl_)
            pd1.write(tl_)
            pc = B("psA2")
            for kt in range(16):
                tv = k.op(TE, lambda e, kt=kt, kvt=kvt, exp_t=exp_t: e.matmul(
                    psA[2][:, 0:256], kvt[:, SEQ + kt * 128: SEQ + (kt + 1) * 128],
                    exp_t[:, kt * 256:(kt + 1) * 256],
                    start=(kt == 0), stop=(kt == 15)),
                    waits=(eb[p].read() + pc.write() + kvb.read()) if kt == 0 else None,
                    inc=(kt == 15))
            eb[p].r.append(tv); kvb.r.append(tv)
            pc.write(tv)
            pmo = B("psA3")
            tv = k.op(TE, lambda e, qof=qof, hp=hp: e.matmul(
                psA[3][:, 0:256], ab_mem[hp][:], ab_qf[hp][:, qof:qof + 256],
                start=True, stop=True),
                waits=B("ab_mem%d" % hp).read() + qfb.read() + pmo.write())
            B("ab_mem%d" % hp).r.append(tv); qfb.r.append(tv)
            pmo.write(tv)
            vr1 = k.op(VE, lambda e: e.reciprocal(rowscr[0:1, 0:256], psD[0][0:1, 0:256]),
                       waits=pd0.read((k.sem[VE], k.sem[VE].n + 1)) + B("rowscr").write())
            vr2 = k.op(VE, lambda e: e.reciprocal(rowscr[0:1, 256:512], psD[1][0:1, 0:256]),
                       waits=pd1.read((k.sem[VE], k.sem[VE].n + 1)))
            B("rowscr").write(vr2)
            pa0, pa1 = B("psA0"), B("psA1")
            tb1 = k.op(TE, lambda e: e.matmul(psA[0][:, 0:256], ones1f[:],
                                              rowscr[0:1, 0:256], start=True, stop=True),
                       waits=[vr1, vr2] + pa0.write())
            B("rowscr").r.append(tb1)
            pa0.write(tb1)
            tb2 = k.op(TE, lambda e: e.matmul(psA[1][:, 0:256], ones1f[:],
                                              rowscr[0:1, 256:512],
                                              start=True, stop=True),
                       waits=pa1.write())
            B("rowscr").r.append(tb2)
            pa1.write(tb2)
            sc1 = k.op(SC, lambda e: e.copy(scrA[:, 1024:1280], psA[0][:, 0:256]),
                       waits=pa0.read((k.sem[SC], k.sem[SC].n + 1)) + B("scrAr").write())
            sc2 = k.op(SC, lambda e: e.copy(scrA[:, 1280:1536], psA[1][:, 0:256]),
                       waits=pa1.read((k.sem[SC], k.sem[SC].n + 1)))
            B("scrAr").write(sc2)
            v1_ = k.op(VE, lambda e: e.tensor_tensor(scrA[:, 0:256], psA[3][:, 0:256],
                                                     scrA[:, 1280:1536], ALU.mult),
                       waits=pmo.read((k.sem[VE], k.sem[VE].n + 1)) + [sc1, sc2])
            v2_ = k.op(VE, lambda e: e.tensor_tensor(scrA[:, 256:512], psA[2][:, 0:256],
                                                     scrA[:, 1024:1280], ALU.mult),
                       waits=pc.read((k.sem[VE], k.sem[VE].n + 1)))
            v3_ = k.op(VE, lambda e: e.tensor_sub(scrA[:, 0:256], scrA[:, 0:256],
                                                  scrA[:, 256:512]),
                       waits=[v1_, v2_])
            v4_ = k.op(VE, lambda e, h=h, qof=qof: e.scalar_tensor_tensor(
                ab_ctx[:, h * TOK + qof: h * TOK + qof + 256],
                scrA[:, 0:256], g_b[:, h:h + 1], scrA[:, 256:512],
                ALU.mult, ALU.add),
                waits=[v3_] + cb.write() + B("g_b").read())
            B("scrAr").write(v4_)
            cb.w.append(v4_)

        prologue(0)
        scores_b(0)
        for i in range(NH * 4):
            h, qq = divmod(i, 4)
            if qq == 0 and h + 1 < NH:
                prologue(h + 1)
            if i + 1 < NH * 4:
                scores_b(i + 1)
            tail_b(i)

        w3b_ = B("ab_w3")
        head_gate = [(k.sem[TE], k.sem[TE].n), (k.sem[VE], k.sem[VE].n),
                     (k.sem[SC], k.sem[SC].n)]
        for ocb in range(4):
            ws_ = w3b_.write() + (head_gate if ocb == 0 else [])
            bq3 = {}
            for qq_ in range(4):
                lv = k.dma(SY, ab_w3[:, qq_ * 2048:(qq_ + 1) * 2048],
                           wobT[ocb][:, qq_ * 2048:(qq_ + 1) * 2048],
                           waits=ws_ if qq_ == 0 else None)
                bq3[qq_] = lv
            w3b_.write(lv)
            w3b_.w = [bq3[3]]
            for tt in range(NTT):
                pa = B("psA%d" % (tt % 4))
                for dc in range(16):
                    tv = k.op(TE, lambda e, tt=tt, dc=dc: e.matmul(
                        psA[tt % 4][:],
                        ab_ctx[:, dc * TOK + tt * 128: dc * TOK + tt * 128 + 128],
                        ab_w3[:, dc * 512:(dc + 1) * 512],
                        start=(dc == 0), stop=(dc == 15)),
                        waits=([bq3[dc // 4]] + (w3b_.read() + cb.read() + pa.write()
                               if dc == 0 else [])), inc=(dc == 15))
                w3b_.r.append(tv); cb.r.append(tv)
                pa.write(tv)
                xb = B("x%d" % tt)
                vv = k.op(VE, lambda e, tt=tt, ocb=ocb: e.tensor_add(
                    x_sb[:, tt * D + ocb * 512: tt * D + ocb * 512 + 512],
                    x_sb[:, tt * D + ocb * 512: tt * D + ocb * 512 + 512],
                    psA[tt % 4][:]),
                    waits=pa.read((k.sem[VE], k.sem[VE].n + 1)) + xb.write())
                xb.write(vv)

    if dbg:
        dbg_k = nc.dram_tensor("dbg_k", [128, 4 * SEQ], BF16, kind="ExternalOutput").ap()
        dbg_v = nc.dram_tensor("dbg_v", [128, 16 * 512], BF16, kind="ExternalOutput").ap()
        dbg_q = nc.dram_tensor("dbg_q", [128, 16 * TOK], BF16, kind="ExternalOutput").ap()
        dbg_c = nc.dram_tensor("dbg_c", [128, 16 * TOK], BF16, kind="ExternalOutput").ap()
        dbg_g = nc.dram_tensor("dbg_g", [4 * D, TOK], BF16, kind="ExternalOutput").ap()

    emit_attn_a2()
    if dbg and n_sub == 1:
        k.barrier()
        k.dma(SY, dbg_k, aa_k[:])
        k.dma(SY, dbg_v, aa_v[:])
        k.dma(SY, dbg_q, aa_q[:])
        k.dma(SY, dbg_c, aa_ctx[:])
    k.barrier()
    if n_sub >= 2:
        emit_ffn()
        k.barrier()
    if n_sub >= 3:
        emit_attn_b()
        if dbg and n_sub == 3:
            k.barrier()
            k.dma(SY, dbg_q, ab_q[:])
            k.dma(SY, dbg_c, ab_ctx[:])
            for i in range(8):
                k.dma(SY, dbg_g[i * TOK:(i + 1) * TOK, :],
                      kv_g[i * TOK:(i + 1) * TOK, :])
        k.barrier()
    if n_sub >= 4:
        emit_ffn()
        k.barrier()

    last = None
    for t in range(NTT):
        last = k.dma(SY, y[t * 128:(t + 1) * 128, :], x_sb[:, t * D:(t + 1) * D],
                     waits=B("x%d" % t).write())
    k.raw("gpsimd", lambda e, v=last[1]: e.wait_ge(last[0].h, v))

    k.replay()
    return nc


def _tables():
    ident = np.eye(128, dtype=np.float32)
    inv_freq = (1.0 / (10000.0 ** (np.arange(0, D, 2, dtype=np.float32) / np.float32(D)))
                ).astype(np.float32)
    return ident, inv_freq


def _prep_src_for_core(inputs, c):
    w1 = np.asarray(inputs["L0_ffn_w1"]); w2 = np.asarray(inputs["L0_ffn_w2"])
    w3 = np.asarray(inputs["L0_ffn_w3"])
    att = [np.asarray(inputs["L0_attn_b_Wq"]), np.asarray(inputs["L0_attn_b_Wk"]),
           np.asarray(inputs["L0_attn_b_Wv"]), np.asarray(inputs["L0_attn_b_Wo"])]
    pans = []
    for b in range(8 * c, 8 * c + 8):
        if b < 16:
            for rb in range(4 * b, 4 * b + 4):
                pans.append(w1[rb * 128:(rb + 1) * 128, :])
        elif b < 32:
            ds = b - 16
            for rb in range(4 * ds, 4 * ds + 4):
                pans.append(w2[rb * 128:(rb + 1) * 128, :])
        elif b < 48:
            g, dms = divmod(b - 32, 4)
            for rb in range(4 * dms, 4 * dms + 4):
                pans.append(w3[rb * 128:(rb + 1) * 128, g * D:(g + 1) * D])
        else:
            wi, ocb = divmod(b - 48, 4)
            for rb in range(4 * ocb, 4 * ocb + 4):
                pans.append(att[wi][rb * 128:(rb + 1) * 128, :])
    return np.ascontiguousarray(np.stack(pans, 0))


def kernel(**inputs):
    x = np.asarray(inputs["L0_attn_a_x"])
    ident, inv_freq = _tables()
    nc = build_program(4)
    in_maps = []
    for c in range(8):
        b, h = c // 2, c % 2
        off = h * TOK
        pos = np.arange(off, off + TOK, dtype=np.float32)
        freqs = pos[:, None] * inv_freq[None, :]
        emb = np.concatenate([freqs, freqs], axis=1)
        cos_t = np.ascontiguousarray(np.cos(emb).T.astype(np.float32))
        sin_t = np.ascontiguousarray(np.sin(emb).T.astype(np.float32))
        kglob = np.concatenate([np.arange(off, off + TOK),
                                np.arange(TOK - off, 2 * TOK - off)])
        qglob = (np.arange(TOK) + off)
        maskT = (kglob[:, None] <= qglob[None, :]).astype(ml_dtypes.float8_e4m3fn)
        m = {
            "x_own": np.ascontiguousarray(x[b, off:off + TOK]),
            "x_oth": np.ascontiguousarray(x[b, TOK - off:2 * TOK - off]),
            "wq_a": np.asarray(inputs["L0_attn_a_Wq"]),
            "wk_a": np.asarray(inputs["L0_attn_a_Wk"]),
            "wv_a": np.asarray(inputs["L0_attn_a_Wv"]),
            "wo_a": np.asarray(inputs["L0_attn_a_Wo"]),
            "bo_a": np.asarray(inputs["L0_attn_a_bo"]).reshape(1, D),
            "wq_b": np.asarray(inputs["L0_attn_b_Wq"]),
            "wk_b": np.asarray(inputs["L0_attn_b_Wk"]),
            "wv_b": np.asarray(inputs["L0_attn_b_Wv"]),
            "wo_b": np.asarray(inputs["L0_attn_b_Wo"]),
            "gate": np.asarray(inputs["L0_attn_b_gate"]).reshape(1, NH),
            "w1": np.asarray(inputs["L0_ffn_w1"]),
            "w2": np.asarray(inputs["L0_ffn_w2"]),
            "w3": np.asarray(inputs["L0_ffn_w3"]),
            "ident": ident,
            "maskT": maskT,
            "cos_t": cos_t,
            "sin_t": sin_t,
            "prep_src": _prep_src_for_core(inputs, c),
            "sel": np.full((128, 1), 1.0 if c < 6 else 0.0, np.float32),
        }
        in_maps.append(m)
    res = run_bass_kernel_spmd(nc, in_maps, core_ids=list(range(8)))
    out = np.empty((4, SEQ, D), dtype=np.float32)
    for c in range(8):
        b, h = c // 2, c % 2
        out[b, h * TOK:(h + 1) * TOK] = res.results[c]["y"]
    return out
